# version 03
# baseline (speedup 1.0000x reference)
import numpy as np
import ml_dtypes
import concourse.bass as bass
import concourse.mybir as mybir
from concourse.alu_op_type import AluOpType
from concourse.bass_utils import run_bass_kernel_spmd
from contextlib import ExitStack

F32 = mybir.dt.float32
BF16 = mybir.dt.bfloat16
AF = mybir.ActivationFunctionType
AX = mybir.AxisListType.X
BF = ml_dtypes.bfloat16

NC = 8
L = 8192
H = 2048
DI = 4096
NSTATE = 16
TR = 128
EX = 7
DH = 128
NH = 16
TLOC = L // NC
DLOC = DI // NC
NEG = -3.0e29
SCALE = 1.0 / np.sqrt(DH)
SIM_SAFE_ACT = False

_prog_cache = {}


def _cached(key, fn):
    if key not in _prog_cache:
        _prog_cache[key] = fn()
    return _prog_cache[key]


def _run(nc, in_maps):
    return run_bass_kernel_spmd(nc, in_maps, core_ids=list(range(NC))).results


def _f32(x):
    return np.ascontiguousarray(np.asarray(x), dtype=np.float32)


def _ctx(nc, ctx):
    sb = lambda n, s, d=F32: ctx.enter_context(nc.sbuf_tensor(n, s, d))
    ps = lambda n, s, d=F32: ctx.enter_context(nc.psum_tensor(n, s, d))
    sem = lambda n: ctx.enter_context(nc.semaphore(n))
    return sb, ps, sem


def _mk_ln(TT):
    nc = bass.Bass("TRN2", target_bir_lowering=False)
    x_in = nc.dram_tensor("x", [TT * 128, H], F32, kind="ExternalInput").ap()
    idn = nc.dram_tensor("idn", [128, 128], BF16, kind="ExternalInput").ap()
    xnt_out = nc.dram_tensor("xnt", [H, TT * 128], BF16, kind="ExternalOutput").ap()
    nJ = H // 128
    with ExitStack() as ctx:
        sb, ps, sem = _ctx(nc, ctx)
        s_x = [sb(f"s_x{p}", [128, H]) for p in range(2)]
        s_sq = sb("s_sq", [128, H])
        s_m = sb("s_m", [128, 1])
        s_r = sb("s_r", [128, 1])
        s_t1 = sb("s_t1", [128, 1])
        s_t2 = sb("s_t2", [128, 1])
        s_xn = [sb(f"s_xn{p}", [128, H], BF16) for p in range(2)]
        s_id = sb("s_id", [128, 128], BF16)
        s_out = [sb(f"s_out{p}", [128, 128], BF16) for p in range(4)]
        p_tp = [ps(f"p_tp{p}", [128, 128], BF16) for p in range(2)]
        lds, lx0, lx1, ac, dv, pe, ev = (sem(n) for n in
            ["lds", "lx0", "lx1", "ac", "dv", "pe", "ev"])
        stb = [sem(f"st{b}") for b in range(4)]
        lx = [lx0, lx1]
        block = ctx.enter_context(nc.Block())

        @block.sync
        def _(sync):
            sync.dma_start(out=s_id[:], in_=idn).then_inc(lds, 16)
            for it in range(min(2, TT)):
                sync.dma_start(out=s_x[it % 2][:],
                               in_=x_in[it * 128:(it + 1) * 128, :]).then_inc(lx[it % 2], 16)
            for it in range(TT):
                for j in range(nJ):
                    k = it * nJ + j
                    sync.wait_ge(ev, k + 1)
                    sync.dma_start(out=xnt_out[j * 128:(j + 1) * 128, it * 128:(it + 1) * 128],
                                   in_=s_out[k % 4][:]).then_inc(stb[k % 4], 16)
                if it + 2 < TT:
                    sync.wait_ge(dv, 3 * it + 3)
                    sync.dma_start(out=s_x[it % 2][:],
                                   in_=x_in[(it + 2) * 128:(it + 3) * 128, :]).then_inc(lx[it % 2], 16)
            for b in range(4):
                sync.wait_ge(stb[b], 16 * ((nJ * TT - b + 3) // 4))

        @block.scalar
        def _(scalar):
            for it in range(TT):
                scalar.wait_ge(lx[it % 2], 16 * (it // 2 + 1))
                if it >= 1:
                    scalar.wait_ge(dv, 3 * (it - 1) + 2)
                scalar.activation(s_sq[:], s_x[it % 2][:], AF.Square).then_inc(ac, 1)
                scalar.wait_ge(dv, it * 3 + 2)
                scalar.activation(s_t2[:, 0:1], s_t1[:, 0:1], AF.Ln)
                scalar.drain()
                scalar.activation(s_r[:, 0:1], s_t2[:, 0:1], AF.Exp, scale=-0.5).then_inc(ac, 1)

        @block.vector
        def _(vector):
            for it in range(TT):
                x = s_x[it % 2]
                vector.wait_ge(lx[it % 2], 16 * (it // 2 + 1))
                vector.reduce_sum(s_m[:, 0:1], x[:], axis=AX)
                vector.drain()
                vector.tensor_scalar(s_m[:, 0:1], s_m[:, 0:1], 1.0 / H, None,
                                     AluOpType.mult).then_inc(dv, 1)
                vector.wait_ge(ac, it * 2 + 1)
                vector.reduce_sum(s_t1[:, 0:1], s_sq[:], axis=AX)
                vector.drain()
                vector.tensor_tensor(s_t2[:, 0:1], s_m[:, 0:1], s_m[:, 0:1], AluOpType.mult)
                vector.drain()
                vector.tensor_scalar(s_t1[:, 0:1], s_t1[:, 0:1], 1.0 / H, None, AluOpType.mult)
                vector.drain()
                vector.tensor_scalar(s_t1[:, 0:1], s_t1[:, 0:1], s_t2[:, 0:1], 1e-5,
                                     AluOpType.subtract, AluOpType.add).then_inc(dv, 1)
                vector.wait_ge(ac, it * 2 + 2)
                if it >= 2:
                    vector.wait_ge(pe, nJ * (it - 1))
                vector.tensor_scalar(s_xn[it % 2][:], x[:], s_m[:, 0:1], s_r[:, 0:1],
                                     AluOpType.subtract, AluOpType.mult).then_inc(dv, 1)
                for j in range(nJ):
                    k = it * nJ + j
                    vector.wait_ge(pe, k + 1)
                    if k >= 4:
                        vector.wait_ge(stb[k % 4], 16 * (k // 4))
                    vector.tensor_copy(s_out[k % 4][:], p_tp[k % 2][:]).then_inc(ev, 1)

        @block.tensor
        def _(tensor):
            tensor.wait_ge(lds, 16)
            for it in range(TT):
                tensor.wait_ge(dv, it * 3 + 3)
                for j in range(nJ):
                    k = it * nJ + j
                    if k >= 2:
                        tensor.wait_ge(ev, k - 1)
                    tensor.transpose(p_tp[k % 2][:], s_xn[it % 2][:, j * 128:(j + 1) * 128],
                                     s_id[:]).then_inc(pe, 1)
    return nc


def _launch_ln(x0):
    nc = _cached(("ln", 9), lambda: _mk_ln(9))
    idn = np.eye(128, dtype=BF)
    ims = []
    for c in range(NC):
        lo = c * TLOC - 128
        if lo < 0:
            xh = np.concatenate([np.zeros((128, H), np.float32), x0[:(c + 1) * TLOC]], 0)
        else:
            xh = x0[lo:(c + 1) * TLOC]
        ims.append(dict(x=_f32(xh), idn=idn))
    return [r["xnt"] for r in _run(nc, ims)]


def _mk_matmul(K, T, outs, out_dtype):
    nKT = K // 128
    nch = (T + 511) // 512
    Otot = sum(o["O"] for o in outs)
    nOT = Otot // 128
    nc = bass.Bass("TRN2", target_bir_lowering=False)
    rhs_in = nc.dram_tensor("rhsT", [K, T], BF16, kind="ExternalInput").ap()
    w_in = nc.dram_tensor("wT", [K, Otot], F32, kind="ExternalInput").ap()
    b_in = nc.dram_tensor("bias", [128, Otot // 128], F32, kind="ExternalInput").ap()
    d_out = {o["name"]: nc.dram_tensor(o["name"], [o["O"], T], out_dtype, kind="ExternalOutput").ap()
             for o in outs}

    def tinfo(ot):
        base = 0
        for o in outs:
            if base <= ot * 128 < base + o["O"]:
                return o, base
            base += o["O"]
        raise AssertionError

    with ExitStack() as ctx:
        sb, ps, sem = _ctx(nc, ctx)
        s_rhs = sb("s_rhs", [128, nKT * T], BF16)
        s_w = [sb(f"s_w{p}", [128, nKT * 128], BF16) for p in range(2)]
        s_b = sb("s_b", [128, nOT])
        s_st = [sb(f"s_st{p}", [128, 512], out_dtype) for p in range(4)]
        p_mm = [ps(f"p_mm{p}", [128, 512]) for p in range(2)]
        ldr, lw0, lw1, mm, ev = (sem(n) for n in ["ldr", "lw0", "lw1", "mm", "ev"])
        stb = [sem(f"st{b}") for b in range(4)]
        lw = [lw0, lw1]
        block = ctx.enter_context(nc.Block())

        @block.sync
        def _(sync):
            sync.dma_start(out=s_rhs[:],
                           in_=rhs_in.rearrange("(kt p) t -> p kt t", p=128)).then_inc(ldr, 16)
            sync.dma_start(out=s_b[:], in_=b_in).then_inc(ldr, 16)
            for ot in range(nOT):
                o, base = tinfo(ot)
                for ch in range(nch):
                    k = ot * nch + ch
                    c0 = ch * 512
                    cw = min(512, T - c0)
                    sync.wait_ge(ev, k + 1)
                    sync.dma_start(out=d_out[o["name"]][ot * 128 - base:ot * 128 - base + 128, c0:c0 + cw],
                                   in_=s_st[k % 4][:, 0:cw]).then_inc(stb[k % 4], 16)
            for b in range(4):
                sync.wait_ge(stb[b], 16 * ((nOT * nch - b + 3) // 4))

        @block.gpsimd
        def _(gpsimd):
            for ot in range(nOT):
                if ot >= 2:
                    gpsimd.wait_ge(mm, (ot - 1) * nch)
                gpsimd.dma_start(out=s_w[ot % 2][:],
                                 in_=w_in[:, ot * 128:(ot + 1) * 128].rearrange(
                                     "(kt p) o -> p kt o", p=128)).then_inc(lw[ot % 2], 16)

        @block.tensor
        def _(tensor):
            tensor.wait_ge(ldr, 32)
            for ot in range(nOT):
                tensor.wait_ge(lw[ot % 2], 16 * (ot // 2 + 1))
                for ch in range(nch):
                    k = ot * nch + ch
                    c0 = ch * 512
                    cw = min(512, T - c0)
                    if k >= 2:
                        tensor.wait_ge(ev, k - 1)
                    for kt in range(nKT):
                        ins = tensor.matmul(p_mm[k % 2][:, 0:cw],
                                            s_w[ot % 2][:, kt * 128:(kt + 1) * 128],
                                            s_rhs[:, kt * T + c0:kt * T + c0 + cw],
                                            start=(kt == 0), stop=(kt == nKT - 1))
                        if kt == nKT - 1:
                            ins.then_inc(mm, 1)

        @block.scalar
        def _(scalar):
            scalar.wait_ge(ldr, 32)
            for ot in range(nOT):
                o, base = tinfo(ot)
                for ch in range(nch):
                    k = ot * nch + ch
                    cw = min(512, T - ch * 512)
                    scalar.wait_ge(mm, k + 1)
                    if k >= 4:
                        scalar.wait_ge(stb[k % 4], 16 * (k // 4))
                    kw = {}
                    fn = o["act"]
                    if o["bias"]:
                        kw["bias"] = s_b[:, ot:ot + 1]
                        if fn == AF.Copy:
                            fn = AF.Relu if SIM_SAFE_ACT else AF.Prelu
                            if not SIM_SAFE_ACT:
                                kw["alpha"] = 1.0
                    if o.get("scale") is not None:
                        kw["scale"] = o["scale"]
                    scalar.activation(s_st[k % 4][:, 0:cw], p_mm[k % 2][:, 0:cw],
                                      fn, **kw).then_inc(ev, 1)
    return nc


def _launch_matmul(key, rhsT_list, wT, bias, outs, out_dtype):
    K, T = wT.shape[0], rhsT_list[0].shape[1]
    nc = _cached((key, K, T), lambda: _mk_matmul(K, T, outs, out_dtype))
    wTc = _f32(wT)
    bc = np.ascontiguousarray(_f32(bias).reshape(-1, 128).T)
    ims = [dict(rhsT=np.ascontiguousarray(r), wT=wTc, bias=bc) for r in rhsT_list]
    return _run(nc, ims)


def _mk_conv():
    TH = TLOC + 3
    nG = DI // 128
    nc = bass.Bass("TRN2", target_bir_lowering=False)
    hs_in = nc.dram_tensor("hsT", [DI, 1152], F32, kind="ExternalInput").ap()
    cw_in = nc.dram_tensor("cw", [DI, 4], F32, kind="ExternalInput").ap()
    cb_in = nc.dram_tensor("cb", [128, DI // 128], F32, kind="ExternalInput").ap()
    xpw_in = nc.dram_tensor("xpwT", [DI, 160], F32, kind="ExternalInput").ap()
    xc_out = nc.dram_tensor("xcT", [DI, TLOC], F32, kind="ExternalOutput").ap()
    ssm_out = nc.dram_tensor("ssm", [160, TLOC], F32, kind="ExternalOutput").ap()
    with ExitStack() as ctx:
        sb, ps, sem = _ctx(nc, ctx)
        s_hs = [sb(f"s_hs{p}", [128, TH]) for p in range(2)]
        s_acc = [sb(f"s_acc{p}", [128, TLOC]) for p in range(2)]
        s_xc = [sb(f"s_xc{p}", [128, TLOC]) for p in range(2)]
        s_xc16 = sb("s_xc16", [128, nG * TLOC], BF16)
        s_cw = sb("s_cw", [128, nG * 4])
        s_cb = sb("s_cb", [128, nG])
        s_w16 = sb("s_w16", [128, nG * 160], BF16)
        s_ssm = sb("s_ssm", [128, TLOC])
        p_mm = [ps(f"p_mm{p}", [128, 512]) for p in range(2)]
        lds, lh0, lh1, ldw, dv, ac, mm, ev = (sem(n) for n in
            ["lds", "lh0", "lh1", "ldw", "dv", "ac", "mm", "ev"])
        stx = [sem("stx0"), sem("stx1")]
        sts = [sem("sts0"), sem("sts1")]
        lh = [lh0, lh1]
        block = ctx.enter_context(nc.Block())

        @block.sync
        def _(sync):
            sync.dma_start(out=s_cw[:], in_=cw_in.rearrange("(g p) k -> p g k", p=128)).then_inc(lds, 16)
            sync.dma_start(out=s_cb[:], in_=cb_in).then_inc(lds, 16)
            sync.dma_start(out=s_hs[0][:], in_=hs_in[0:128, 125:125 + TH]).then_inc(lh0, 16)
            sync.dma_start(out=s_hs[1][:], in_=hs_in[128:256, 125:125 + TH]).then_inc(lh1, 16)
            for g in range(nG):
                sync.wait_ge(ac, g + 1)
                sync.dma_start(out=xc_out[g * 128:(g + 1) * 128, :],
                               in_=s_xc[g % 2][:]).then_inc(stx[g % 2], 16)
                if g + 2 < nG:
                    sync.wait_ge(dv, 2 * g + 1)
                    sync.dma_start(out=s_hs[g % 2][:],
                                   in_=hs_in[(g + 2) * 128:(g + 3) * 128, 125:125 + TH]).then_inc(lh[g % 2], 16)
            for k in range(4):
                sync.wait_ge(ev, k + 1)
                mt, chk = divmod(k, 2)
                mw = 128 if mt == 0 else 32
                sync.dma_start(out=ssm_out[mt * 128:mt * 128 + mw, chk * 512:(chk + 1) * 512],
                               in_=s_ssm[0:mw, chk * 512:(chk + 1) * 512]).then_inc(sts[k % 2], 16)
            sync.wait_ge(stx[0], 16 * (nG // 2))
            sync.wait_ge(stx[1], 16 * (nG // 2))
            sync.wait_ge(sts[0], 32)
            sync.wait_ge(sts[1], 32)

        @block.gpsimd
        def _(gpsimd):
            gpsimd.dma_start(out=s_w16[:],
                             in_=xpw_in.rearrange("(g p) o -> p g o", p=128)).then_inc(ldw, 16)

        @block.vector
        def _(vector):
            vector.wait_ge(lds, 32)
            for g in range(nG):
                vector.wait_ge(lh[g % 2], 16 * (g // 2 + 1))
                hs, acc = s_hs[g % 2], s_acc[g % 2]
                vector.wait_ge(ac, g)
                vector.tensor_scalar(acc[:], hs[:, 0:TLOC], s_cw[:, g * 4:g * 4 + 1], None,
                                     AluOpType.mult)
                vector.drain()
                for j in range(1, 4):
                    vector.scalar_tensor_tensor(acc[:], hs[:, j:j + TLOC],
                                                s_cw[:, g * 4 + j:g * 4 + j + 1],
                                                acc[:], AluOpType.mult, AluOpType.add)
                    vector.drain()
                vector.sem_inc(dv, 1)
                vector.wait_ge(ac, g + 1)
                vector.tensor_copy(s_xc16[:, g * TLOC:(g + 1) * TLOC], s_xc[g % 2][:]).then_inc(dv, 1)
            for k in range(4):
                vector.wait_ge(mm, k + 1)
                mt, chk = divmod(k, 2)
                mw = 128 if mt == 0 else 32
                if k >= 2:
                    vector.wait_ge(sts[k % 2], 16)
                vector.tensor_copy(s_ssm[0:mw, chk * 512:(chk + 1) * 512],
                                   p_mm[k % 2][0:mw, :]).then_inc(ev, 1)

        @block.scalar
        def _(scalar):
            scalar.wait_ge(lds, 32)
            for g in range(nG):
                scalar.wait_ge(dv, 2 * g + 1)
                if g >= 2:
                    scalar.wait_ge(stx[g % 2], 16 * (g // 2))
                    scalar.wait_ge(dv, 2 * (g - 2) + 2)
                scalar.activation(s_xc[g % 2][:], s_acc[g % 2][:], AF.Silu,
                                  bias=s_cb[:, g:g + 1]).then_inc(ac, 1)

        @block.tensor
        def _(tensor):
            tensor.wait_ge(ldw, 16)
            tensor.wait_ge(dv, 2 * nG)
            for k in range(4):
                mt, chk = divmod(k, 2)
                mw = 128 if mt == 0 else 32
                if k >= 2:
                    tensor.wait_ge(ev, k - 1)
                for g in range(nG):
                    ins = tensor.matmul(p_mm[k % 2][0:mw, :],
                                        s_w16[:, g * 160 + mt * 128:g * 160 + mt * 128 + mw],
                                        s_xc16[:, g * TLOC + chk * 512:g * TLOC + (chk + 1) * 512],
                                        start=(g == 0), stop=(g == nG - 1))
                    if g == nG - 1:
                        ins.then_inc(mm, 1)
    return nc


def _launch_conv(hsT_list, conv_w, conv_b, x_proj_w):
    nc = _cached("conv", _mk_conv)
    cw = _f32(conv_w)
    cb = np.ascontiguousarray(_f32(conv_b).reshape(DI // 128, 128).T)
    xpwT = _f32(np.asarray(x_proj_w).T)
    ims = [dict(hsT=np.ascontiguousarray(h), cw=cw, cb=cb, xpwT=xpwT) for h in hsT_list]
    res = _run(nc, ims)
    return [r["xcT"] for r in res], [r["ssm"] for r in res]


def _mk_dtproj():
    nMT = DLOC // 128
    nch = L // 512
    nc = bass.Bass("TRN2", target_bir_lowering=False)
    ts_in = nc.dram_tensor("tsT", [TR, L], F32, kind="ExternalInput").ap()
    w_in = nc.dram_tensor("dtwT", [TR, DLOC], F32, kind="ExternalInput").ap()
    b_in = nc.dram_tensor("dtb", [128, DLOC // 128], F32, kind="ExternalInput").ap()
    dt_out = nc.dram_tensor("dtT", [DLOC, L], F32, kind="ExternalOutput").ap()
    with ExitStack() as ctx:
        sb, ps, sem = _ctx(nc, ctx)
        s_tsf = sb("s_tsf", [128, L])
        s_ts = sb("s_ts", [128, L], BF16)
        s_wf = sb("s_wf", [128, DLOC])
        s_w = sb("s_w", [128, DLOC], BF16)
        s_b = sb("s_b", [128, nMT])
        s_x = [sb(f"s_x{p}", [128, 512]) for p in range(2)]
        s_a = [sb(f"s_a{p}", [128, 512]) for p in range(2)]
        s_o = [sb(f"s_o{p}", [128, 512]) for p in range(4)]
        p_mm = [ps(f"p_mm{p}", [128, 512]) for p in range(2)]
        ld, mm, dv, ac, ev = (sem(n) for n in ["ld", "mm", "dv", "ac", "ev"])
        stb = [sem(f"st{b}") for b in range(4)]
        block = ctx.enter_context(nc.Block())

        @block.sync
        def _(sync):
            sync.dma_start(out=s_tsf[:], in_=ts_in).then_inc(ld, 16)
            sync.dma_start(out=s_wf[:], in_=w_in).then_inc(ld, 16)
            sync.dma_start(out=s_b[:], in_=b_in).then_inc(ld, 16)
            for k in range(nMT * nch):
                mt, ch = divmod(k, nch)
                sync.wait_ge(dv, 1 + 2 * k + 2)
                sync.dma_start(out=dt_out[mt * 128:(mt + 1) * 128, ch * 512:(ch + 1) * 512],
                               in_=s_o[k % 4][:]).then_inc(stb[k % 4], 16)
            for b in range(4):
                sync.wait_ge(stb[b], 16 * ((nMT * nch - b + 3) // 4))

        @block.vector
        def _(vector):
            vector.wait_ge(ld, 48)
            vector.tensor_copy(s_ts[:], s_tsf[:])
            vector.drain()
            vector.tensor_copy(s_w[:], s_wf[:]).then_inc(dv, 1)
            for k in range(nMT * nch):
                mt, ch = divmod(k, nch)
                vector.wait_ge(mm, k + 1)
                vector.tensor_scalar(s_x[k % 2][:], p_mm[k % 2][:], s_b[:, mt:mt + 1], None,
                                     AluOpType.add)
                vector.drain()
                vector.sem_inc(dv, 1)
                vector.wait_ge(ac, k + 1)
                vector.tensor_tensor(s_o[k % 4][:], s_o[k % 4][:], s_a[k % 2][:], AluOpType.add)
                vector.drain()
                vector.sem_inc(dv, 1)

        @block.scalar
        def _(scalar):
            for k in range(nMT * nch):
                scalar.wait_ge(dv, 1 + 2 * k + 1)
                scalar.activation(s_a[k % 2][:], s_x[k % 2][:], AF.Abs)
                scalar.drain()
                scalar.activation(s_a[k % 2][:], s_a[k % 2][:], AF.Exp, scale=-1.0)
                scalar.drain()
                scalar.activation(s_a[k % 2][:], s_a[k % 2][:], AF.Ln, bias=1.0)
                scalar.drain()
                if k >= 4:
                    scalar.wait_ge(stb[k % 4], 16 * (k // 4))
                scalar.activation(s_o[k % 4][:], s_x[k % 2][:], AF.Relu).then_inc(ac, 1)

        @block.tensor
        def _(tensor):
            tensor.wait_ge(dv, 1)
            for k in range(nMT * nch):
                mt, ch = divmod(k, nch)
                if k >= 2:
                    tensor.wait_ge(dv, 1 + 2 * (k - 2) + 1)
                tensor.matmul(p_mm[k % 2][:], s_w[:, mt * 128:(mt + 1) * 128],
                              s_ts[:, ch * 512:(ch + 1) * 512],
                              start=True, stop=True).then_inc(mm, 1)
    return nc


def _launch_dtproj(tsT, dt_proj_w, dt_proj_b):
    nc = _cached("dtproj", _mk_dtproj)
    dtwT = _f32(np.asarray(dt_proj_w).T)
    ims = []
    for c in range(NC):
        sl = slice(c * DLOC, (c + 1) * DLOC)
        ims.append(dict(tsT=_f32(tsT), dtwT=_f32(dtwT[:, sl]),
                        dtb=np.ascontiguousarray(_f32(np.asarray(dt_proj_b)[sl]).reshape(DLOC // 128, 128).T)))
    return [r["dtT"] for r in _run(nc, ims)]


def _mk_scan():
    TCH = 1024
    nTC = L // TCH
    nG = DLOC // 128
    W4 = nG * TCH
    nc = bass.Bass("TRN2", target_bir_lowering=False)
    dt_in = nc.dram_tensor("dtT", [DLOC, L], F32, kind="ExternalInput").ap()
    xc_in = nc.dram_tensor("xcT", [DLOC, L], F32, kind="ExternalInput").ap()
    b_in = nc.dram_tensor("Bm", [NSTATE, L], F32, kind="ExternalInput").ap()
    c_in = nc.dram_tensor("Cm", [NSTATE, L], F32, kind="ExternalInput").ap()
    al_in = nc.dram_tensor("Alog", [DLOC, NSTATE], F32, kind="ExternalInput").ap()
    d_in = nc.dram_tensor("Dv", [128, DLOC // 128], F32, kind="ExternalInput").ap()
    y_out = nc.dram_tensor("yT", [DLOC, L], F32, kind="ExternalOutput").ap()

    def dv_u(tc):
        return 1 + tc * 17 + 1

    def dv_step(tc, n):
        return dv_u(tc) + n + 1

    def ac_dA(q):
        return 1 + q + 1

    with ExitStack() as ctx:
        sb, ps, sem = _ctx(nc, ctx)
        s_dt = sb("s_dt", [128, W4])
        s_xc = sb("s_xc", [128, W4])
        s_u = sb("s_u", [128, W4])
        s_B = [sb(f"s_B{p}", [128, W4]) for p in range(2)]
        s_C = [sb(f"s_C{p}", [128, W4]) for p in range(2)]
        s_al = sb("s_al", [128, nG * NSTATE])
        s_D = sb("s_D", [128, nG])
        s_dA = [sb(f"s_dA{p}", [128, W4]) for p in range(2)]
        s_W = sb("s_W", [128, W4])
        s_h = sb("s_h", [128, W4])
        s_state = sb("s_state", [128, nG * NSTATE])
        s_y = sb("s_y", [128, W4])
        lds, lt0, lt1, lb0, lb1, dv, ac, sty = (sem(n) for n in
            ["lds", "lt0", "lt1", "lb0", "lb1", "dv", "ac", "sty"])
        lb = [lb0, lb1]
        block = ctx.enter_context(nc.Block())

        @block.sync
        def _(sync):
            sync.dma_start(out=s_al[:], in_=al_in.rearrange("(g p) n -> p g n", p=128)).then_inc(lds, 16)
            sync.dma_start(out=s_D[:], in_=d_in).then_inc(lds, 16)
            sync.dma_start(out=s_dt[:], in_=dt_in[:, 0:TCH].rearrange("(g p) t -> p g t", p=128)).then_inc(lt0, 16)
            sync.dma_start(out=s_xc[:], in_=xc_in[:, 0:TCH].rearrange("(g p) t -> p g t", p=128)).then_inc(lt1, 16)
            for tc in range(nTC):
                sync.wait_ge(dv, dv_step(tc, 15))
                sync.dma_start(out=y_out[:, tc * TCH:(tc + 1) * TCH].rearrange("(g p) t -> p g t", p=128),
                               in_=s_y[:]).then_inc(sty, 16)
                if tc + 1 < nTC:
                    t2 = tc + 1
                    sync.wait_ge(ac, ac_dA(tc * 16 + 15))
                    sync.dma_start(out=s_dt[:],
                                   in_=dt_in[:, t2 * TCH:(t2 + 1) * TCH].rearrange("(g p) t -> p g t", p=128)).then_inc(lt0, 16)
                    sync.dma_start(out=s_xc[:],
                                   in_=xc_in[:, t2 * TCH:(t2 + 1) * TCH].rearrange("(g p) t -> p g t", p=128)).then_inc(lt1, 16)
            sync.wait_ge(sty, 16 * nTC)

        @block.gpsimd
        def _(gpsimd):
            for tc in range(nTC):
                for n in range(NSTATE):
                    q = tc * NSTATE + n
                    if q >= 2:
                        tcp, npp = divmod(q - 2, NSTATE)
                        gpsimd.wait_ge(dv, dv_step(tcp, npp))
                    for g in range(nG):
                        gpsimd.dma_start(out=s_B[q % 2][:, g * TCH:(g + 1) * TCH],
                                         in_=b_in[n:n + 1, tc * TCH:(tc + 1) * TCH].broadcast_to([128, TCH])).then_inc(lb[q % 2], 16)
                        gpsimd.dma_start(out=s_C[q % 2][:, g * TCH:(g + 1) * TCH],
                                         in_=c_in[n:n + 1, tc * TCH:(tc + 1) * TCH].broadcast_to([128, TCH])).then_inc(lb[q % 2], 16)

        @block.scalar
        def _(scalar):
            scalar.wait_ge(lds, 32)
            scalar.activation(s_al[:], s_al[:], AF.Exp).then_inc(ac, 1)
            for tc in range(nTC):
                scalar.wait_ge(lt0, 16 * (tc + 1))
                scalar.wait_ge(dv, 1)
                for n in range(NSTATE):
                    q = tc * NSTATE + n
                    if q >= 2:
                        tcp, npp = divmod(q - 2, NSTATE)
                        scalar.wait_ge(dv, dv_step(tcp, npp))
                    for g in range(nG):
                        ins = scalar.activation(s_dA[q % 2][:, g * TCH:(g + 1) * TCH],
                                                s_dt[:, g * TCH:(g + 1) * TCH],
                                                AF.Exp,
                                                scale=s_al[:, g * NSTATE + n:g * NSTATE + n + 1])
                        if g == nG - 1:
                            ins.then_inc(ac, 1)

        @block.vector
        def _(vector):
            vector.wait_ge(ac, 1)
            vector.tensor_scalar(s_al[:], s_al[:], -1.0, None, AluOpType.mult)
            vector.drain()
            vector.sem_inc(dv, 1)
            for tc in range(nTC):
                vector.wait_ge(lt0, 16 * (tc + 1))
                vector.wait_ge(lt1, 16 * (tc + 1))
                vector.tensor_tensor(s_u[:], s_dt[:], s_xc[:], AluOpType.mult)
                vector.drain()
                vector.sem_inc(dv, 1)
                for n in range(NSTATE):
                    q = tc * NSTATE + n
                    vector.wait_ge(lb[q % 2], 128 * (q // 2 + 1))
                    vector.tensor_tensor(s_W[:], s_u[:], s_B[q % 2][:], AluOpType.mult)
                    vector.drain()
                    vector.wait_ge(ac, ac_dA(q))
                    for g in range(nG):
                        ini = s_state[:, g * NSTATE + n:g * NSTATE + n + 1] if tc > 0 else 0.0
                        vector.tensor_tensor_scan(s_h[:, g * TCH:(g + 1) * TCH],
                                                  s_dA[q % 2][:, g * TCH:(g + 1) * TCH],
                                                  s_W[:, g * TCH:(g + 1) * TCH],
                                                  ini, AluOpType.mult, AluOpType.add)
                    vector.drain()
                    for g in range(nG):
                        vector.tensor_copy(s_state[:, g * NSTATE + n:g * NSTATE + n + 1],
                                           s_h[:, (g + 1) * TCH - 1:(g + 1) * TCH])
                    if n == 0:
                        if tc >= 1:
                            vector.wait_ge(sty, 16 * tc)
                        for g in range(nG):
                            vector.tensor_scalar(s_y[:, g * TCH:(g + 1) * TCH],
                                                 s_xc[:, g * TCH:(g + 1) * TCH],
                                                 s_D[:, g:g + 1], None, AluOpType.mult)
                        vector.drain()
                    vector.tensor_tensor(s_W[:], s_h[:], s_C[q % 2][:], AluOpType.mult)
                    vector.drain()
                    vector.tensor_tensor(s_y[:], s_y[:], s_W[:], AluOpType.add).then_inc(dv, 1)
    return nc


def _launch_scan(dtT_list, xcT_byd, Bm, Cm, A_log, Dvec):
    nc = _cached("scan", _mk_scan)
    ims = []
    for c in range(NC):
        sl = slice(c * DLOC, (c + 1) * DLOC)
        ims.append(dict(dtT=np.ascontiguousarray(dtT_list[c]),
                        xcT=np.ascontiguousarray(xcT_byd[c]),
                        Bm=_f32(Bm), Cm=_f32(Cm),
                        Alog=_f32(np.asarray(A_log)[sl]),
                        Dv=np.ascontiguousarray(_f32(np.asarray(Dvec)[sl]).reshape(DLOC // 128, 128).T)))
    return [r["yT"] for r in _run(nc, ims)]


def _mk_mul(O, T):
    nOT = O // 128
    nc = bass.Bass("TRN2", target_bir_lowering=False)
    a_in = nc.dram_tensor("a", [O, T], F32, kind="ExternalInput").ap()
    b_in = nc.dram_tensor("b", [O, T], F32, kind="ExternalInput").ap()
    o_out = nc.dram_tensor("o", [O, T], BF16, kind="ExternalOutput").ap()
    with ExitStack() as ctx:
        sb, ps, sem = _ctx(nc, ctx)
        s_a = [sb(f"s_a{p}", [128, T]) for p in range(2)]
        s_b = [sb(f"s_b{p}", [128, T]) for p in range(2)]
        s_o = [sb(f"s_o{p}", [128, T], BF16) for p in range(2)]
        la0, la1, dv, st = (sem(n) for n in ["la0", "la1", "dv", "st"])
        la = [la0, la1]
        block = ctx.enter_context(nc.Block())

        @block.sync
        def _(sync):
            for ot in range(min(2, nOT)):
                sync.dma_start(out=s_a[ot % 2][:], in_=a_in[ot * 128:(ot + 1) * 128, :]).then_inc(la[ot % 2], 16)
                sync.dma_start(out=s_b[ot % 2][:], in_=b_in[ot * 128:(ot + 1) * 128, :]).then_inc(la[ot % 2], 16)
            for ot in range(nOT):
                sync.wait_ge(dv, ot + 1)
                sync.dma_start(out=o_out[ot * 128:(ot + 1) * 128, :], in_=s_o[ot % 2][:]).then_inc(st, 16)
                if ot + 2 < nOT:
                    sync.dma_start(out=s_a[ot % 2][:], in_=a_in[(ot + 2) * 128:(ot + 3) * 128, :]).then_inc(la[ot % 2], 16)
                    sync.dma_start(out=s_b[ot % 2][:], in_=b_in[(ot + 2) * 128:(ot + 3) * 128, :]).then_inc(la[ot % 2], 16)
            sync.wait_ge(st, 16 * nOT)

        @block.vector
        def _(vector):
            for ot in range(nOT):
                vector.wait_ge(la[ot % 2], 32 * (ot // 2 + 1))
                if ot >= 2:
                    vector.wait_ge(st, 16 * (ot - 1))
                vector.tensor_tensor(s_o[ot % 2][:], s_a[ot % 2][:], s_b[ot % 2][:],
                                     AluOpType.mult)
                vector.drain()
                vector.sem_inc(dv, 1)
    return nc


def _launch_mul(key, a_list, b_list):
    O, T = a_list[0].shape
    nc = _cached((key, "mul", O, T), lambda: _mk_mul(O, T))
    ims = [dict(a=np.ascontiguousarray(a), b=np.ascontiguousarray(b))
           for a, b in zip(a_list, b_list)]
    return [r["o"] for r in _run(nc, ims)]


def _mk_epilogue(with_silu, with_ln):
    TT = TLOC // 128
    nJ = H // 128
    nc = bass.Bass("TRN2", target_bir_lowering=False)
    x_in = nc.dram_tensor("x", [TLOC, H], F32, kind="ExternalInput").ap()
    a_in = nc.dram_tensor("a", [TLOC, H], F32, kind="ExternalInput").ap()
    b_in = nc.dram_tensor("b", [TLOC, H], F32, kind="ExternalInput").ap()
    xo_out = nc.dram_tensor("xo", [TLOC, H], F32, kind="ExternalOutput").ap()
    if with_ln:
        idn = nc.dram_tensor("idn", [128, 128], BF16, kind="ExternalInput").ap()
        xnt_out = nc.dram_tensor("xnt", [H, TLOC], BF16, kind="ExternalOutput").ap()
    with ExitStack() as ctx:
        sb, ps, sem = _ctx(nc, ctx)
        s_x = [sb(f"s_x{p}", [128, H]) for p in range(2)]
        s_a = [sb(f"s_a{p}", [128, H]) for p in range(2)]
        s_b = [sb(f"s_b{p}", [128, H]) for p in range(2)]
        s_sq = sb("s_sq", [128, H])
        s_m = sb("s_m", [128, 1])
        s_r = sb("s_r", [128, 1])
        s_t1 = sb("s_t1", [128, 1])
        s_t2 = sb("s_t2", [128, 1])
        if with_ln:
            s_xn = [sb(f"s_xn{p}", [128, H], BF16) for p in range(2)]
            s_id = sb("s_id", [128, 128], BF16)
            s_out = [sb(f"s_out{p}", [128, 128], BF16) for p in range(4)]
            p_tp = [ps(f"p_tp{p}", [128, 128], BF16) for p in range(2)]
        lds, le0, le1, ac, dv, pe, ev = (sem(n) for n in
            ["lds", "le0", "le1", "ac", "dv", "pe", "ev"])
        stx = [sem("stx0"), sem("stx1")]
        stb = [sem(f"st{b}") for b in range(4)]
        le = [le0, le1]
        block = ctx.enter_context(nc.Block())
        NST = (1 + nJ) if with_ln else 1
        DVT = 4 if with_ln else 1
        ACT = (1 if with_silu else 0) + (2 if with_ln else 0)

        @block.sync
        def _(sync):
            if with_ln:
                sync.dma_start(out=s_id[:], in_=idn).then_inc(lds, 16)
            for it in range(min(2, TT)):
                sl = slice(it * 128, (it + 1) * 128)
                sync.dma_start(out=s_x[it % 2][:], in_=x_in[sl, :]).then_inc(le[it % 2], 16)
                sync.dma_start(out=s_a[it % 2][:], in_=a_in[sl, :]).then_inc(le[it % 2], 16)
                sync.dma_start(out=s_b[it % 2][:], in_=b_in[sl, :]).then_inc(le[it % 2], 16)
            for it in range(TT):
                sync.wait_ge(dv, DVT * it + 1)
                sync.dma_start(out=xo_out[it * 128:(it + 1) * 128, :],
                               in_=s_x[it % 2][:]).then_inc(stx[it % 2], 16)
                if with_ln:
                    for j in range(nJ):
                        k = it * nJ + j
                        sync.wait_ge(ev, k + 1)
                        sync.dma_start(out=xnt_out[j * 128:(j + 1) * 128, it * 128:(it + 1) * 128],
                                       in_=s_out[k % 4][:]).then_inc(stb[k % 4], 16)
                if it + 2 < TT:
                    sync.wait_ge(dv, DVT * it + DVT)
                    sync.wait_ge(stx[it % 2], 16 * (it // 2 + 1))
                    sl = slice((it + 2) * 128, (it + 3) * 128)
                    sync.dma_start(out=s_x[it % 2][:], in_=x_in[sl, :]).then_inc(le[it % 2], 16)
                    sync.dma_start(out=s_a[it % 2][:], in_=a_in[sl, :]).then_inc(le[it % 2], 16)
                    sync.dma_start(out=s_b[it % 2][:], in_=b_in[sl, :]).then_inc(le[it % 2], 16)
            sync.wait_ge(stx[0], 16 * ((TT + 1) // 2))
            sync.wait_ge(stx[1], 16 * (TT // 2))
            if with_ln:
                for b in range(4):
                    sync.wait_ge(stb[b], 16 * ((nJ * TT - b + 3) // 4))

        @block.scalar
        def _(scalar):
            for it in range(TT):
                scalar.wait_ge(le[it % 2], 48 * (it // 2 + 1))
                if with_silu:
                    scalar.activation(s_b[it % 2][:], s_b[it % 2][:], AF.Silu).then_inc(ac, 1)
                if with_ln:
                    scalar.wait_ge(dv, DVT * it + 1)
                    if it >= 1:
                        scalar.wait_ge(dv, DVT * (it - 1) + 3)
                    scalar.activation(s_sq[:], s_x[it % 2][:], AF.Square).then_inc(ac, 1)
                    scalar.wait_ge(dv, DVT * it + 3)
                    scalar.activation(s_t2[:, 0:1], s_t1[:, 0:1], AF.Ln)
                    scalar.drain()
                    scalar.activation(s_r[:, 0:1], s_t2[:, 0:1], AF.Exp, scale=-0.5).then_inc(ac, 1)

        @block.vector
        def _(vector):
            for it in range(TT):
                vector.wait_ge(le[it % 2], 48 * (it // 2 + 1))
                if with_silu:
                    vector.wait_ge(ac, ACT * it + 1)
                vector.tensor_tensor(s_a[it % 2][:], s_a[it % 2][:], s_b[it % 2][:], AluOpType.mult)
                vector.drain()
                vector.tensor_tensor(s_x[it % 2][:], s_x[it % 2][:], s_a[it % 2][:], AluOpType.add)
                vector.drain()
                vector.sem_inc(dv, 1)
                if with_ln:
                    x = s_x[it % 2]
                    vector.reduce_sum(s_m[:, 0:1], x[:], axis=AX)
                    vector.drain()
                    vector.tensor_scalar(s_m[:, 0:1], s_m[:, 0:1], 1.0 / H, None,
                                         AluOpType.mult).then_inc(dv, 1)
                    vector.wait_ge(ac, ACT * it + (2 if with_silu else 1))
                    vector.reduce_sum(s_t1[:, 0:1], s_sq[:], axis=AX)
                    vector.drain()
                    vector.tensor_tensor(s_t2[:, 0:1], s_m[:, 0:1], s_m[:, 0:1], AluOpType.mult)
                    vector.drain()
                    vector.tensor_scalar(s_t1[:, 0:1], s_t1[:, 0:1], 1.0 / H, None, AluOpType.mult)
                    vector.drain()
                    vector.tensor_scalar(s_t1[:, 0:1], s_t1[:, 0:1], s_t2[:, 0:1], 1e-5,
                                         AluOpType.subtract, AluOpType.add).then_inc(dv, 1)
                    vector.wait_ge(ac, ACT * it + ACT)
                    if it >= 2:
                        vector.wait_ge(pe, nJ * (it - 1))
                    vector.tensor_scalar(s_xn[it % 2][:], x[:], s_m[:, 0:1], s_r[:, 0:1],
                                         AluOpType.subtract, AluOpType.mult).then_inc(dv, 1)
                    for j in range(nJ):
                        k = it * nJ + j
                        vector.wait_ge(pe, k + 1)
                        if k >= 4:
                            vector.wait_ge(stb[k % 4], 16 * (k // 4))
                        vector.tensor_copy(s_out[k % 4][:], p_tp[k % 2][:]).then_inc(ev, 1)

        if with_ln:
            @block.tensor
            def _(tensor):
                tensor.wait_ge(lds, 16)
                for it in range(TT):
                    tensor.wait_ge(dv, DVT * it + 4)
                    for j in range(nJ):
                        k = it * nJ + j
                        if k >= 2:
                            tensor.wait_ge(ev, k - 1)
                        tensor.transpose(p_tp[k % 2][:], s_xn[it % 2][:, j * 128:(j + 1) * 128],
                                         s_id[:]).then_inc(pe, 1)
    return nc


def _launch_epilogue(key, with_silu, with_ln, x_list, a_list, b_list):
    nc = _cached((key, with_silu, with_ln), lambda: _mk_epilogue(with_silu, with_ln))
    idn = np.eye(128, dtype=BF)
    ims = []
    for c in range(NC):
        im = dict(x=np.ascontiguousarray(x_list[c]), a=np.ascontiguousarray(a_list[c]),
                  b=np.ascontiguousarray(b_list[c]))
        if with_ln:
            im["idn"] = idn
        ims.append(im)
    return _run(nc, ims)


def _mk_attn():
    nJT = L // 128
    nIT = TLOC // 128
    nHT = H // 128
    NM = nJT - 1
    UH = nIT + NM

    def I0(j):
        return (j + 1) // 8

    nc = bass.Bass("TRN2", target_bir_lowering=False)
    q_in = nc.dram_tensor("qT", [H, TLOC], BF16, kind="ExternalInput").ap()
    k_in = nc.dram_tensor("kT", [DH, L], BF16, kind="ExternalInput").ap()
    v_in = nc.dram_tensor("vtok", [L, DH], BF16, kind="ExternalInput").ap()
    kl_in = nc.dram_tensor("kloc", [DH, TLOC], BF16, kind="ExternalInput").ap()
    vl_in = nc.dram_tensor("vloc", [TLOC, DH], BF16, kind="ExternalInput").ap()
    wo_in = nc.dram_tensor("woT", [H, H], F32, kind="ExternalInput").ap()
    bo_in = nc.dram_tensor("bo", [128, H // 128], F32, kind="ExternalInput").ap()
    mv_in = nc.dram_tensor("mvec", [nJT, TLOC], BF16, kind="ExternalInput").ap()
    sel_in = nc.dram_tensor("sel64", [64, 64 * 128], BF16, kind="ExternalInput").ap()
    dm_in = nc.dram_tensor("dmask", [128, 128], F32, kind="ExternalInput").ap()
    lf_in = nc.dram_tensor("lfix", [1, TLOC], F32, kind="ExternalInput").ap()
    rm_in = nc.dram_tensor("rmask", [1, TLOC], F32, kind="ExternalInput").ap()
    on1_in = nc.dram_tensor("one1", [128, 1], BF16, kind="ExternalInput").ap()
    s1_out = nc.dram_tensor("s1T", [H, TLOC], F32, kind="ExternalOutput").ap()
    rb_dram = nc.dram_tensor("rb_dram", [NH, TLOC], F32)
    with ExitStack() as ctx:
        sb, ps, sem = _ctx(nc, ctx)
        s_q = sb("s_q", [128, NH * TLOC], BF16)
        s_k = sb("s_k", [128, L], BF16)
        s_v = sb("s_v", [128, nJT * DH], BF16)
        s_kl = sb("s_kl", [128, TLOC], BF16)
        s_vl = sb("s_vl", [128, nIT * DH], BF16)
        s_mv = sb("s_mv", [64, TLOC], BF16)
        s_sel = sb("s_sel", [64, 64 * 128], BF16)
        s_dm = sb("s_dm", [128, 128])
        s_lf = sb("s_lf", [1, TLOC])
        s_rm = sb("s_rm", [1, TLOC])
        s_onec = sb("s_onec", [128, 1], BF16)
        s_P = [sb(f"s_P{p}", [128, 1024], BF16) for p in range(3)]
        s_l = sb("s_l", [1, TLOC])
        s_lr = sb("s_lr", [1, TLOC])
        s_rb = [sb(f"s_rb{p}", [128, TLOC]) for p in range(2)]
        s_ao = sb("s_ao", [128, NH * TLOC], BF16)
        s_wo = [sb(f"s_wo{p}", [128, nHT * 128], BF16) for p in range(2)]
        s_bo = sb("s_bo", [128, nHT])
        s_st = [sb(f"s_st{p}", [128, 512]) for p in range(4)]
        p_S = [ps(f"p_S{p}", [128, 1024]) for p in range(2)]
        p_O = ps("p_O", [128, 1024])
        p_l = ps("p_l", [128, 1024])
        ld, ldm, lw0, lw1, vd, mm, ac, lv, oe, pvd, pvu, mm2, ev = (
            sem(n) for n in ["ld", "ldm", "lw0", "lw1", "vd", "mm", "ac", "lv", "oe",
                             "pvd", "pvu", "mm2", "ev"])
        stb = [sem(f"st{b}") for b in range(4)]
        lw = [lw0, lw1]
        block = ctx.enter_context(nc.Block())
        NLD = 12

        def emit_pv(tensor, w):
            h, u = divmod(w, UH)
            last = (u == UH - 1)
            if u == 0 and h >= 1:
                tensor.wait_ge(oe, h)
            tensor.wait_ge(ac, w + 1)
            if u < nIT:
                i = u
                st0 = (i == 0 or i == 4)
                tensor.matmul(p_O[:, i * 128:(i + 1) * 128],
                              s_vl[:, i * 128:(i + 1) * 128], s_P[w % 3][:, 0:128],
                              start=st0, stop=False)
                ins = tensor.matmul(p_l[0:1, i * 128:(i + 1) * 128],
                                    s_onec[:, 0:1], s_P[w % 3][:, 0:128],
                                    start=st0, stop=False)
            else:
                j = u - nIT
                t0 = I0(j) * 128
                chunks = ([(t0, 512), (512, 1024)] if t0 < 512 else [(t0, 1024)])
                ml = None
                for a, b in chunks:
                    stp = (j == 30 and b == 512) or (j == 62)
                    tensor.matmul(p_O[:, a:b],
                                  s_v[:, j * 128:(j + 1) * 128], s_P[w % 3][:, a:b],
                                  start=False, stop=stp)
                    ml = tensor.matmul(p_l[0:1, a:b],
                                       s_onec[:, 0:1], s_P[w % 3][:, a:b],
                                       start=False, stop=stp)
                ml.then_inc(pvu, 1)
                return
            ins.then_inc(pvu, 1)

        @block.sync
        def _(sync):
            for dst, src in [(s_q, q_in.rearrange("(ht p) t -> p ht t", p=128)),
                             (s_k, k_in), (s_v, v_in.rearrange("(j p) d -> p j d", p=128)),
                             (s_kl, kl_in), (s_vl, vl_in.rearrange("(i p) d -> p i d", p=128)),
                             (s_mv, mv_in), (s_sel, sel_in), (s_dm, dm_in),
                             (s_lf, lf_in), (s_rm, rm_in), (s_onec, on1_in)]:
                sync.dma_start(out=dst[:], in_=src).then_inc(ld, 16)
            sync.dma_start(out=s_bo[:], in_=bo_in).then_inc(ld, 16)
            for h in range(NH):
                sync.wait_ge(lv, h + 1)
                sync.dma_start(out=rb_dram.ap()[h:h + 1, :], in_=s_lr[:]).then_inc(ldm, 16)
            sync.wait_ge(ldm, 16 * NH)
            for h in range(NH):
                if h >= 2:
                    sync.wait_ge(oe, NH + h - 1)
                sync.dma_start(out=s_rb[h % 2][:],
                               in_=rb_dram.ap()[h:h + 1, :].broadcast_to([128, TLOC])).then_inc(ldm, 16)
            for k in range(nHT * 2):
                sync.wait_ge(ev, k + 1)
                ot, ch = divmod(k, 2)
                sync.dma_start(out=s1_out[ot * 128:(ot + 1) * 128, ch * 512:(ch + 1) * 512],
                               in_=s_st[k % 4][:]).then_inc(stb[k % 4], 16)
            for b in range(4):
                sync.wait_ge(stb[b], 16 * ((nHT * 2 - b + 3) // 4))

        @block.gpsimd
        def _(gpsimd):
            for ot in range(nHT):
                if ot >= 2:
                    gpsimd.wait_ge(mm2, (ot - 1) * 2)
                gpsimd.dma_start(out=s_wo[ot % 2][:],
                                 in_=wo_in[:, ot * 128:(ot + 1) * 128].rearrange(
                                     "(kt p) o -> p kt o", p=128)).then_inc(lw[ot % 2], 16)

        @block.tensor
        def _(tensor):
            tensor.wait_ge(ld, 16 * NLD)
            NW = NH * UH
            for w in range(NW):
                h, u = divmod(w, UH)
                if w >= 2:
                    tensor.wait_ge(ac, w - 1)
                if u < nIT:
                    i = u
                    tensor.matmul(p_S[w % 2][:, 0:128],
                                  s_kl[:, i * 128:(i + 1) * 128],
                                  s_q[:, h * TLOC + i * 128:h * TLOC + (i + 1) * 128],
                                  start=True, stop=True).then_inc(mm, 1)
                else:
                    j = u - nIT
                    t0 = I0(j) * 128
                    chunks = ([(t0, 512), (512, 1024)] if t0 < 512 else [(t0, 1024)])
                    chunks = [(a, b) for a, b in chunks if b - a <= 512] or [(t0, 512), (512, 1024)]
                    for ci, (a, b) in enumerate(chunks):
                        tensor.matmul(p_S[w % 2][:, a:b],
                                      s_k[:, j * 128:(j + 1) * 128],
                                      s_q[:, h * TLOC + a:h * TLOC + b],
                                      start=True, stop=False)
                        ins = tensor.matmul(p_S[w % 2][:, a:b],
                                            s_sel[:, j * 128:(j + 1) * 128],
                                            s_mv[:, a:b],
                                            start=False, stop=True)
                        if ci == len(chunks) - 1:
                            ins.then_inc(mm, 1)
                if w >= 2:
                    emit_pv(tensor, w - 2)
            for wl in range(NW - 2, NW):
                emit_pv(tensor, wl)
            tensor.wait_ge(oe, 2 * NH)
            for ot in range(nHT):
                tensor.wait_ge(lw[ot % 2], 16 * (ot // 2 + 1))
                for ch in range(2):
                    k = ot * 2 + ch
                    if k >= 2:
                        tensor.wait_ge(ev, k - 1)
                    for kt in range(nHT):
                        ins = tensor.matmul(p_S[k % 2][:, 0:512],
                                            s_wo[ot % 2][:, kt * 128:(kt + 1) * 128],
                                            s_ao[:, kt * TLOC + ch * 512:kt * TLOC + ch * 512 + 512],
                                            start=(kt == 0), stop=(kt == nHT - 1))
                        if kt == nHT - 1:
                            ins.then_inc(mm2, 1)

        @block.scalar
        def _(scalar):
            for h in range(NH):
                for u in range(UH):
                    w = h * UH + u
                    if w >= 3:
                        scalar.wait_ge(pvu, w - 2)
                    if u < nIT:
                        scalar.wait_ge(vd, h * nIT + u + 1)
                        scalar.activation(s_P[w % 3][:, 0:128], p_S[w % 2][:, 0:128],
                                          AF.Exp, scale=SCALE).then_inc(ac, 1)
                    else:
                        t0 = I0(u - nIT) * 128
                        scalar.wait_ge(mm, w + 1)
                        scalar.activation(s_P[w % 3][:, t0:1024], p_S[w % 2][:, t0:1024],
                                          AF.Exp, scale=SCALE).then_inc(ac, 1)
            for k in range(nHT * 2):
                scalar.wait_ge(mm2, k + 1)
                if k >= 4:
                    scalar.wait_ge(stb[k % 4], 16 * (k // 4))
                ot = k // 2
                if SIM_SAFE_ACT:
                    scalar.activation(s_st[k % 4][:], p_S[k % 2][:, 0:512], AF.Relu,
                                      bias=s_bo[:, ot:ot + 1]).then_inc(ev, 1)
                else:
                    scalar.activation(s_st[k % 4][:], p_S[k % 2][:, 0:512], AF.Prelu,
                                      bias=s_bo[:, ot:ot + 1], alpha=1.0).then_inc(ev, 1)

        @block.vector
        def _(vector):
            vector.wait_ge(ld, 16 * NLD)

            def evac_head(h):
                vector.wait_ge(pvu, (h + 1) * UH)
                vector.tensor_copy(s_ao[:, h * TLOC:(h + 1) * TLOC], p_O[:, 0:1024])
                vector.drain()
                vector.tensor_copy(s_l[:, 0:1024], p_l[0:1, 0:1024])
                vector.drain()
                vector.sem_inc(oe, 1)
                vector.tensor_tensor(s_l[:], s_l[:], s_lf[:], AluOpType.add)
                vector.drain()
                vector.reciprocal(s_l[:], s_l[:])
                vector.drain()
                if h >= 1:
                    vector.wait_ge(ldm, 16 * h)
                vector.tensor_tensor(s_lr[:], s_l[:], s_rm[:], AluOpType.mult)
                vector.drain()
                vector.sem_inc(lv, 1)

            def dmask_add(h, i):
                w = h * UH + i
                vector.wait_ge(mm, w + 1)
                vector.tensor_tensor(p_S[w % 2][:, 0:128], p_S[w % 2][:, 0:128],
                                     s_dm[:], AluOpType.add)
                vector.drain()
                vector.sem_inc(vd, 1)

            for h in range(NH):
                dmask_add(h, 0)
                dmask_add(h, 1)
                if h >= 1:
                    evac_head(h - 1)
                for i in range(2, nIT):
                    dmask_add(h, i)
            evac_head(NH - 1)
            for h in range(NH):
                vector.wait_ge(ldm, 16 * NH + 16 * (h + 1))
                vector.tensor_tensor(s_ao[:, h * TLOC:(h + 1) * TLOC],
                                     s_ao[:, h * TLOC:(h + 1) * TLOC],
                                     s_rb[h % 2][:], AluOpType.mult)
                vector.drain()
                vector.sem_inc(oe, 1)
    return nc


def _launch_attn(qT_list, kT_full, v_full, Wo, bo):
    nc = _cached("attn", _mk_attn)
    woT = _f32(np.asarray(Wo).T)
    boc = np.ascontiguousarray(_f32(bo).reshape(H // 128, 128).T)
    dmask = np.where(np.arange(128)[:, None] < np.arange(128)[None, :], -30.0 / SCALE, NEG).astype(np.float32)
    sel64 = np.repeat(np.eye(64, dtype=BF), 128, axis=1)
    one1 = np.ones((128, 1), BF)
    qT_full = np.concatenate(qT_list, axis=1)
    ims = []
    for c in range(NC):
        cols = (np.arange(TLOC) // 128) * 8 * 128 + c * 128 + np.arange(TLOC) % 128
        g_of_t = 8 * (np.arange(TLOC) // 128) + c
        mvec = np.where(np.arange(64)[:, None] < g_of_t[None, :], -30.0 / SCALE, NEG).astype(BF)
        lfix = np.zeros((1, TLOC), np.float32)
        rmask = np.ones((1, TLOC), np.float32)
        if c == 0:
            lfix[0, 0] = 1.0
            rmask[0, 0] = 0.0
        ims.append(dict(qT=np.ascontiguousarray(qT_full[:, cols]), kT=kT_full, vtok=v_full,
                        kloc=np.ascontiguousarray(kT_full[:, cols]),
                        vloc=np.ascontiguousarray(v_full[cols, :]),
                        woT=woT, bo=boc, mvec=mvec, sel64=sel64, dmask=dmask,
                        lfix=lfix, rmask=rmask, one1=one1))
    res = _run(nc, ims)
    s1_full = np.empty((H, L), np.float32)
    for c in range(NC):
        cols = (np.arange(TLOC) // 128) * 8 * 128 + c * 128 + np.arange(TLOC) % 128
        s1_full[:, cols] = res[c]["s1T"]
    return [np.ascontiguousarray(s1_full[:, c * TLOC:(c + 1) * TLOC]) for c in range(NC)]
def _mk_moe():
    nKT = H // 128
    nOT = H // 128
    nTT = TLOC // 128
    NPH = 3 * EX + 3
    nc = bass.Bass("TRN2", target_bir_lowering=False)
    x_in = nc.dram_tensor("xn1T", [H, TLOC], BF16, kind="ExternalInput").ap()
    wr_in = nc.dram_tensor("Wr", [H, EX], F32, kind="ExternalInput").ap()
    br_in = nc.dram_tensor("br", [EX, 1], F32, kind="ExternalInput").ap()
    gb_in = nc.dram_tensor("gbb", [128, EX], F32, kind="ExternalInput").ap()
    wup_in = nc.dram_tensor("Wup", [EX * H, H], F32, kind="ExternalInput").ap()
    wgt_in = nc.dram_tensor("Wgate", [EX * H, H], F32, kind="ExternalInput").ap()
    wdn_in = nc.dram_tensor("Wdown", [EX * H, H], F32, kind="ExternalInput").ap()
    bup_in = nc.dram_tensor("bup", [128, EX * H // 128], F32, kind="ExternalInput").ap()
    bgt_in = nc.dram_tensor("bgate", [128, EX * H // 128], F32, kind="ExternalInput").ap()
    bdn_in = nc.dram_tensor("bdn", [EX, H], F32, kind="ExternalInput").ap()
    wsu_in = nc.dram_tensor("Wsu", [H, H], F32, kind="ExternalInput").ap()
    wsg_in = nc.dram_tensor("Wsg", [H, H], F32, kind="ExternalInput").ap()
    wsd_in = nc.dram_tensor("Wsd", [H, H], F32, kind="ExternalInput").ap()
    bsu_in = nc.dram_tensor("bsu", [128, H // 128], F32, kind="ExternalInput").ap()
    bsg_in = nc.dram_tensor("bsg", [128, H // 128], F32, kind="ExternalInput").ap()
    bsd_in = nc.dram_tensor("bsd", [128, H // 128], F32, kind="ExternalInput").ap()
    idf_in = nc.dram_tensor("idf", [128, 128], F32, kind="ExternalInput").ap()
    acc_out = nc.dram_tensor("accT", [H, TLOC], F32, kind="ExternalOutput").ap()
    pr_dram = nc.dram_tensor("pr_dram", [EX, TLOC], BF16)

    W_of = {}
    for e in range(EX):
        W_of[3 * e] = ("up", e, wup_in, e * H)
        W_of[3 * e + 1] = ("gate", e, wgt_in, e * H)
        W_of[3 * e + 2] = ("down", e, wdn_in, e * H)
    W_of[21] = ("up", -1, wsu_in, 0)
    W_of[22] = ("gate", -1, wsg_in, 0)
    W_of[23] = ("down", -1, wsd_in, 0)

    with ExitStack() as ctx:
        sb, ps, sem = _ctx(nc, ctx)
        s_x = sb("s_x", [128, nKT * TLOC], BF16)
        s_u = sb("s_u", [128, nOT * TLOC], BF16)
        s_sg = sb("s_sg", [128, nOT * TLOC], BF16)
        s_acc = sb("s_acc", [128, nOT * TLOC])
        s_w = [sb(f"s_w{p}", [128, nKT * 128], BF16) for p in range(2)]
        s_wr = sb("s_wr", [128, nKT * EX], BF16)
        s_br = sb("s_br", [EX, 1])
        s_gbb = sb("s_gbb", [128, EX])
        s_bup = sb("s_bup", [128, EX * nOT])
        s_bgt = sb("s_bgt", [128, EX * nOT])
        s_bd16 = sb("s_bd16", [EX, H], BF16)
        s_bsu = sb("s_bsu", [128, nOT])
        s_bsg = sb("s_bsg", [128, nOT])
        s_bsd = sb("s_bsd", [128, nOT])
        s_idf = sb("s_idf", [128, 128])
        s_g = sb("s_g", [128, TLOC])
        s_gt = sb("s_gt", [128, nTT * EX])
        s_pt = sb("s_pt", [128, nTT * EX])
        s_prT = sb("s_prT", [EX, TLOC])
        s_prT16 = sb("s_prT16", [EX, TLOC], BF16)
        s_pb = sb("s_pb", [128, TLOC], BF16)
        s_m1 = sb("s_m1", [128, 1])
        s_m2 = sb("s_m2", [128, 1])
        s_nm = sb("s_nm", [128, 1])
        s_z = sb("s_z", [128, 1])
        s_e7 = sb("s_e7", [128, EX])
        s_bg = sb("s_bg", [128, EX])
        s_msk = sb("s_msk", [128, EX])
        s_t7 = sb("s_t7", [128, EX])
        p_mm = [ps(f"p_mm{p}", [128, 512]) for p in range(2)]
        p_t = ps("p_t", [128, 128])
        ld, ldg, lw0, lw1, mm, eva, evv, te, vv, g1, g2, pv2, gv, lpb, fin, st = (
            sem(n) for n in ["ld", "ldg", "lw0", "lw1", "mm", "eva", "evv", "te", "vv", "g1",
                             "g2", "pv2", "gv", "lpb", "fin", "st"])
        lpr = sem("lpr")

        def SUG(q):
            return sum(1 for qq in range(q) if qq % 3 != 2 if qq < 21) + \
                   sum(1 for qq in range(21, q) if qq in (21, 22))

        def SD(q):
            return sum(1 for qq in range(min(q, 24)) if (qq < 21 and qq % 3 == 2) or qq == 23)

        def evac_of_chunk(c):
            if c < 2:
                return ("v", c + 1)
            if c < 2 + NPH * 32:
                q, r = divmod(c - 2, 32)
                kind = W_of[q][0]
                if kind == "down":
                    return ("v", 2 + 32 * SD(q) + r + 1)
                return ("a", 32 * SUG(q) + r + 1)
            r = c - (2 + NPH * 32)
            return ("v", 2 + 32 * (SD(24)) + r + 1)
        lw = [lw0, lw1]
        block = ctx.enter_context(nc.Block())
        MM_GATE, MM_GT, MM_PT, MM_EXP, MM_BD = 0, 2, 10, 18, 786
        EV_GATE, EV_EXP, EV_BD = 0, 2, 770
        def vv_after(e):
            return 32 * (e + 1)
        VV_SH = 32 * EX + 16

        @block.sync
        def _(sync):
            for dst, src in [
                (s_x, x_in.rearrange("(kt p) t -> p kt t", p=128)),
                (s_br, br_in),
                (s_gbb, gb_in),
                (s_bup, bup_in),
                (s_bgt, bgt_in),
                (s_bsu, bsu_in),
                (s_bsg, bsg_in),
                (s_bsd, bsd_in),
                (s_idf, idf_in),
            ]:
                sync.dma_start(out=dst[:], in_=src).then_inc(ld, 16)
            sync.wait_ge(gv, 1)
            sync.dma_start(out=pr_dram.ap(), in_=s_prT16[:]).then_inc(lpr, 16)
            for ot in range(nOT):
                sync.wait_ge(fin, ot + 1)
                sync.dma_start(out=acc_out[ot * 128:(ot + 1) * 128, :],
                               in_=s_acc[:, ot * TLOC:(ot + 1) * TLOC]).then_inc(st, 16)
            sync.wait_ge(st, 16 * nOT)

        @block.gpsimd
        def _(gpsimd):
            gpsimd.dma_start(out=s_wr[:], in_=wr_in.rearrange("(kt p) e -> p kt e", p=128)).then_inc(ldg, 16)
            gpsimd.dma_start(out=s_bd16[:], in_=bdn_in).then_inc(ldg, 16)
            for q in range(NPH):
                kind, e, w_dram, row0 = W_of[q]
                for ot in range(nOT):
                    u = q * nOT + ot
                    if u >= 2:
                        gpsimd.wait_ge(mm, MM_EXP + (u - 1) * 2)
                    gpsimd.dma_start(out=s_w[u % 2][:],
                                     in_=w_dram[row0:row0 + H, ot * 128:(ot + 1) * 128].rearrange(
                                         "(kt p) o -> p kt o", p=128)).then_inc(lw[u % 2], 16)
                if kind == "up" and e >= 0:
                    gpsimd.wait_ge(lpr, 16)
                    if e >= 1:
                        gpsimd.wait_ge(vv, vv_after(e - 1))
                    gpsimd.dma_start(out=s_pb[:],
                                     in_=pr_dram.ap()[e:e + 1, :].broadcast_to([128, TLOC])).then_inc(lpb, 16)

        @block.tensor
        def _(tensor):
            tensor.wait_ge(ld, 16 * 9)
            tensor.wait_ge(ldg, 32)
            for ch in range(2):
                for kt in range(nKT):
                    ins = tensor.matmul(p_mm[ch][0:EX, 0:512],
                                        s_wr[:, kt * EX:(kt + 1) * EX],
                                        s_x[:, kt * TLOC + ch * 512:kt * TLOC + ch * 512 + 512],
                                        start=(kt == 0), stop=(kt == nKT - 1))
                    if kt == nKT - 1:
                        ins.then_inc(mm, 1)
            for tt in range(nTT):
                tensor.wait_ge(evv, 2)
                tensor.wait_ge(te, tt)
                tensor.transpose(p_t[:, 0:128], s_g[:, tt * 128:(tt + 1) * 128],
                                 s_idf[:]).then_inc(mm, 1)
            for tt in range(nTT):
                tensor.wait_ge(pv2, tt + 1)
                tensor.wait_ge(te, nTT + tt)
                tensor.transpose(p_t[0:EX, 0:128], s_pt[:, tt * EX:(tt + 1) * EX],
                                 s_idf[:]).then_inc(mm, 1)
            for q in range(NPH):
                kind, e, w_dram, row0 = W_of[q]
                rhs = {"up": s_x, "gate": s_u, "down": s_sg}[kind]
                if kind == "gate":
                    tensor.wait_ge(eva, 32 * (SUG(q - 1) + 1))
                if kind == "down":
                    tensor.wait_ge(vv, vv_after(e) if e >= 0 else VV_SH)
                if q == 21:
                    tensor.wait_ge(vv, vv_after(EX - 1))
                for ot in range(nOT):
                    u = q * nOT + ot
                    tensor.wait_ge(lw[u % 2], 16 * (u // 2 + 1))
                    for ch in range(2):
                        c = u * 2 + ch
                        cg = 2 + c
                        kk, cnt = evac_of_chunk(cg - 2)
                        tensor.wait_ge(eva if kk == "a" else evv, cnt)
                        for kt in range(nKT):
                            ins = tensor.matmul(p_mm[c % 2][:, 0:512],
                                                s_w[u % 2][:, kt * 128:(kt + 1) * 128],
                                                rhs[:, kt * TLOC + ch * 512:kt * TLOC + ch * 512 + 512],
                                                start=(kt == 0), stop=(kt == nKT - 1))
                            if kt == nKT - 1:
                                ins.then_inc(mm, 1)
            tensor.wait_ge(gv, 2)
            for ot in range(nOT):
                for ch in range(2):
                    c = NPH * 32 + ot * 2 + ch
                    cg = 2 + c
                    kk, cnt = evac_of_chunk(cg - 2)
                    tensor.wait_ge(eva if kk == "a" else evv, cnt)
                    tensor.matmul(p_mm[(cg) % 2][:, 0:512],
                                  s_bd16[:, ot * 128:(ot + 1) * 128],
                                  s_prT16[:, ch * 512:(ch + 1) * 512],
                                  start=True, stop=True).then_inc(mm, 1)

        @block.scalar
        def _(scalar):
            for tt in range(nTT):
                scalar.wait_ge(g1, tt + 1)
                scalar.activation(s_e7[:], s_gt[:, tt * EX:(tt + 1) * EX], AF.Exp,
                                  bias=s_nm[:, 0:1]).then_inc(g2, 1)
            for q in range(NPH):
                kind, e, w_dram, row0 = W_of[q]
                if kind == "down":
                    continue
                dst = s_u if kind == "up" else s_sg
                bias = (s_bup if kind == "up" else s_bgt)
                act = (AF.Relu if SIM_SAFE_ACT else AF.Prelu) if kind == "up" else AF.Silu
                for ot in range(nOT):
                    u = q * nOT + ot
                    for ch in range(2):
                        c = u * 2 + ch
                        scalar.wait_ge(mm, MM_EXP + c + 1)
                        bcol = (bias[:, e * nOT + ot:e * nOT + ot + 1] if e >= 0
                                else (s_bsu if kind == "up" else s_bsg)[:, ot:ot + 1])
                        kw = dict(bias=bcol)
                        if act == AF.Prelu:
                            kw["alpha"] = 1.0
                        scalar.activation(dst[:, ot * TLOC + ch * 512:ot * TLOC + ch * 512 + 512],
                                          p_mm[(2 + c) % 2][:, 0:512], act, **kw).then_inc(eva, 1)

        @block.vector
        def _(vector):
            vector.memset(s_acc[:], 0.0)
            vector.memset(s_g[:], 0.0)
            vector.drain()
            for ch in range(2):
                vector.wait_ge(mm, ch + 1)
                vector.tensor_scalar(s_g[0:EX, ch * 512:(ch + 1) * 512], p_mm[ch][0:EX, 0:512],
                                     s_br[:, 0:1], None, AluOpType.add)
                vector.drain()
                vector.sem_inc(evv, 1)
            for tt in range(nTT):
                vector.wait_ge(mm, MM_GT + tt + 1)
                vector.tensor_copy(s_gt[:, tt * EX:(tt + 1) * EX], p_t[:, 0:EX])
                vector.drain()
                vector.sem_inc(te, 1)
                gsl = s_gt[:, tt * EX:(tt + 1) * EX]
                vector.wait_ge(ld, 16 * 9)
                vector.tensor_tensor(s_bg[:], gsl, s_gbb[:], AluOpType.add)
                vector.drain()
                vector.reduce_max(s_m1[:, 0:1], s_bg[:], axis=AX)
                vector.drain()
                vector.tensor_scalar(s_msk[:], s_bg[:], s_m1[:, 0:1], None, AluOpType.is_equal)
                vector.drain()
                vector.scalar_tensor_tensor(s_t7[:], s_msk[:], NEG, s_bg[:],
                                            AluOpType.mult, AluOpType.add)
                vector.drain()
                vector.reduce_max(s_m2[:, 0:1], s_t7[:], axis=AX)
                vector.drain()
                vector.tensor_scalar(s_msk[:], s_bg[:], s_m2[:, 0:1], None, AluOpType.is_ge)
                vector.drain()
                vector.reduce_max(s_m1[:, 0:1], gsl, axis=AX)
                vector.drain()
                vector.tensor_scalar(s_nm[:, 0:1], s_m1[:, 0:1], -1.0, None,
                                     AluOpType.mult)
                vector.drain()
                vector.sem_inc(g1, 1)
                vector.wait_ge(g2, tt + 1)
                vector.tensor_tensor(s_t7[:], s_msk[:], s_e7[:], AluOpType.mult)
                vector.drain()
                vector.reduce_sum(s_z[:, 0:1], s_t7[:], axis=AX)
                vector.drain()
                vector.reciprocal(s_z[:, 0:1], s_z[:, 0:1])
                vector.drain()
                vector.tensor_scalar(s_pt[:, tt * EX:(tt + 1) * EX], s_t7[:],
                                     s_z[:, 0:1], None, AluOpType.mult)
                vector.drain()
                vector.sem_inc(pv2, 1)
            for tt in range(nTT):
                vector.wait_ge(mm, MM_PT + tt + 1)
                vector.tensor_copy(s_prT[0:EX, tt * 128:(tt + 1) * 128], p_t[0:EX, 0:128])
                vector.drain()
                vector.sem_inc(te, 1)
            vector.tensor_copy(s_prT16[0:EX, :], s_prT[0:EX, :])
            vector.drain()
            vector.sem_inc(gv, 1)
            vector.sem_inc(gv, 1)
            for e in range(EX):
                q = 3 * e + 1
                vector.wait_ge(lpb, 16 * (e + 1))
                for ot in range(nOT):
                    vector.wait_ge(eva, 32 * SUG(q) + ot * 2 + 2)
                    usl = s_u[:, ot * TLOC:(ot + 1) * TLOC]
                    ssl = s_sg[:, ot * TLOC:(ot + 1) * TLOC]
                    vector.tensor_tensor(ssl, ssl, usl, AluOpType.mult)
                    vector.drain()
                    vector.sem_inc(vv, 1)
                    vector.tensor_tensor(ssl, ssl, s_pb[:], AluOpType.mult)
                    vector.drain()
                    vector.sem_inc(vv, 1)
                qd = 3 * e + 2
                for ot in range(nOT):
                    for ch in range(2):
                        c = (qd * nOT + ot) * 2 + ch
                        vector.wait_ge(mm, MM_EXP + c + 1)
                        asl = s_acc[:, ot * TLOC + ch * 512:ot * TLOC + ch * 512 + 512]
                        vector.tensor_tensor(asl, asl, p_mm[(2 + c) % 2][:, 0:512], AluOpType.add)
                        vector.drain()
                        vector.sem_inc(evv, 1)
            q = 22
            for ot in range(nOT):
                vector.wait_ge(eva, 32 * SUG(q) + ot * 2 + 2)
                usl = s_u[:, ot * TLOC:(ot + 1) * TLOC]
                ssl = s_sg[:, ot * TLOC:(ot + 1) * TLOC]
                vector.tensor_tensor(ssl, ssl, usl, AluOpType.mult)
                vector.drain()
                vector.sem_inc(vv, 1)
            qd = 23
            for ot in range(nOT):
                for ch in range(2):
                    c = (qd * nOT + ot) * 2 + ch
                    vector.wait_ge(mm, MM_EXP + c + 1)
                    asl = s_acc[:, ot * TLOC + ch * 512:ot * TLOC + ch * 512 + 512]
                    vector.tensor_tensor(asl, asl, p_mm[(2 + c) % 2][:, 0:512], AluOpType.add)
                    vector.drain()
                    vector.sem_inc(evv, 1)
            for ot in range(nOT):
                asl = s_acc[:, ot * TLOC:(ot + 1) * TLOC]
                vector.tensor_scalar(asl, asl, s_bsd[:, ot:ot + 1], None, AluOpType.add)
                vector.drain()
                for ch in range(2):
                    c = NPH * 32 + ot * 2 + ch
                    vector.wait_ge(mm, MM_EXP + c + 1)
                    a2 = s_acc[:, ot * TLOC + ch * 512:ot * TLOC + ch * 512 + 512]
                    vector.tensor_tensor(a2, a2, p_mm[(2 + c) % 2][:, 0:512], AluOpType.add)
                    vector.drain()
                    vector.sem_inc(evv, 1)
                vector.sem_inc(fin, 1)
    return nc


def _launch_moe(xn1T_list, W):
    nc = _cached("moe", _mk_moe)
    com = dict(
        Wr=_f32(W["Wr"]), br=_f32(W["br"]).reshape(EX, 1),
        gbb=np.tile(_f32(W["gate_bias"]).reshape(1, EX), (128, 1)),
        Wup=_f32(W["Wup"]).reshape(EX * H, H), Wgate=_f32(W["Wgate"]).reshape(EX * H, H),
        Wdown=_f32(W["Wdown"]).reshape(EX * H, H),
        bup=np.ascontiguousarray(_f32(W["bup"]).reshape(-1, 128).T),
        bgate=np.ascontiguousarray(_f32(W["bgate"]).reshape(-1, 128).T),
        bdn=_f32(W["bdown"]), Wsu=_f32(W["Wsu"]), Wsg=_f32(W["Wsg"]), Wsd=_f32(W["Wsd"]),
        bsu=np.ascontiguousarray(_f32(W["bsu"]).reshape(-1, 128).T),
        bsg=np.ascontiguousarray(_f32(W["bsg"]).reshape(-1, 128).T),
        bsd=np.ascontiguousarray(_f32(W["bsd"]).reshape(-1, 128).T), idf=np.eye(128, dtype=np.float32))
    ims = [dict(xn1T=np.ascontiguousarray(x), **com) for x in xn1T_list]
    return [r["accT"] for r in _run(nc, ims)]


def kernel(**inp):
    x0 = _f32(inp["L0_attn_hidden_states"]).reshape(L, H)

    xnT = _launch_ln(x0)
    ipT = _f32(np.asarray(inp["L0_attn_in_proj_w"]).T)
    res = _launch_matmul("inproj", xnT, ipT, np.zeros(2 * DI, np.float32),
                         [dict(name="hsT", O=DI, act=AF.Copy, bias=False, scale=None),
                          dict(name="gsT", O=DI, act=AF.Silu, bias=False, scale=None)],
                         F32)
    hsT = [r["hsT"] for r in res]
    gsT = [r["gsT"][:, 128:] for r in res]
    xcT, ssm_loc = _launch_conv(hsT, inp["L0_attn_conv_w"], inp["L0_attn_conv_b"],
                                inp["L0_attn_x_proj_w"])
    ssmT = np.concatenate(ssm_loc, axis=1)
    tsT, Bm, Cm = ssmT[:TR], ssmT[TR:TR + NSTATE], ssmT[TR + NSTATE:]
    dtT = _launch_dtproj(tsT, inp["L0_attn_dt_proj_w"], inp["L0_attn_dt_proj_b"])
    xcT_full = np.concatenate(xcT, axis=1)
    xcT_byd = [xcT_full[c * DLOC:(c + 1) * DLOC] for c in range(NC)]
    yT_s = _launch_scan(dtT, xcT_byd, Bm, Cm, inp["L0_attn_A_log"], inp["L0_attn_D"])
    yT_full = np.concatenate(yT_s, axis=0)
    yT_tok = [np.ascontiguousarray(yT_full[:, c * TLOC:(c + 1) * TLOC]) for c in range(NC)]
    ygT = _launch_mul("yg", yT_tok, gsT)
    opT = _f32(np.asarray(inp["L0_attn_out_proj_w"]).T)
    res = _launch_matmul("outproj", ygT, opT, np.zeros(H, np.float32),
                         [dict(name="s0T", O=H, act=AF.Copy, bias=False, scale=None)], F32)
    s0T = [r["s0T"] for r in res]
    xnTo = [np.ascontiguousarray(x[:, 128:]) for x in xnT]
    w0T = _f32(np.asarray(inp["L0_ffn_W0"]).T)
    res = _launch_matmul("ffn1", xnTo, w0T, inp["L0_ffn_bias0"],
                         [dict(name="fT", O=4 * H, act=AF.Gelu, bias=True, scale=None)], BF16)
    fT = [r["fT"] for r in res]
    w1T = _f32(np.asarray(inp["L0_ffn_W1"]).T)
    res = _launch_matmul("ffn2", fT, w1T, inp["L0_ffn_bias1"],
                         [dict(name="g0T", O=H, act=AF.Silu, bias=True, scale=None)], F32)
    g0T = [r["g0T"] for r in res]
    res = _launch_epilogue("epi0", False, True,
                           [x0[c * TLOC:(c + 1) * TLOC] for c in range(NC)],
                           [np.ascontiguousarray(s.T) for s in s0T],
                           [np.ascontiguousarray(g.T) for g in g0T])
    x1 = [r["xo"] for r in res]
    xn1T = [r["xnt"] for r in res]

    wqkvT = np.concatenate([_f32(np.asarray(inp["L1_attn_Wq"]).T),
                            _f32(np.asarray(inp["L1_attn_Wk"]).T),
                            _f32(np.asarray(inp["L1_attn_Wv"]).T)], axis=1)
    bqkv = np.concatenate([_f32(inp["L1_attn_bq"]), _f32(inp["L1_attn_bk"]),
                           _f32(inp["L1_attn_bv"])])
    res = _launch_matmul("qkv", xn1T, wqkvT, bqkv,
                         [dict(name="qT", O=H, act=AF.Copy, bias=True, scale=None),
                          dict(name="kT", O=DH, act=AF.Copy, bias=True, scale=None),
                          dict(name="vT", O=DH, act=AF.Copy, bias=True, scale=None)], BF16)
    qT = [r["qT"] for r in res]
    kT_full = np.ascontiguousarray(np.concatenate([r["kT"] for r in res], axis=1))
    v_full = np.ascontiguousarray(np.concatenate([r["vT"] for r in res], axis=1).T)
    s1T = _launch_attn(qT, kT_full, v_full, inp["L1_attn_Wo"], inp["L1_attn_bo"])
    accT = _launch_moe(xn1T, {k[len("L1_ffn_"):]: v for k, v in inp.items()
                              if k.startswith("L1_ffn_")})
    res = _launch_epilogue("epi1", True, False,
                           x1,
                           [np.ascontiguousarray(s.T) for s in s1T],
                           [np.ascontiguousarray(a.T) for a in accT])
    x2 = np.concatenate([r["xo"] for r in res], axis=0)
    return x2.reshape(1, L, H).astype(np.float32)


# version 04
# speedup vs baseline: 1.0046x; 1.0046x over previous
import numpy as np
import ml_dtypes
import concourse.bass as bass
import concourse.mybir as mybir
from concourse.alu_op_type import AluOpType
from concourse.bass_utils import run_bass_kernel_spmd
from contextlib import ExitStack

F32 = mybir.dt.float32
BF16 = mybir.dt.bfloat16
AF = mybir.ActivationFunctionType
AX = mybir.AxisListType.X
BF = ml_dtypes.bfloat16

NC = 8
L = 8192
H = 2048
DI = 4096
NSTATE = 16
TR = 128
EX = 7
DH = 128
NH = 16
TLOC = L // NC
DLOC = DI // NC
NEG = -3.0e29
SCALE = 1.0 / np.sqrt(DH)
SIM_SAFE_ACT = False

_prog_cache = {}


def _cached(key, fn):
    if key not in _prog_cache:
        _prog_cache[key] = fn()
    return _prog_cache[key]


def _run(nc, in_maps):
    return run_bass_kernel_spmd(nc, in_maps, core_ids=list(range(NC))).results


def _f32(x):
    return np.ascontiguousarray(np.asarray(x), dtype=np.float32)


def _ctx(nc, ctx):
    sb = lambda n, s, d=F32: ctx.enter_context(nc.sbuf_tensor(n, s, d))
    ps = lambda n, s, d=F32: ctx.enter_context(nc.psum_tensor(n, s, d))
    sem = lambda n: ctx.enter_context(nc.semaphore(n))
    return sb, ps, sem


def _mk_ln(TT):
    nc = bass.Bass("TRN2", target_bir_lowering=False)
    x_in = nc.dram_tensor("x", [TT * 128, H], F32, kind="ExternalInput").ap()
    idn = nc.dram_tensor("idn", [128, 128], BF16, kind="ExternalInput").ap()
    xnt_out = nc.dram_tensor("xnt", [H, TT * 128], BF16, kind="ExternalOutput").ap()
    nJ = H // 128
    with ExitStack() as ctx:
        sb, ps, sem = _ctx(nc, ctx)
        s_x = [sb(f"s_x{p}", [128, H]) for p in range(2)]
        s_sq = sb("s_sq", [128, H])
        s_m = sb("s_m", [128, 1])
        s_r = sb("s_r", [128, 1])
        s_t1 = sb("s_t1", [128, 1])
        s_t2 = sb("s_t2", [128, 1])
        s_xn = [sb(f"s_xn{p}", [128, H], BF16) for p in range(2)]
        s_id = sb("s_id", [128, 128], BF16)
        s_out = [sb(f"s_out{p}", [128, 128], BF16) for p in range(4)]
        p_tp = [ps(f"p_tp{p}", [128, 128], BF16) for p in range(2)]
        lds, lx0, lx1, ac, dv, pe, ev = (sem(n) for n in
            ["lds", "lx0", "lx1", "ac", "dv", "pe", "ev"])
        stb = [sem(f"st{b}") for b in range(4)]
        lx = [lx0, lx1]
        block = ctx.enter_context(nc.Block())

        @block.sync
        def _(sync):
            sync.dma_start(out=s_id[:], in_=idn).then_inc(lds, 16)
            for it in range(min(2, TT)):
                sync.dma_start(out=s_x[it % 2][:],
                               in_=x_in[it * 128:(it + 1) * 128, :]).then_inc(lx[it % 2], 16)
            for it in range(TT):
                for j in range(nJ):
                    k = it * nJ + j
                    sync.wait_ge(ev, k + 1)
                    sync.dma_start(out=xnt_out[j * 128:(j + 1) * 128, it * 128:(it + 1) * 128],
                                   in_=s_out[k % 4][:]).then_inc(stb[k % 4], 16)
                if it + 2 < TT:
                    sync.wait_ge(dv, 3 * it + 3)
                    sync.dma_start(out=s_x[it % 2][:],
                                   in_=x_in[(it + 2) * 128:(it + 3) * 128, :]).then_inc(lx[it % 2], 16)
            for b in range(4):
                sync.wait_ge(stb[b], 16 * ((nJ * TT - b + 3) // 4))

        @block.scalar
        def _(scalar):
            for it in range(TT):
                scalar.wait_ge(lx[it % 2], 16 * (it // 2 + 1))
                if it >= 1:
                    scalar.wait_ge(dv, 3 * (it - 1) + 2)
                scalar.activation(s_sq[:], s_x[it % 2][:], AF.Square).then_inc(ac, 1)
                scalar.wait_ge(dv, it * 3 + 2)
                scalar.activation(s_t2[:, 0:1], s_t1[:, 0:1], AF.Ln)
                scalar.drain()
                scalar.activation(s_r[:, 0:1], s_t2[:, 0:1], AF.Exp, scale=-0.5).then_inc(ac, 1)

        @block.vector
        def _(vector):
            for it in range(TT):
                x = s_x[it % 2]
                vector.wait_ge(lx[it % 2], 16 * (it // 2 + 1))
                vector.reduce_sum(s_m[:, 0:1], x[:], axis=AX)
                vector.drain()
                vector.tensor_scalar(s_m[:, 0:1], s_m[:, 0:1], 1.0 / H, None,
                                     AluOpType.mult).then_inc(dv, 1)
                vector.wait_ge(ac, it * 2 + 1)
                vector.reduce_sum(s_t1[:, 0:1], s_sq[:], axis=AX)
                vector.drain()
                vector.tensor_tensor(s_t2[:, 0:1], s_m[:, 0:1], s_m[:, 0:1], AluOpType.mult)
                vector.drain()
                vector.tensor_scalar(s_t1[:, 0:1], s_t1[:, 0:1], 1.0 / H, None, AluOpType.mult)
                vector.drain()
                vector.tensor_scalar(s_t1[:, 0:1], s_t1[:, 0:1], s_t2[:, 0:1], 1e-5,
                                     AluOpType.subtract, AluOpType.add).then_inc(dv, 1)
                vector.wait_ge(ac, it * 2 + 2)
                if it >= 2:
                    vector.wait_ge(pe, nJ * (it - 1))
                vector.tensor_scalar(s_xn[it % 2][:], x[:], s_m[:, 0:1], s_r[:, 0:1],
                                     AluOpType.subtract, AluOpType.mult).then_inc(dv, 1)
                for j in range(nJ):
                    k = it * nJ + j
                    vector.wait_ge(pe, k + 1)
                    if k >= 4:
                        vector.wait_ge(stb[k % 4], 16 * (k // 4))
                    vector.tensor_copy(s_out[k % 4][:], p_tp[k % 2][:]).then_inc(ev, 1)

        @block.tensor
        def _(tensor):
            tensor.wait_ge(lds, 16)
            for it in range(TT):
                tensor.wait_ge(dv, it * 3 + 3)
                for j in range(nJ):
                    k = it * nJ + j
                    if k >= 2:
                        tensor.wait_ge(ev, k - 1)
                    tensor.transpose(p_tp[k % 2][:], s_xn[it % 2][:, j * 128:(j + 1) * 128],
                                     s_id[:]).then_inc(pe, 1)
    return nc


def _launch_ln(x0):
    nc = _cached(("ln", 9), lambda: _mk_ln(9))
    idn = np.eye(128, dtype=BF)
    ims = []
    for c in range(NC):
        lo = c * TLOC - 128
        if lo < 0:
            xh = np.concatenate([np.zeros((128, H), np.float32), x0[:(c + 1) * TLOC]], 0)
        else:
            xh = x0[lo:(c + 1) * TLOC]
        ims.append(dict(x=_f32(xh), idn=idn))
    return [r["xnt"] for r in _run(nc, ims)]


def _mk_matmul(K, T, outs, out_dtype):
    nKT = K // 128
    nch = (T + 511) // 512
    Otot = sum(o["O"] for o in outs)
    nOT = Otot // 128
    nc = bass.Bass("TRN2", target_bir_lowering=False)
    rhs_in = nc.dram_tensor("rhsT", [K, T], BF16, kind="ExternalInput").ap()
    w_in = nc.dram_tensor("wT", [K, Otot], F32, kind="ExternalInput").ap()
    b_in = nc.dram_tensor("bias", [128, Otot // 128], F32, kind="ExternalInput").ap()
    d_out = {o["name"]: nc.dram_tensor(o["name"], [o["O"], T], out_dtype, kind="ExternalOutput").ap()
             for o in outs}

    def tinfo(ot):
        base = 0
        for o in outs:
            if base <= ot * 128 < base + o["O"]:
                return o, base
            base += o["O"]
        raise AssertionError

    with ExitStack() as ctx:
        sb, ps, sem = _ctx(nc, ctx)
        s_rhs = sb("s_rhs", [128, nKT * T], BF16)
        s_w = [sb(f"s_w{p}", [128, nKT * 128], BF16) for p in range(2)]
        s_b = sb("s_b", [128, nOT])
        s_st = [sb(f"s_st{p}", [128, 512], out_dtype) for p in range(4)]
        p_mm = [ps(f"p_mm{p}", [128, 512]) for p in range(2)]
        ldr, lw0, lw1, mm, ev = (sem(n) for n in ["ldr", "lw0", "lw1", "mm", "ev"])
        stb = [sem(f"st{b}") for b in range(4)]
        lw = [lw0, lw1]
        block = ctx.enter_context(nc.Block())

        @block.sync
        def _(sync):
            sync.dma_start(out=s_rhs[:],
                           in_=rhs_in.rearrange("(kt p) t -> p kt t", p=128)).then_inc(ldr, 16)
            sync.dma_start(out=s_b[:], in_=b_in).then_inc(ldr, 16)
            for ot in range(nOT):
                o, base = tinfo(ot)
                for ch in range(nch):
                    k = ot * nch + ch
                    c0 = ch * 512
                    cw = min(512, T - c0)
                    sync.wait_ge(ev, k + 1)
                    sync.dma_start(out=d_out[o["name"]][ot * 128 - base:ot * 128 - base + 128, c0:c0 + cw],
                                   in_=s_st[k % 4][:, 0:cw]).then_inc(stb[k % 4], 16)
            for b in range(4):
                sync.wait_ge(stb[b], 16 * ((nOT * nch - b + 3) // 4))

        @block.gpsimd
        def _(gpsimd):
            for ot in range(nOT):
                if ot >= 2:
                    gpsimd.wait_ge(mm, (ot - 1) * nch)
                gpsimd.dma_start(out=s_w[ot % 2][:],
                                 in_=w_in[:, ot * 128:(ot + 1) * 128].rearrange(
                                     "(kt p) o -> p kt o", p=128)).then_inc(lw[ot % 2], 16)

        @block.tensor
        def _(tensor):
            tensor.wait_ge(ldr, 32)
            for ot in range(nOT):
                tensor.wait_ge(lw[ot % 2], 16 * (ot // 2 + 1))
                for ch in range(nch):
                    k = ot * nch + ch
                    c0 = ch * 512
                    cw = min(512, T - c0)
                    if k >= 2:
                        tensor.wait_ge(ev, k - 1)
                    for kt in range(nKT):
                        ins = tensor.matmul(p_mm[k % 2][:, 0:cw],
                                            s_w[ot % 2][:, kt * 128:(kt + 1) * 128],
                                            s_rhs[:, kt * T + c0:kt * T + c0 + cw],
                                            start=(kt == 0), stop=(kt == nKT - 1))
                        if kt == nKT - 1:
                            ins.then_inc(mm, 1)

        @block.scalar
        def _(scalar):
            scalar.wait_ge(ldr, 32)
            for ot in range(nOT):
                o, base = tinfo(ot)
                for ch in range(nch):
                    k = ot * nch + ch
                    cw = min(512, T - ch * 512)
                    scalar.wait_ge(mm, k + 1)
                    if k >= 4:
                        scalar.wait_ge(stb[k % 4], 16 * (k // 4))
                    kw = {}
                    fn = o["act"]
                    if o["bias"]:
                        kw["bias"] = s_b[:, ot:ot + 1]
                        if fn == AF.Copy:
                            fn = AF.Relu if SIM_SAFE_ACT else AF.Prelu
                            if not SIM_SAFE_ACT:
                                kw["alpha"] = 1.0
                    if o.get("scale") is not None:
                        kw["scale"] = o["scale"]
                    scalar.activation(s_st[k % 4][:, 0:cw], p_mm[k % 2][:, 0:cw],
                                      fn, **kw).then_inc(ev, 1)
    return nc


def _launch_matmul(key, rhsT_list, wT, bias, outs, out_dtype):
    K, T = wT.shape[0], rhsT_list[0].shape[1]
    nc = _cached((key, K, T), lambda: _mk_matmul(K, T, outs, out_dtype))
    wTc = _f32(wT)
    bc = np.ascontiguousarray(_f32(bias).reshape(-1, 128).T)
    ims = [dict(rhsT=np.ascontiguousarray(r), wT=wTc, bias=bc) for r in rhsT_list]
    return _run(nc, ims)


def _mk_conv():
    TH = TLOC + 3
    nG = DI // 128
    nc = bass.Bass("TRN2", target_bir_lowering=False)
    hs_in = nc.dram_tensor("hsT", [DI, 1152], F32, kind="ExternalInput").ap()
    cw_in = nc.dram_tensor("cw", [DI, 4], F32, kind="ExternalInput").ap()
    cb_in = nc.dram_tensor("cb", [128, DI // 128], F32, kind="ExternalInput").ap()
    xpw_in = nc.dram_tensor("xpwT", [DI, 160], F32, kind="ExternalInput").ap()
    xc_out = nc.dram_tensor("xcT", [DI, TLOC], F32, kind="ExternalOutput").ap()
    ssm_out = nc.dram_tensor("ssm", [160, TLOC], F32, kind="ExternalOutput").ap()
    with ExitStack() as ctx:
        sb, ps, sem = _ctx(nc, ctx)
        s_hs = [sb(f"s_hs{p}", [128, TH]) for p in range(2)]
        s_acc = [sb(f"s_acc{p}", [128, TLOC]) for p in range(2)]
        s_xc = [sb(f"s_xc{p}", [128, TLOC]) for p in range(2)]
        s_xc16 = sb("s_xc16", [128, nG * TLOC], BF16)
        s_cw = sb("s_cw", [128, nG * 4])
        s_cb = sb("s_cb", [128, nG])
        s_w16 = sb("s_w16", [128, nG * 160], BF16)
        s_ssm = sb("s_ssm", [128, TLOC])
        p_mm = [ps(f"p_mm{p}", [128, 512]) for p in range(2)]
        lds, lh0, lh1, ldw, dv, ac, mm, ev = (sem(n) for n in
            ["lds", "lh0", "lh1", "ldw", "dv", "ac", "mm", "ev"])
        stx = [sem("stx0"), sem("stx1")]
        sts = [sem("sts0"), sem("sts1")]
        lh = [lh0, lh1]
        block = ctx.enter_context(nc.Block())

        @block.sync
        def _(sync):
            sync.dma_start(out=s_cw[:], in_=cw_in.rearrange("(g p) k -> p g k", p=128)).then_inc(lds, 16)
            sync.dma_start(out=s_cb[:], in_=cb_in).then_inc(lds, 16)
            sync.dma_start(out=s_hs[0][:], in_=hs_in[0:128, 125:125 + TH]).then_inc(lh0, 16)
            sync.dma_start(out=s_hs[1][:], in_=hs_in[128:256, 125:125 + TH]).then_inc(lh1, 16)
            for g in range(nG):
                sync.wait_ge(ac, g + 1)
                sync.dma_start(out=xc_out[g * 128:(g + 1) * 128, :],
                               in_=s_xc[g % 2][:]).then_inc(stx[g % 2], 16)
                if g + 2 < nG:
                    sync.wait_ge(dv, 2 * g + 1)
                    sync.dma_start(out=s_hs[g % 2][:],
                                   in_=hs_in[(g + 2) * 128:(g + 3) * 128, 125:125 + TH]).then_inc(lh[g % 2], 16)
            for k in range(4):
                sync.wait_ge(ev, k + 1)
                mt, chk = divmod(k, 2)
                mw = 128 if mt == 0 else 32
                sync.dma_start(out=ssm_out[mt * 128:mt * 128 + mw, chk * 512:(chk + 1) * 512],
                               in_=s_ssm[0:mw, chk * 512:(chk + 1) * 512]).then_inc(sts[k % 2], 16)
            sync.wait_ge(stx[0], 16 * (nG // 2))
            sync.wait_ge(stx[1], 16 * (nG // 2))
            sync.wait_ge(sts[0], 32)
            sync.wait_ge(sts[1], 32)

        @block.gpsimd
        def _(gpsimd):
            gpsimd.dma_start(out=s_w16[:],
                             in_=xpw_in.rearrange("(g p) o -> p g o", p=128)).then_inc(ldw, 16)

        @block.vector
        def _(vector):
            vector.wait_ge(lds, 32)
            for g in range(nG):
                vector.wait_ge(lh[g % 2], 16 * (g // 2 + 1))
                hs, acc = s_hs[g % 2], s_acc[g % 2]
                vector.wait_ge(ac, g)
                vector.tensor_scalar(acc[:], hs[:, 0:TLOC], s_cw[:, g * 4:g * 4 + 1], None,
                                     AluOpType.mult)
                vector.drain()
                for j in range(1, 4):
                    vector.scalar_tensor_tensor(acc[:], hs[:, j:j + TLOC],
                                                s_cw[:, g * 4 + j:g * 4 + j + 1],
                                                acc[:], AluOpType.mult, AluOpType.add)
                    vector.drain()
                vector.sem_inc(dv, 1)
                vector.wait_ge(ac, g + 1)
                vector.tensor_copy(s_xc16[:, g * TLOC:(g + 1) * TLOC], s_xc[g % 2][:]).then_inc(dv, 1)
            for k in range(4):
                vector.wait_ge(mm, k + 1)
                mt, chk = divmod(k, 2)
                mw = 128 if mt == 0 else 32
                if k >= 2:
                    vector.wait_ge(sts[k % 2], 16)
                vector.tensor_copy(s_ssm[0:mw, chk * 512:(chk + 1) * 512],
                                   p_mm[k % 2][0:mw, :]).then_inc(ev, 1)

        @block.scalar
        def _(scalar):
            scalar.wait_ge(lds, 32)
            for g in range(nG):
                scalar.wait_ge(dv, 2 * g + 1)
                if g >= 2:
                    scalar.wait_ge(stx[g % 2], 16 * (g // 2))
                    scalar.wait_ge(dv, 2 * (g - 2) + 2)
                scalar.activation(s_xc[g % 2][:], s_acc[g % 2][:], AF.Silu,
                                  bias=s_cb[:, g:g + 1]).then_inc(ac, 1)

        @block.tensor
        def _(tensor):
            tensor.wait_ge(ldw, 16)
            tensor.wait_ge(dv, 2 * nG)
            for k in range(4):
                mt, chk = divmod(k, 2)
                mw = 128 if mt == 0 else 32
                if k >= 2:
                    tensor.wait_ge(ev, k - 1)
                for g in range(nG):
                    ins = tensor.matmul(p_mm[k % 2][0:mw, :],
                                        s_w16[:, g * 160 + mt * 128:g * 160 + mt * 128 + mw],
                                        s_xc16[:, g * TLOC + chk * 512:g * TLOC + (chk + 1) * 512],
                                        start=(g == 0), stop=(g == nG - 1))
                    if g == nG - 1:
                        ins.then_inc(mm, 1)
    return nc


def _launch_conv(hsT_list, conv_w, conv_b, x_proj_w):
    nc = _cached("conv", _mk_conv)
    cw = _f32(conv_w)
    cb = np.ascontiguousarray(_f32(conv_b).reshape(DI // 128, 128).T)
    xpwT = _f32(np.asarray(x_proj_w).T)
    ims = [dict(hsT=np.ascontiguousarray(h), cw=cw, cb=cb, xpwT=xpwT) for h in hsT_list]
    res = _run(nc, ims)
    return [r["xcT"] for r in res], [r["ssm"] for r in res]


def _mk_dtproj():
    nMT = DLOC // 128
    nch = L // 512
    nc = bass.Bass("TRN2", target_bir_lowering=False)
    ts_in = nc.dram_tensor("tsT", [TR, L], F32, kind="ExternalInput").ap()
    w_in = nc.dram_tensor("dtwT", [TR, DLOC], F32, kind="ExternalInput").ap()
    b_in = nc.dram_tensor("dtb", [128, DLOC // 128], F32, kind="ExternalInput").ap()
    dt_out = nc.dram_tensor("dtT", [DLOC, L], F32, kind="ExternalOutput").ap()
    with ExitStack() as ctx:
        sb, ps, sem = _ctx(nc, ctx)
        s_tsf = sb("s_tsf", [128, L])
        s_ts = sb("s_ts", [128, L], BF16)
        s_wf = sb("s_wf", [128, DLOC])
        s_w = sb("s_w", [128, DLOC], BF16)
        s_b = sb("s_b", [128, nMT])
        s_x = [sb(f"s_x{p}", [128, 512]) for p in range(2)]
        s_a = [sb(f"s_a{p}", [128, 512]) for p in range(2)]
        s_o = [sb(f"s_o{p}", [128, 512]) for p in range(4)]
        p_mm = [ps(f"p_mm{p}", [128, 512]) for p in range(2)]
        ld, mm, dv, ac, ev = (sem(n) for n in ["ld", "mm", "dv", "ac", "ev"])
        stb = [sem(f"st{b}") for b in range(4)]
        block = ctx.enter_context(nc.Block())

        @block.sync
        def _(sync):
            sync.dma_start(out=s_tsf[:], in_=ts_in).then_inc(ld, 16)
            sync.dma_start(out=s_wf[:], in_=w_in).then_inc(ld, 16)
            sync.dma_start(out=s_b[:], in_=b_in).then_inc(ld, 16)
            for k in range(nMT * nch):
                mt, ch = divmod(k, nch)
                sync.wait_ge(dv, 1 + 2 * k + 2)
                sync.dma_start(out=dt_out[mt * 128:(mt + 1) * 128, ch * 512:(ch + 1) * 512],
                               in_=s_o[k % 4][:]).then_inc(stb[k % 4], 16)
            for b in range(4):
                sync.wait_ge(stb[b], 16 * ((nMT * nch - b + 3) // 4))

        @block.vector
        def _(vector):
            vector.wait_ge(ld, 48)
            vector.tensor_copy(s_ts[:], s_tsf[:])
            vector.drain()
            vector.tensor_copy(s_w[:], s_wf[:]).then_inc(dv, 1)
            for k in range(nMT * nch):
                mt, ch = divmod(k, nch)
                vector.wait_ge(mm, k + 1)
                vector.tensor_scalar(s_x[k % 2][:], p_mm[k % 2][:], s_b[:, mt:mt + 1], None,
                                     AluOpType.add)
                vector.drain()
                vector.sem_inc(dv, 1)
                vector.wait_ge(ac, k + 1)
                vector.tensor_tensor(s_o[k % 4][:], s_o[k % 4][:], s_a[k % 2][:], AluOpType.add)
                vector.drain()
                vector.sem_inc(dv, 1)

        @block.scalar
        def _(scalar):
            for k in range(nMT * nch):
                scalar.wait_ge(dv, 1 + 2 * k + 1)
                scalar.activation(s_a[k % 2][:], s_x[k % 2][:], AF.Abs)
                scalar.drain()
                scalar.activation(s_a[k % 2][:], s_a[k % 2][:], AF.Exp, scale=-1.0)
                scalar.drain()
                scalar.activation(s_a[k % 2][:], s_a[k % 2][:], AF.Ln, bias=1.0)
                scalar.drain()
                if k >= 4:
                    scalar.wait_ge(stb[k % 4], 16 * (k // 4))
                scalar.activation(s_o[k % 4][:], s_x[k % 2][:], AF.Relu).then_inc(ac, 1)

        @block.tensor
        def _(tensor):
            tensor.wait_ge(dv, 1)
            for k in range(nMT * nch):
                mt, ch = divmod(k, nch)
                if k >= 2:
                    tensor.wait_ge(dv, 1 + 2 * (k - 2) + 1)
                tensor.matmul(p_mm[k % 2][:], s_w[:, mt * 128:(mt + 1) * 128],
                              s_ts[:, ch * 512:(ch + 1) * 512],
                              start=True, stop=True).then_inc(mm, 1)
    return nc


def _launch_dtproj(tsT, dt_proj_w, dt_proj_b):
    nc = _cached("dtproj", _mk_dtproj)
    dtwT = _f32(np.asarray(dt_proj_w).T)
    ims = []
    for c in range(NC):
        sl = slice(c * DLOC, (c + 1) * DLOC)
        ims.append(dict(tsT=_f32(tsT), dtwT=_f32(dtwT[:, sl]),
                        dtb=np.ascontiguousarray(_f32(np.asarray(dt_proj_b)[sl]).reshape(DLOC // 128, 128).T)))
    return [r["dtT"] for r in _run(nc, ims)]


def _mk_scan():
    TCH = 1024
    nTC = L // TCH
    nG = DLOC // 128
    W4 = nG * TCH
    nc = bass.Bass("TRN2", target_bir_lowering=False)
    dt_in = nc.dram_tensor("dtT", [DLOC, L], F32, kind="ExternalInput").ap()
    xc_in = nc.dram_tensor("xcT", [DLOC, L], F32, kind="ExternalInput").ap()
    b_in = nc.dram_tensor("Bm", [NSTATE, L], F32, kind="ExternalInput").ap()
    c_in = nc.dram_tensor("Cm", [NSTATE, L], F32, kind="ExternalInput").ap()
    al_in = nc.dram_tensor("Alog", [DLOC, NSTATE], F32, kind="ExternalInput").ap()
    d_in = nc.dram_tensor("Dv", [128, DLOC // 128], F32, kind="ExternalInput").ap()
    y_out = nc.dram_tensor("yT", [DLOC, L], F32, kind="ExternalOutput").ap()

    def dv_u(tc):
        return 1 + tc * 17 + 1

    def dv_step(tc, n):
        return dv_u(tc) + n + 1

    def ac_dA(q):
        return 1 + q + 1

    with ExitStack() as ctx:
        sb, ps, sem = _ctx(nc, ctx)
        s_dt = sb("s_dt", [128, W4])
        s_xc = sb("s_xc", [128, W4])
        s_u = sb("s_u", [128, W4])
        s_B = [sb(f"s_B{p}", [128, W4]) for p in range(2)]
        s_C = [sb(f"s_C{p}", [128, W4]) for p in range(2)]
        s_al = sb("s_al", [128, nG * NSTATE])
        s_D = sb("s_D", [128, nG])
        s_dA = [sb(f"s_dA{p}", [128, W4]) for p in range(2)]
        s_W = sb("s_W", [128, W4])
        s_h = sb("s_h", [128, W4])
        s_state = sb("s_state", [128, nG * NSTATE])
        s_y = sb("s_y", [128, W4])
        lds, lt0, lt1, lb0, lb1, dv, ac, sty = (sem(n) for n in
            ["lds", "lt0", "lt1", "lb0", "lb1", "dv", "ac", "sty"])
        lb = [lb0, lb1]
        block = ctx.enter_context(nc.Block())

        @block.sync
        def _(sync):
            sync.dma_start(out=s_al[:], in_=al_in.rearrange("(g p) n -> p g n", p=128)).then_inc(lds, 16)
            sync.dma_start(out=s_D[:], in_=d_in).then_inc(lds, 16)
            sync.dma_start(out=s_dt[:], in_=dt_in[:, 0:TCH].rearrange("(g p) t -> p g t", p=128)).then_inc(lt0, 16)
            sync.dma_start(out=s_xc[:], in_=xc_in[:, 0:TCH].rearrange("(g p) t -> p g t", p=128)).then_inc(lt1, 16)
            for tc in range(nTC):
                sync.wait_ge(dv, dv_step(tc, 15))
                sync.dma_start(out=y_out[:, tc * TCH:(tc + 1) * TCH].rearrange("(g p) t -> p g t", p=128),
                               in_=s_y[:]).then_inc(sty, 16)
                if tc + 1 < nTC:
                    t2 = tc + 1
                    sync.wait_ge(ac, ac_dA(tc * 16 + 15))
                    sync.dma_start(out=s_dt[:],
                                   in_=dt_in[:, t2 * TCH:(t2 + 1) * TCH].rearrange("(g p) t -> p g t", p=128)).then_inc(lt0, 16)
                    sync.dma_start(out=s_xc[:],
                                   in_=xc_in[:, t2 * TCH:(t2 + 1) * TCH].rearrange("(g p) t -> p g t", p=128)).then_inc(lt1, 16)
            sync.wait_ge(sty, 16 * nTC)

        @block.gpsimd
        def _(gpsimd):
            for tc in range(nTC):
                for n in range(NSTATE):
                    q = tc * NSTATE + n
                    if q >= 2:
                        tcp, npp = divmod(q - 2, NSTATE)
                        gpsimd.wait_ge(dv, dv_step(tcp, npp))
                    for g in range(nG):
                        gpsimd.dma_start(out=s_B[q % 2][:, g * TCH:(g + 1) * TCH],
                                         in_=b_in[n:n + 1, tc * TCH:(tc + 1) * TCH].broadcast_to([128, TCH])).then_inc(lb[q % 2], 16)
                        gpsimd.dma_start(out=s_C[q % 2][:, g * TCH:(g + 1) * TCH],
                                         in_=c_in[n:n + 1, tc * TCH:(tc + 1) * TCH].broadcast_to([128, TCH])).then_inc(lb[q % 2], 16)

        @block.scalar
        def _(scalar):
            scalar.wait_ge(lds, 32)
            scalar.activation(s_al[:], s_al[:], AF.Exp).then_inc(ac, 1)
            for tc in range(nTC):
                scalar.wait_ge(lt0, 16 * (tc + 1))
                scalar.wait_ge(dv, 1)
                for n in range(NSTATE):
                    q = tc * NSTATE + n
                    if q >= 2:
                        tcp, npp = divmod(q - 2, NSTATE)
                        scalar.wait_ge(dv, dv_step(tcp, npp))
                    for g in range(nG):
                        ins = scalar.activation(s_dA[q % 2][:, g * TCH:(g + 1) * TCH],
                                                s_dt[:, g * TCH:(g + 1) * TCH],
                                                AF.Exp,
                                                scale=s_al[:, g * NSTATE + n:g * NSTATE + n + 1])
                        if g == nG - 1:
                            ins.then_inc(ac, 1)

        @block.vector
        def _(vector):
            vector.wait_ge(ac, 1)
            vector.tensor_scalar(s_al[:], s_al[:], -1.0, None, AluOpType.mult)
            vector.drain()
            vector.sem_inc(dv, 1)
            for tc in range(nTC):
                vector.wait_ge(lt0, 16 * (tc + 1))
                vector.wait_ge(lt1, 16 * (tc + 1))
                vector.tensor_tensor(s_u[:], s_dt[:], s_xc[:], AluOpType.mult)
                vector.drain()
                vector.sem_inc(dv, 1)
                for n in range(NSTATE):
                    q = tc * NSTATE + n
                    vector.wait_ge(lb[q % 2], 128 * (q // 2 + 1))
                    vector.tensor_tensor(s_W[:], s_u[:], s_B[q % 2][:], AluOpType.mult)
                    vector.drain()
                    vector.wait_ge(ac, ac_dA(q))
                    for g in range(nG):
                        ini = s_state[:, g * NSTATE + n:g * NSTATE + n + 1] if tc > 0 else 0.0
                        vector.tensor_tensor_scan(s_h[:, g * TCH:(g + 1) * TCH],
                                                  s_dA[q % 2][:, g * TCH:(g + 1) * TCH],
                                                  s_W[:, g * TCH:(g + 1) * TCH],
                                                  ini, AluOpType.mult, AluOpType.add)
                    vector.drain()
                    for g in range(nG):
                        vector.tensor_copy(s_state[:, g * NSTATE + n:g * NSTATE + n + 1],
                                           s_h[:, (g + 1) * TCH - 1:(g + 1) * TCH])
                    if n == 0:
                        if tc >= 1:
                            vector.wait_ge(sty, 16 * tc)
                        for g in range(nG):
                            vector.tensor_scalar(s_y[:, g * TCH:(g + 1) * TCH],
                                                 s_xc[:, g * TCH:(g + 1) * TCH],
                                                 s_D[:, g:g + 1], None, AluOpType.mult)
                        vector.drain()
                    vector.tensor_tensor(s_W[:], s_h[:], s_C[q % 2][:], AluOpType.mult)
                    vector.drain()
                    vector.tensor_tensor(s_y[:], s_y[:], s_W[:], AluOpType.add).then_inc(dv, 1)
                    vector.drain()
    return nc


def _launch_scan(dtT_list, xcT_byd, Bm, Cm, A_log, Dvec):
    nc = _cached("scan", _mk_scan)
    ims = []
    for c in range(NC):
        sl = slice(c * DLOC, (c + 1) * DLOC)
        ims.append(dict(dtT=np.ascontiguousarray(dtT_list[c]),
                        xcT=np.ascontiguousarray(xcT_byd[c]),
                        Bm=_f32(Bm), Cm=_f32(Cm),
                        Alog=_f32(np.asarray(A_log)[sl]),
                        Dv=np.ascontiguousarray(_f32(np.asarray(Dvec)[sl]).reshape(DLOC // 128, 128).T)))
    return [r["yT"] for r in _run(nc, ims)]


def _mk_mul(O, T):
    nOT = O // 128
    nc = bass.Bass("TRN2", target_bir_lowering=False)
    a_in = nc.dram_tensor("a", [O, T], F32, kind="ExternalInput").ap()
    b_in = nc.dram_tensor("b", [O, T], F32, kind="ExternalInput").ap()
    o_out = nc.dram_tensor("o", [O, T], BF16, kind="ExternalOutput").ap()
    with ExitStack() as ctx:
        sb, ps, sem = _ctx(nc, ctx)
        s_a = [sb(f"s_a{p}", [128, T]) for p in range(2)]
        s_b = [sb(f"s_b{p}", [128, T]) for p in range(2)]
        s_o = [sb(f"s_o{p}", [128, T], BF16) for p in range(2)]
        la0, la1, dv, st = (sem(n) for n in ["la0", "la1", "dv", "st"])
        la = [la0, la1]
        block = ctx.enter_context(nc.Block())

        @block.sync
        def _(sync):
            for ot in range(min(2, nOT)):
                sync.dma_start(out=s_a[ot % 2][:], in_=a_in[ot * 128:(ot + 1) * 128, :]).then_inc(la[ot % 2], 16)
                sync.dma_start(out=s_b[ot % 2][:], in_=b_in[ot * 128:(ot + 1) * 128, :]).then_inc(la[ot % 2], 16)
            for ot in range(nOT):
                sync.wait_ge(dv, ot + 1)
                sync.dma_start(out=o_out[ot * 128:(ot + 1) * 128, :], in_=s_o[ot % 2][:]).then_inc(st, 16)
                if ot + 2 < nOT:
                    sync.dma_start(out=s_a[ot % 2][:], in_=a_in[(ot + 2) * 128:(ot + 3) * 128, :]).then_inc(la[ot % 2], 16)
                    sync.dma_start(out=s_b[ot % 2][:], in_=b_in[(ot + 2) * 128:(ot + 3) * 128, :]).then_inc(la[ot % 2], 16)
            sync.wait_ge(st, 16 * nOT)

        @block.vector
        def _(vector):
            for ot in range(nOT):
                vector.wait_ge(la[ot % 2], 32 * (ot // 2 + 1))
                if ot >= 2:
                    vector.wait_ge(st, 16 * (ot - 1))
                vector.tensor_tensor(s_o[ot % 2][:], s_a[ot % 2][:], s_b[ot % 2][:],
                                     AluOpType.mult)
                vector.drain()
                vector.sem_inc(dv, 1)
    return nc


def _launch_mul(key, a_list, b_list):
    O, T = a_list[0].shape
    nc = _cached((key, "mul", O, T), lambda: _mk_mul(O, T))
    ims = [dict(a=np.ascontiguousarray(a), b=np.ascontiguousarray(b))
           for a, b in zip(a_list, b_list)]
    return [r["o"] for r in _run(nc, ims)]


def _mk_epilogue(with_silu, with_ln):
    TT = TLOC // 128
    nJ = H // 128
    nc = bass.Bass("TRN2", target_bir_lowering=False)
    x_in = nc.dram_tensor("x", [TLOC, H], F32, kind="ExternalInput").ap()
    a_in = nc.dram_tensor("a", [TLOC, H], F32, kind="ExternalInput").ap()
    b_in = nc.dram_tensor("b", [TLOC, H], F32, kind="ExternalInput").ap()
    xo_out = nc.dram_tensor("xo", [TLOC, H], F32, kind="ExternalOutput").ap()
    if with_ln:
        idn = nc.dram_tensor("idn", [128, 128], BF16, kind="ExternalInput").ap()
        xnt_out = nc.dram_tensor("xnt", [H, TLOC], BF16, kind="ExternalOutput").ap()
    with ExitStack() as ctx:
        sb, ps, sem = _ctx(nc, ctx)
        s_x = [sb(f"s_x{p}", [128, H]) for p in range(2)]
        s_a = [sb(f"s_a{p}", [128, H]) for p in range(2)]
        s_b = [sb(f"s_b{p}", [128, H]) for p in range(2)]
        s_sq = sb("s_sq", [128, H])
        s_m = sb("s_m", [128, 1])
        s_r = sb("s_r", [128, 1])
        s_t1 = sb("s_t1", [128, 1])
        s_t2 = sb("s_t2", [128, 1])
        if with_ln:
            s_xn = [sb(f"s_xn{p}", [128, H], BF16) for p in range(2)]
            s_id = sb("s_id", [128, 128], BF16)
            s_out = [sb(f"s_out{p}", [128, 128], BF16) for p in range(4)]
            p_tp = [ps(f"p_tp{p}", [128, 128], BF16) for p in range(2)]
        lds, le0, le1, ac, dv, pe, ev = (sem(n) for n in
            ["lds", "le0", "le1", "ac", "dv", "pe", "ev"])
        stx = [sem("stx0"), sem("stx1")]
        stb = [sem(f"st{b}") for b in range(4)]
        le = [le0, le1]
        block = ctx.enter_context(nc.Block())
        NST = (1 + nJ) if with_ln else 1
        DVT = 4 if with_ln else 1
        ACT = (1 if with_silu else 0) + (2 if with_ln else 0)

        @block.sync
        def _(sync):
            if with_ln:
                sync.dma_start(out=s_id[:], in_=idn).then_inc(lds, 16)
            for it in range(min(2, TT)):
                sl = slice(it * 128, (it + 1) * 128)
                sync.dma_start(out=s_x[it % 2][:], in_=x_in[sl, :]).then_inc(le[it % 2], 16)
                sync.dma_start(out=s_a[it % 2][:], in_=a_in[sl, :]).then_inc(le[it % 2], 16)
                sync.dma_start(out=s_b[it % 2][:], in_=b_in[sl, :]).then_inc(le[it % 2], 16)
            for it in range(TT):
                sync.wait_ge(dv, DVT * it + 1)
                sync.dma_start(out=xo_out[it * 128:(it + 1) * 128, :],
                               in_=s_x[it % 2][:]).then_inc(stx[it % 2], 16)
                if with_ln:
                    for j in range(nJ):
                        k = it * nJ + j
                        sync.wait_ge(ev, k + 1)
                        sync.dma_start(out=xnt_out[j * 128:(j + 1) * 128, it * 128:(it + 1) * 128],
                                       in_=s_out[k % 4][:]).then_inc(stb[k % 4], 16)
                if it + 2 < TT:
                    sync.wait_ge(dv, DVT * it + DVT)
                    sync.wait_ge(stx[it % 2], 16 * (it // 2 + 1))
                    sl = slice((it + 2) * 128, (it + 3) * 128)
                    sync.dma_start(out=s_x[it % 2][:], in_=x_in[sl, :]).then_inc(le[it % 2], 16)
                    sync.dma_start(out=s_a[it % 2][:], in_=a_in[sl, :]).then_inc(le[it % 2], 16)
                    sync.dma_start(out=s_b[it % 2][:], in_=b_in[sl, :]).then_inc(le[it % 2], 16)
            sync.wait_ge(stx[0], 16 * ((TT + 1) // 2))
            sync.wait_ge(stx[1], 16 * (TT // 2))
            if with_ln:
                for b in range(4):
                    sync.wait_ge(stb[b], 16 * ((nJ * TT - b + 3) // 4))

        @block.scalar
        def _(scalar):
            for it in range(TT):
                scalar.wait_ge(le[it % 2], 48 * (it // 2 + 1))
                if with_silu:
                    scalar.activation(s_b[it % 2][:], s_b[it % 2][:], AF.Silu).then_inc(ac, 1)
                if with_ln:
                    scalar.wait_ge(dv, DVT * it + 1)
                    if it >= 1:
                        scalar.wait_ge(dv, DVT * (it - 1) + 3)
                    scalar.activation(s_sq[:], s_x[it % 2][:], AF.Square).then_inc(ac, 1)
                    scalar.wait_ge(dv, DVT * it + 3)
                    scalar.activation(s_t2[:, 0:1], s_t1[:, 0:1], AF.Ln)
                    scalar.drain()
                    scalar.activation(s_r[:, 0:1], s_t2[:, 0:1], AF.Exp, scale=-0.5).then_inc(ac, 1)

        @block.vector
        def _(vector):
            for it in range(TT):
                vector.wait_ge(le[it % 2], 48 * (it // 2 + 1))
                if with_silu:
                    vector.wait_ge(ac, ACT * it + 1)
                vector.tensor_tensor(s_a[it % 2][:], s_a[it % 2][:], s_b[it % 2][:], AluOpType.mult)
                vector.drain()
                vector.tensor_tensor(s_x[it % 2][:], s_x[it % 2][:], s_a[it % 2][:], AluOpType.add)
                vector.drain()
                vector.sem_inc(dv, 1)
                if with_ln:
                    x = s_x[it % 2]
                    vector.reduce_sum(s_m[:, 0:1], x[:], axis=AX)
                    vector.drain()
                    vector.tensor_scalar(s_m[:, 0:1], s_m[:, 0:1], 1.0 / H, None,
                                         AluOpType.mult).then_inc(dv, 1)
                    vector.wait_ge(ac, ACT * it + (2 if with_silu else 1))
                    vector.reduce_sum(s_t1[:, 0:1], s_sq[:], axis=AX)
                    vector.drain()
                    vector.tensor_tensor(s_t2[:, 0:1], s_m[:, 0:1], s_m[:, 0:1], AluOpType.mult)
                    vector.drain()
                    vector.tensor_scalar(s_t1[:, 0:1], s_t1[:, 0:1], 1.0 / H, None, AluOpType.mult)
                    vector.drain()
                    vector.tensor_scalar(s_t1[:, 0:1], s_t1[:, 0:1], s_t2[:, 0:1], 1e-5,
                                         AluOpType.subtract, AluOpType.add).then_inc(dv, 1)
                    vector.wait_ge(ac, ACT * it + ACT)
                    if it >= 2:
                        vector.wait_ge(pe, nJ * (it - 1))
                    vector.tensor_scalar(s_xn[it % 2][:], x[:], s_m[:, 0:1], s_r[:, 0:1],
                                         AluOpType.subtract, AluOpType.mult).then_inc(dv, 1)
                    for j in range(nJ):
                        k = it * nJ + j
                        vector.wait_ge(pe, k + 1)
                        if k >= 4:
                            vector.wait_ge(stb[k % 4], 16 * (k // 4))
                        vector.tensor_copy(s_out[k % 4][:], p_tp[k % 2][:]).then_inc(ev, 1)

        if with_ln:
            @block.tensor
            def _(tensor):
                tensor.wait_ge(lds, 16)
                for it in range(TT):
                    tensor.wait_ge(dv, DVT * it + 4)
                    for j in range(nJ):
                        k = it * nJ + j
                        if k >= 2:
                            tensor.wait_ge(ev, k - 1)
                        tensor.transpose(p_tp[k % 2][:], s_xn[it % 2][:, j * 128:(j + 1) * 128],
                                         s_id[:]).then_inc(pe, 1)
    return nc


def _launch_epilogue(key, with_silu, with_ln, x_list, a_list, b_list):
    nc = _cached((key, with_silu, with_ln), lambda: _mk_epilogue(with_silu, with_ln))
    idn = np.eye(128, dtype=BF)
    ims = []
    for c in range(NC):
        im = dict(x=np.ascontiguousarray(x_list[c]), a=np.ascontiguousarray(a_list[c]),
                  b=np.ascontiguousarray(b_list[c]))
        if with_ln:
            im["idn"] = idn
        ims.append(im)
    return _run(nc, ims)


def _mk_attn():
    nJT = L // 128
    nIT = TLOC // 128
    nHT = H // 128
    NM = nJT - 1
    UH = nIT + NM

    def I0(j):
        return (j + 1) // 8

    nc = bass.Bass("TRN2", target_bir_lowering=False)
    q_in = nc.dram_tensor("qT", [H, TLOC], BF16, kind="ExternalInput").ap()
    k_in = nc.dram_tensor("kT", [DH, L], BF16, kind="ExternalInput").ap()
    v_in = nc.dram_tensor("vtok", [L, DH], BF16, kind="ExternalInput").ap()
    kl_in = nc.dram_tensor("kloc", [DH, TLOC], BF16, kind="ExternalInput").ap()
    vl_in = nc.dram_tensor("vloc", [TLOC, DH], BF16, kind="ExternalInput").ap()
    wo_in = nc.dram_tensor("woT", [H, H], F32, kind="ExternalInput").ap()
    bo_in = nc.dram_tensor("bo", [128, H // 128], F32, kind="ExternalInput").ap()
    mv_in = nc.dram_tensor("mvec", [nJT, TLOC], BF16, kind="ExternalInput").ap()
    sel_in = nc.dram_tensor("sel64", [64, 64 * 128], BF16, kind="ExternalInput").ap()
    dm_in = nc.dram_tensor("dmask", [128, 128], F32, kind="ExternalInput").ap()
    lf_in = nc.dram_tensor("lfix", [1, TLOC], F32, kind="ExternalInput").ap()
    rm_in = nc.dram_tensor("rmask", [1, TLOC], F32, kind="ExternalInput").ap()
    on1_in = nc.dram_tensor("one1", [128, 1], BF16, kind="ExternalInput").ap()
    s1_out = nc.dram_tensor("s1T", [H, TLOC], F32, kind="ExternalOutput").ap()
    rb_dram = nc.dram_tensor("rb_dram", [NH, TLOC], F32)
    with ExitStack() as ctx:
        sb, ps, sem = _ctx(nc, ctx)
        s_q = sb("s_q", [128, NH * TLOC], BF16)
        s_k = sb("s_k", [128, L], BF16)
        s_v = sb("s_v", [128, nJT * DH], BF16)
        s_kl = sb("s_kl", [128, TLOC], BF16)
        s_vl = sb("s_vl", [128, nIT * DH], BF16)
        s_mv = sb("s_mv", [64, TLOC], BF16)
        s_sel = sb("s_sel", [64, 64 * 128], BF16)
        s_dm = sb("s_dm", [128, 128])
        s_lf = sb("s_lf", [1, TLOC])
        s_rm = sb("s_rm", [1, TLOC])
        s_onec = sb("s_onec", [128, 1], BF16)
        s_P = [sb(f"s_P{p}", [128, 1024], BF16) for p in range(3)]
        s_l = sb("s_l", [1, TLOC])
        s_lr = sb("s_lr", [1, TLOC])
        s_rb = [sb(f"s_rb{p}", [128, TLOC]) for p in range(2)]
        s_ao = sb("s_ao", [128, NH * TLOC], BF16)
        s_wo = [sb(f"s_wo{p}", [128, nHT * 128], BF16) for p in range(2)]
        s_bo = sb("s_bo", [128, nHT])
        s_st = [sb(f"s_st{p}", [128, 512]) for p in range(4)]
        p_S = [ps(f"p_S{p}", [128, 1024]) for p in range(2)]
        p_O = ps("p_O", [128, 1024])
        p_l = ps("p_l", [128, 1024])
        ld, ldm, lw0, lw1, vd, mm, ac, lv, oe, pvd, pvu, mm2, ev = (
            sem(n) for n in ["ld", "ldm", "lw0", "lw1", "vd", "mm", "ac", "lv", "oe",
                             "pvd", "pvu", "mm2", "ev"])
        stb = [sem(f"st{b}") for b in range(4)]
        lw = [lw0, lw1]
        block = ctx.enter_context(nc.Block())
        NLD = 12

        def emit_pv(tensor, w):
            h, u = divmod(w, UH)
            last = (u == UH - 1)
            if u == 0 and h >= 1:
                tensor.wait_ge(oe, h)
            tensor.wait_ge(ac, w + 1)
            if u < nIT:
                i = u
                st0 = (i == 0 or i == 4)
                tensor.matmul(p_O[:, i * 128:(i + 1) * 128],
                              s_vl[:, i * 128:(i + 1) * 128], s_P[w % 3][:, 0:128],
                              start=st0, stop=False)
                ins = tensor.matmul(p_l[0:1, i * 128:(i + 1) * 128],
                                    s_onec[:, 0:1], s_P[w % 3][:, 0:128],
                                    start=st0, stop=False)
            else:
                j = u - nIT
                t0 = I0(j) * 128
                chunks = ([(t0, 512), (512, 1024)] if t0 < 512 else [(t0, 1024)])
                ml = None
                for a, b in chunks:
                    stp = (j == 30 and b == 512) or (j == 62)
                    tensor.matmul(p_O[:, a:b],
                                  s_v[:, j * 128:(j + 1) * 128], s_P[w % 3][:, a:b],
                                  start=False, stop=stp)
                    ml = tensor.matmul(p_l[0:1, a:b],
                                       s_onec[:, 0:1], s_P[w % 3][:, a:b],
                                       start=False, stop=stp)
                ml.then_inc(pvu, 1)
                return
            ins.then_inc(pvu, 1)

        @block.sync
        def _(sync):
            for dst, src in [(s_q, q_in.rearrange("(ht p) t -> p ht t", p=128)),
                             (s_k, k_in), (s_v, v_in.rearrange("(j p) d -> p j d", p=128)),
                             (s_kl, kl_in), (s_vl, vl_in.rearrange("(i p) d -> p i d", p=128)),
                             (s_mv, mv_in), (s_sel, sel_in), (s_dm, dm_in),
                             (s_lf, lf_in), (s_rm, rm_in), (s_onec, on1_in)]:
                sync.dma_start(out=dst[:], in_=src).then_inc(ld, 16)
            sync.dma_start(out=s_bo[:], in_=bo_in).then_inc(ld, 16)
            for h in range(NH):
                sync.wait_ge(lv, h + 1)
                sync.dma_start(out=rb_dram.ap()[h:h + 1, :], in_=s_lr[:]).then_inc(ldm, 16)
            sync.wait_ge(ldm, 16 * NH)
            for h in range(NH):
                if h >= 2:
                    sync.wait_ge(oe, NH + h - 1)
                sync.dma_start(out=s_rb[h % 2][:],
                               in_=rb_dram.ap()[h:h + 1, :].broadcast_to([128, TLOC])).then_inc(ldm, 16)
            for k in range(nHT * 2):
                sync.wait_ge(ev, k + 1)
                ot, ch = divmod(k, 2)
                sync.dma_start(out=s1_out[ot * 128:(ot + 1) * 128, ch * 512:(ch + 1) * 512],
                               in_=s_st[k % 4][:]).then_inc(stb[k % 4], 16)
            for b in range(4):
                sync.wait_ge(stb[b], 16 * ((nHT * 2 - b + 3) // 4))

        @block.gpsimd
        def _(gpsimd):
            for ot in range(nHT):
                if ot >= 2:
                    gpsimd.wait_ge(mm2, (ot - 1) * 2)
                gpsimd.dma_start(out=s_wo[ot % 2][:],
                                 in_=wo_in[:, ot * 128:(ot + 1) * 128].rearrange(
                                     "(kt p) o -> p kt o", p=128)).then_inc(lw[ot % 2], 16)

        @block.tensor
        def _(tensor):
            tensor.wait_ge(ld, 16 * NLD)
            NW = NH * UH
            for w in range(NW):
                h, u = divmod(w, UH)
                if w >= 2:
                    tensor.wait_ge(ac, w - 1)
                if u < nIT:
                    i = u
                    tensor.matmul(p_S[w % 2][:, 0:128],
                                  s_kl[:, i * 128:(i + 1) * 128],
                                  s_q[:, h * TLOC + i * 128:h * TLOC + (i + 1) * 128],
                                  start=True, stop=True).then_inc(mm, 1)
                else:
                    j = u - nIT
                    t0 = I0(j) * 128
                    chunks = ([(t0, 512), (512, 1024)] if t0 < 512 else [(t0, 1024)])
                    chunks = [(a, b) for a, b in chunks if b - a <= 512] or [(t0, 512), (512, 1024)]
                    for ci, (a, b) in enumerate(chunks):
                        tensor.matmul(p_S[w % 2][:, a:b],
                                      s_k[:, j * 128:(j + 1) * 128],
                                      s_q[:, h * TLOC + a:h * TLOC + b],
                                      start=True, stop=False)
                        ins = tensor.matmul(p_S[w % 2][:, a:b],
                                            s_sel[:, j * 128:(j + 1) * 128],
                                            s_mv[:, a:b],
                                            start=False, stop=True)
                        if ci == len(chunks) - 1:
                            ins.then_inc(mm, 1)
                if w >= 2:
                    emit_pv(tensor, w - 2)
            for wl in range(NW - 2, NW):
                emit_pv(tensor, wl)
            tensor.wait_ge(oe, 2 * NH)
            for ot in range(nHT):
                tensor.wait_ge(lw[ot % 2], 16 * (ot // 2 + 1))
                for ch in range(2):
                    k = ot * 2 + ch
                    if k >= 2:
                        tensor.wait_ge(ev, k - 1)
                    for kt in range(nHT):
                        ins = tensor.matmul(p_S[k % 2][:, 0:512],
                                            s_wo[ot % 2][:, kt * 128:(kt + 1) * 128],
                                            s_ao[:, kt * TLOC + ch * 512:kt * TLOC + ch * 512 + 512],
                                            start=(kt == 0), stop=(kt == nHT - 1))
                        if kt == nHT - 1:
                            ins.then_inc(mm2, 1)

        @block.scalar
        def _(scalar):
            for h in range(NH):
                for u in range(UH):
                    w = h * UH + u
                    if w >= 3:
                        scalar.wait_ge(pvu, w - 2)
                    if u < nIT:
                        scalar.wait_ge(vd, h * nIT + u + 1)
                        scalar.activation(s_P[w % 3][:, 0:128], p_S[w % 2][:, 0:128],
                                          AF.Exp, scale=SCALE).then_inc(ac, 1)
                    else:
                        t0 = I0(u - nIT) * 128
                        scalar.wait_ge(mm, w + 1)
                        scalar.activation(s_P[w % 3][:, t0:1024], p_S[w % 2][:, t0:1024],
                                          AF.Exp, scale=SCALE).then_inc(ac, 1)
            for k in range(nHT * 2):
                scalar.wait_ge(mm2, k + 1)
                if k >= 4:
                    scalar.wait_ge(stb[k % 4], 16 * (k // 4))
                ot = k // 2
                if SIM_SAFE_ACT:
                    scalar.activation(s_st[k % 4][:], p_S[k % 2][:, 0:512], AF.Relu,
                                      bias=s_bo[:, ot:ot + 1]).then_inc(ev, 1)
                else:
                    scalar.activation(s_st[k % 4][:], p_S[k % 2][:, 0:512], AF.Prelu,
                                      bias=s_bo[:, ot:ot + 1], alpha=1.0).then_inc(ev, 1)

        @block.vector
        def _(vector):
            vector.wait_ge(ld, 16 * NLD)

            def evac_head(h):
                vector.wait_ge(pvu, (h + 1) * UH)
                vector.tensor_copy(s_ao[:, h * TLOC:(h + 1) * TLOC], p_O[:, 0:1024])
                vector.drain()
                vector.tensor_copy(s_l[:, 0:1024], p_l[0:1, 0:1024])
                vector.drain()
                vector.sem_inc(oe, 1)
                vector.tensor_tensor(s_l[:], s_l[:], s_lf[:], AluOpType.add)
                vector.drain()
                vector.reciprocal(s_l[:], s_l[:])
                vector.drain()
                if h >= 1:
                    vector.wait_ge(ldm, 16 * h)
                vector.tensor_tensor(s_lr[:], s_l[:], s_rm[:], AluOpType.mult)
                vector.drain()
                vector.sem_inc(lv, 1)

            def dmask_add(h, i):
                w = h * UH + i
                vector.wait_ge(mm, w + 1)
                vector.tensor_tensor(p_S[w % 2][:, 0:128], p_S[w % 2][:, 0:128],
                                     s_dm[:], AluOpType.add)
                vector.drain()
                vector.sem_inc(vd, 1)

            for h in range(NH):
                dmask_add(h, 0)
                dmask_add(h, 1)
                if h >= 1:
                    evac_head(h - 1)
                for i in range(2, nIT):
                    dmask_add(h, i)
            evac_head(NH - 1)
            for h in range(NH):
                vector.wait_ge(ldm, 16 * NH + 16 * (h + 1))
                vector.tensor_tensor(s_ao[:, h * TLOC:(h + 1) * TLOC],
                                     s_ao[:, h * TLOC:(h + 1) * TLOC],
                                     s_rb[h % 2][:], AluOpType.mult)
                vector.drain()
                vector.sem_inc(oe, 1)
    return nc


def _launch_attn(qT_list, kT_full, v_full, Wo, bo):
    nc = _cached("attn", _mk_attn)
    woT = _f32(np.asarray(Wo).T)
    boc = np.ascontiguousarray(_f32(bo).reshape(H // 128, 128).T)
    dmask = np.where(np.arange(128)[:, None] < np.arange(128)[None, :], -30.0 / SCALE, NEG).astype(np.float32)
    sel64 = np.repeat(np.eye(64, dtype=BF), 128, axis=1)
    one1 = np.ones((128, 1), BF)
    qT_full = np.concatenate(qT_list, axis=1)
    ims = []
    for c in range(NC):
        cols = (np.arange(TLOC) // 128) * 8 * 128 + c * 128 + np.arange(TLOC) % 128
        g_of_t = 8 * (np.arange(TLOC) // 128) + c
        mvec = np.where(np.arange(64)[:, None] < g_of_t[None, :], -30.0 / SCALE, NEG).astype(BF)
        lfix = np.zeros((1, TLOC), np.float32)
        rmask = np.ones((1, TLOC), np.float32)
        if c == 0:
            lfix[0, 0] = 1.0
            rmask[0, 0] = 0.0
        ims.append(dict(qT=np.ascontiguousarray(qT_full[:, cols]), kT=kT_full, vtok=v_full,
                        kloc=np.ascontiguousarray(kT_full[:, cols]),
                        vloc=np.ascontiguousarray(v_full[cols, :]),
                        woT=woT, bo=boc, mvec=mvec, sel64=sel64, dmask=dmask,
                        lfix=lfix, rmask=rmask, one1=one1))
    res = _run(nc, ims)
    s1_full = np.empty((H, L), np.float32)
    for c in range(NC):
        cols = (np.arange(TLOC) // 128) * 8 * 128 + c * 128 + np.arange(TLOC) % 128
        s1_full[:, cols] = res[c]["s1T"]
    return [np.ascontiguousarray(s1_full[:, c * TLOC:(c + 1) * TLOC]) for c in range(NC)]
def _mk_moe():
    nKT = H // 128
    nOT = H // 128
    nTT = TLOC // 128
    NPH = 3 * EX + 3
    nc = bass.Bass("TRN2", target_bir_lowering=False)
    x_in = nc.dram_tensor("xn1T", [H, TLOC], BF16, kind="ExternalInput").ap()
    wr_in = nc.dram_tensor("Wr", [H, EX], F32, kind="ExternalInput").ap()
    br_in = nc.dram_tensor("br", [EX, 1], F32, kind="ExternalInput").ap()
    gb_in = nc.dram_tensor("gbb", [128, EX], F32, kind="ExternalInput").ap()
    wup_in = nc.dram_tensor("Wup", [EX * H, H], F32, kind="ExternalInput").ap()
    wgt_in = nc.dram_tensor("Wgate", [EX * H, H], F32, kind="ExternalInput").ap()
    wdn_in = nc.dram_tensor("Wdown", [EX * H, H], F32, kind="ExternalInput").ap()
    bup_in = nc.dram_tensor("bup", [128, EX * H // 128], F32, kind="ExternalInput").ap()
    bgt_in = nc.dram_tensor("bgate", [128, EX * H // 128], F32, kind="ExternalInput").ap()
    bdn_in = nc.dram_tensor("bdn", [EX, H], F32, kind="ExternalInput").ap()
    wsu_in = nc.dram_tensor("Wsu", [H, H], F32, kind="ExternalInput").ap()
    wsg_in = nc.dram_tensor("Wsg", [H, H], F32, kind="ExternalInput").ap()
    wsd_in = nc.dram_tensor("Wsd", [H, H], F32, kind="ExternalInput").ap()
    bsu_in = nc.dram_tensor("bsu", [128, H // 128], F32, kind="ExternalInput").ap()
    bsg_in = nc.dram_tensor("bsg", [128, H // 128], F32, kind="ExternalInput").ap()
    bsd_in = nc.dram_tensor("bsd", [128, H // 128], F32, kind="ExternalInput").ap()
    idf_in = nc.dram_tensor("idf", [128, 128], F32, kind="ExternalInput").ap()
    acc_out = nc.dram_tensor("accT", [H, TLOC], F32, kind="ExternalOutput").ap()
    pr_dram = nc.dram_tensor("pr_dram", [EX, TLOC], BF16)

    W_of = {}
    for e in range(EX):
        W_of[3 * e] = ("up", e, wup_in, e * H)
        W_of[3 * e + 1] = ("gate", e, wgt_in, e * H)
        W_of[3 * e + 2] = ("down", e, wdn_in, e * H)
    W_of[21] = ("up", -1, wsu_in, 0)
    W_of[22] = ("gate", -1, wsg_in, 0)
    W_of[23] = ("down", -1, wsd_in, 0)

    with ExitStack() as ctx:
        sb, ps, sem = _ctx(nc, ctx)
        s_x = sb("s_x", [128, nKT * TLOC], BF16)
        s_u = sb("s_u", [128, nOT * TLOC], BF16)
        s_sg = sb("s_sg", [128, nOT * TLOC], BF16)
        s_acc = sb("s_acc", [128, nOT * TLOC])
        s_w = [sb(f"s_w{p}", [128, nKT * 128], BF16) for p in range(2)]
        s_wr = sb("s_wr", [128, nKT * EX], BF16)
        s_br = sb("s_br", [EX, 1])
        s_gbb = sb("s_gbb", [128, EX])
        s_bup = sb("s_bup", [128, EX * nOT])
        s_bgt = sb("s_bgt", [128, EX * nOT])
        s_bd16 = sb("s_bd16", [EX, H], BF16)
        s_bsu = sb("s_bsu", [128, nOT])
        s_bsg = sb("s_bsg", [128, nOT])
        s_bsd = sb("s_bsd", [128, nOT])
        s_idf = sb("s_idf", [128, 128])
        s_g = sb("s_g", [128, TLOC])
        s_gt = sb("s_gt", [128, nTT * EX])
        s_pt = sb("s_pt", [128, nTT * EX])
        s_prT = sb("s_prT", [EX, TLOC])
        s_prT16 = sb("s_prT16", [EX, TLOC], BF16)
        s_pb = sb("s_pb", [128, TLOC], BF16)
        s_m1 = sb("s_m1", [128, 1])
        s_m2 = sb("s_m2", [128, 1])
        s_nm = sb("s_nm", [128, 1])
        s_z = sb("s_z", [128, 1])
        s_e7 = sb("s_e7", [128, EX])
        s_bg = sb("s_bg", [128, EX])
        s_msk = sb("s_msk", [128, EX])
        s_t7 = sb("s_t7", [128, EX])
        p_mm = [ps(f"p_mm{p}", [128, 512]) for p in range(2)]
        p_t = ps("p_t", [128, 128])
        ld, ldg, lw0, lw1, mm, eva, evv, te, vv, g1, g2, pv2, gv, lpb, fin, st = (
            sem(n) for n in ["ld", "ldg", "lw0", "lw1", "mm", "eva", "evv", "te", "vv", "g1",
                             "g2", "pv2", "gv", "lpb", "fin", "st"])
        lpr = sem("lpr")

        def SUG(q):
            return sum(1 for qq in range(q) if qq % 3 != 2 if qq < 21) + \
                   sum(1 for qq in range(21, q) if qq in (21, 22))

        def SD(q):
            return sum(1 for qq in range(min(q, 24)) if (qq < 21 and qq % 3 == 2) or qq == 23)

        def evac_of_chunk(c):
            if c < 2:
                return ("v", c + 1)
            if c < 2 + NPH * 32:
                q, r = divmod(c - 2, 32)
                kind = W_of[q][0]
                if kind == "down":
                    return ("v", 2 + 32 * SD(q) + r + 1)
                return ("a", 32 * SUG(q) + r + 1)
            r = c - (2 + NPH * 32)
            return ("v", 2 + 32 * (SD(24)) + r + 1)
        lw = [lw0, lw1]
        block = ctx.enter_context(nc.Block())
        MM_GATE, MM_GT, MM_PT, MM_EXP, MM_BD = 0, 2, 10, 18, 786
        EV_GATE, EV_EXP, EV_BD = 0, 2, 770
        def vv_after(e):
            return 32 * (e + 1)
        VV_SH = 32 * EX + 16

        @block.sync
        def _(sync):
            for dst, src in [
                (s_x, x_in.rearrange("(kt p) t -> p kt t", p=128)),
                (s_br, br_in),
                (s_gbb, gb_in),
                (s_bup, bup_in),
                (s_bgt, bgt_in),
                (s_bsu, bsu_in),
                (s_bsg, bsg_in),
                (s_bsd, bsd_in),
                (s_idf, idf_in),
            ]:
                sync.dma_start(out=dst[:], in_=src).then_inc(ld, 16)
            sync.wait_ge(gv, 1)
            sync.dma_start(out=pr_dram.ap(), in_=s_prT16[:]).then_inc(lpr, 16)
            for ot in range(nOT):
                sync.wait_ge(fin, ot + 1)
                sync.dma_start(out=acc_out[ot * 128:(ot + 1) * 128, :],
                               in_=s_acc[:, ot * TLOC:(ot + 1) * TLOC]).then_inc(st, 16)
            sync.wait_ge(st, 16 * nOT)

        @block.gpsimd
        def _(gpsimd):
            gpsimd.dma_start(out=s_wr[:], in_=wr_in.rearrange("(kt p) e -> p kt e", p=128)).then_inc(ldg, 16)
            gpsimd.dma_start(out=s_bd16[:], in_=bdn_in).then_inc(ldg, 16)
            for q in range(NPH):
                kind, e, w_dram, row0 = W_of[q]
                for ot in range(nOT):
                    u = q * nOT + ot
                    if u >= 2:
                        gpsimd.wait_ge(mm, MM_EXP + (u - 1) * 2)
                    gpsimd.dma_start(out=s_w[u % 2][:],
                                     in_=w_dram[row0:row0 + H, ot * 128:(ot + 1) * 128].rearrange(
                                         "(kt p) o -> p kt o", p=128)).then_inc(lw[u % 2], 16)
                if kind == "up" and e >= 0:
                    gpsimd.wait_ge(lpr, 16)
                    if e >= 1:
                        gpsimd.wait_ge(vv, vv_after(e - 1))
                    gpsimd.dma_start(out=s_pb[:],
                                     in_=pr_dram.ap()[e:e + 1, :].broadcast_to([128, TLOC])).then_inc(lpb, 16)

        @block.tensor
        def _(tensor):
            tensor.wait_ge(ld, 16 * 9)
            tensor.wait_ge(ldg, 32)
            for ch in range(2):
                for kt in range(nKT):
                    ins = tensor.matmul(p_mm[ch][0:EX, 0:512],
                                        s_wr[:, kt * EX:(kt + 1) * EX],
                                        s_x[:, kt * TLOC + ch * 512:kt * TLOC + ch * 512 + 512],
                                        start=(kt == 0), stop=(kt == nKT - 1))
                    if kt == nKT - 1:
                        ins.then_inc(mm, 1)
            for tt in range(nTT):
                tensor.wait_ge(evv, 2)
                tensor.wait_ge(te, tt)
                tensor.transpose(p_t[:, 0:128], s_g[:, tt * 128:(tt + 1) * 128],
                                 s_idf[:]).then_inc(mm, 1)
            for tt in range(nTT):
                tensor.wait_ge(pv2, tt + 1)
                tensor.wait_ge(te, nTT + tt)
                tensor.transpose(p_t[0:EX, 0:128], s_pt[:, tt * EX:(tt + 1) * EX],
                                 s_idf[:]).then_inc(mm, 1)
            for q in range(NPH):
                kind, e, w_dram, row0 = W_of[q]
                rhs = {"up": s_x, "gate": s_u, "down": s_sg}[kind]
                if kind == "gate":
                    tensor.wait_ge(eva, 32 * (SUG(q - 1) + 1))
                if kind == "down":
                    tensor.wait_ge(vv, vv_after(e) if e >= 0 else VV_SH)
                if q == 21:
                    tensor.wait_ge(vv, vv_after(EX - 1))
                for ot in range(nOT):
                    u = q * nOT + ot
                    tensor.wait_ge(lw[u % 2], 16 * (u // 2 + 1))
                    for ch in range(2):
                        c = u * 2 + ch
                        cg = 2 + c
                        kk, cnt = evac_of_chunk(cg - 2)
                        tensor.wait_ge(eva if kk == "a" else evv, cnt)
                        for kt in range(nKT):
                            ins = tensor.matmul(p_mm[c % 2][:, 0:512],
                                                s_w[u % 2][:, kt * 128:(kt + 1) * 128],
                                                rhs[:, kt * TLOC + ch * 512:kt * TLOC + ch * 512 + 512],
                                                start=(kt == 0), stop=(kt == nKT - 1))
                            if kt == nKT - 1:
                                ins.then_inc(mm, 1)
            tensor.wait_ge(gv, 2)
            for ot in range(nOT):
                for ch in range(2):
                    c = NPH * 32 + ot * 2 + ch
                    cg = 2 + c
                    kk, cnt = evac_of_chunk(cg - 2)
                    tensor.wait_ge(eva if kk == "a" else evv, cnt)
                    tensor.matmul(p_mm[(cg) % 2][:, 0:512],
                                  s_bd16[:, ot * 128:(ot + 1) * 128],
                                  s_prT16[:, ch * 512:(ch + 1) * 512],
                                  start=True, stop=True).then_inc(mm, 1)

        @block.scalar
        def _(scalar):
            for tt in range(nTT):
                scalar.wait_ge(g1, tt + 1)
                scalar.activation(s_e7[:], s_gt[:, tt * EX:(tt + 1) * EX], AF.Exp,
                                  bias=s_nm[:, 0:1]).then_inc(g2, 1)
            for q in range(NPH):
                kind, e, w_dram, row0 = W_of[q]
                if kind == "down":
                    continue
                dst = s_u if kind == "up" else s_sg
                bias = (s_bup if kind == "up" else s_bgt)
                act = (AF.Relu if SIM_SAFE_ACT else AF.Prelu) if kind == "up" else AF.Silu
                for ot in range(nOT):
                    u = q * nOT + ot
                    for ch in range(2):
                        c = u * 2 + ch
                        scalar.wait_ge(mm, MM_EXP + c + 1)
                        bcol = (bias[:, e * nOT + ot:e * nOT + ot + 1] if e >= 0
                                else (s_bsu if kind == "up" else s_bsg)[:, ot:ot + 1])
                        kw = dict(bias=bcol)
                        if act == AF.Prelu:
                            kw["alpha"] = 1.0
                        scalar.activation(dst[:, ot * TLOC + ch * 512:ot * TLOC + ch * 512 + 512],
                                          p_mm[(2 + c) % 2][:, 0:512], act, **kw).then_inc(eva, 1)

        @block.vector
        def _(vector):
            vector.memset(s_acc[:], 0.0)
            vector.memset(s_g[:], 0.0)
            vector.drain()
            for ch in range(2):
                vector.wait_ge(mm, ch + 1)
                vector.tensor_scalar(s_g[0:EX, ch * 512:(ch + 1) * 512], p_mm[ch][0:EX, 0:512],
                                     s_br[:, 0:1], None, AluOpType.add)
                vector.drain()
                vector.sem_inc(evv, 1)
            for tt in range(nTT):
                vector.wait_ge(mm, MM_GT + tt + 1)
                vector.tensor_copy(s_gt[:, tt * EX:(tt + 1) * EX], p_t[:, 0:EX])
                vector.drain()
                vector.sem_inc(te, 1)
                gsl = s_gt[:, tt * EX:(tt + 1) * EX]
                vector.wait_ge(ld, 16 * 9)
                vector.tensor_tensor(s_bg[:], gsl, s_gbb[:], AluOpType.add)
                vector.drain()
                vector.reduce_max(s_m1[:, 0:1], s_bg[:], axis=AX)
                vector.drain()
                vector.tensor_scalar(s_msk[:], s_bg[:], s_m1[:, 0:1], None, AluOpType.is_equal)
                vector.drain()
                vector.scalar_tensor_tensor(s_t7[:], s_msk[:], NEG, s_bg[:],
                                            AluOpType.mult, AluOpType.add)
                vector.drain()
                vector.reduce_max(s_m2[:, 0:1], s_t7[:], axis=AX)
                vector.drain()
                vector.tensor_scalar(s_msk[:], s_bg[:], s_m2[:, 0:1], None, AluOpType.is_ge)
                vector.drain()
                vector.reduce_max(s_m1[:, 0:1], gsl, axis=AX)
                vector.drain()
                vector.tensor_scalar(s_nm[:, 0:1], s_m1[:, 0:1], -1.0, None,
                                     AluOpType.mult)
                vector.drain()
                vector.sem_inc(g1, 1)
                vector.wait_ge(g2, tt + 1)
                vector.tensor_tensor(s_t7[:], s_msk[:], s_e7[:], AluOpType.mult)
                vector.drain()
                vector.reduce_sum(s_z[:, 0:1], s_t7[:], axis=AX)
                vector.drain()
                vector.reciprocal(s_z[:, 0:1], s_z[:, 0:1])
                vector.drain()
                vector.tensor_scalar(s_pt[:, tt * EX:(tt + 1) * EX], s_t7[:],
                                     s_z[:, 0:1], None, AluOpType.mult)
                vector.drain()
                vector.sem_inc(pv2, 1)
            for tt in range(nTT):
                vector.wait_ge(mm, MM_PT + tt + 1)
                vector.tensor_copy(s_prT[0:EX, tt * 128:(tt + 1) * 128], p_t[0:EX, 0:128])
                vector.drain()
                vector.sem_inc(te, 1)
            vector.tensor_copy(s_prT16[0:EX, :], s_prT[0:EX, :])
            vector.drain()
            vector.sem_inc(gv, 1)
            vector.sem_inc(gv, 1)
            for e in range(EX):
                q = 3 * e + 1
                vector.wait_ge(lpb, 16 * (e + 1))
                for ot in range(nOT):
                    vector.wait_ge(eva, 32 * SUG(q) + ot * 2 + 2)
                    usl = s_u[:, ot * TLOC:(ot + 1) * TLOC]
                    ssl = s_sg[:, ot * TLOC:(ot + 1) * TLOC]
                    vector.tensor_tensor(ssl, ssl, usl, AluOpType.mult)
                    vector.drain()
                    vector.sem_inc(vv, 1)
                    vector.tensor_tensor(ssl, ssl, s_pb[:], AluOpType.mult)
                    vector.drain()
                    vector.sem_inc(vv, 1)
                qd = 3 * e + 2
                for ot in range(nOT):
                    for ch in range(2):
                        c = (qd * nOT + ot) * 2 + ch
                        vector.wait_ge(mm, MM_EXP + c + 1)
                        asl = s_acc[:, ot * TLOC + ch * 512:ot * TLOC + ch * 512 + 512]
                        vector.tensor_tensor(asl, asl, p_mm[(2 + c) % 2][:, 0:512], AluOpType.add)
                        vector.drain()
                        vector.sem_inc(evv, 1)
            q = 22
            for ot in range(nOT):
                vector.wait_ge(eva, 32 * SUG(q) + ot * 2 + 2)
                usl = s_u[:, ot * TLOC:(ot + 1) * TLOC]
                ssl = s_sg[:, ot * TLOC:(ot + 1) * TLOC]
                vector.tensor_tensor(ssl, ssl, usl, AluOpType.mult)
                vector.drain()
                vector.sem_inc(vv, 1)
            qd = 23
            for ot in range(nOT):
                for ch in range(2):
                    c = (qd * nOT + ot) * 2 + ch
                    vector.wait_ge(mm, MM_EXP + c + 1)
                    asl = s_acc[:, ot * TLOC + ch * 512:ot * TLOC + ch * 512 + 512]
                    vector.tensor_tensor(asl, asl, p_mm[(2 + c) % 2][:, 0:512], AluOpType.add)
                    vector.drain()
                    vector.sem_inc(evv, 1)
            for ot in range(nOT):
                asl = s_acc[:, ot * TLOC:(ot + 1) * TLOC]
                vector.tensor_scalar(asl, asl, s_bsd[:, ot:ot + 1], None, AluOpType.add)
                vector.drain()
                for ch in range(2):
                    c = NPH * 32 + ot * 2 + ch
                    vector.wait_ge(mm, MM_EXP + c + 1)
                    a2 = s_acc[:, ot * TLOC + ch * 512:ot * TLOC + ch * 512 + 512]
                    vector.tensor_tensor(a2, a2, p_mm[(2 + c) % 2][:, 0:512], AluOpType.add)
                    vector.drain()
                    vector.sem_inc(evv, 1)
                vector.sem_inc(fin, 1)
    return nc


def _launch_moe(xn1T_list, W):
    nc = _cached("moe", _mk_moe)
    com = dict(
        Wr=_f32(W["Wr"]), br=_f32(W["br"]).reshape(EX, 1),
        gbb=np.tile(_f32(W["gate_bias"]).reshape(1, EX), (128, 1)),
        Wup=_f32(W["Wup"]).reshape(EX * H, H), Wgate=_f32(W["Wgate"]).reshape(EX * H, H),
        Wdown=_f32(W["Wdown"]).reshape(EX * H, H),
        bup=np.ascontiguousarray(_f32(W["bup"]).reshape(-1, 128).T),
        bgate=np.ascontiguousarray(_f32(W["bgate"]).reshape(-1, 128).T),
        bdn=_f32(W["bdown"]), Wsu=_f32(W["Wsu"]), Wsg=_f32(W["Wsg"]), Wsd=_f32(W["Wsd"]),
        bsu=np.ascontiguousarray(_f32(W["bsu"]).reshape(-1, 128).T),
        bsg=np.ascontiguousarray(_f32(W["bsg"]).reshape(-1, 128).T),
        bsd=np.ascontiguousarray(_f32(W["bsd"]).reshape(-1, 128).T), idf=np.eye(128, dtype=np.float32))
    ims = [dict(xn1T=np.ascontiguousarray(x), **com) for x in xn1T_list]
    return [r["accT"] for r in _run(nc, ims)]


def kernel(**inp):
    x0 = _f32(inp["L0_attn_hidden_states"]).reshape(L, H)

    xnT = _launch_ln(x0)
    ipT = _f32(np.asarray(inp["L0_attn_in_proj_w"]).T)
    res = _launch_matmul("inproj", xnT, ipT, np.zeros(2 * DI, np.float32),
                         [dict(name="hsT", O=DI, act=AF.Copy, bias=False, scale=None),
                          dict(name="gsT", O=DI, act=AF.Silu, bias=False, scale=None)],
                         F32)
    hsT = [r["hsT"] for r in res]
    gsT = [r["gsT"][:, 128:] for r in res]
    xcT, ssm_loc = _launch_conv(hsT, inp["L0_attn_conv_w"], inp["L0_attn_conv_b"],
                                inp["L0_attn_x_proj_w"])
    ssmT = np.concatenate(ssm_loc, axis=1)
    tsT, Bm, Cm = ssmT[:TR], ssmT[TR:TR + NSTATE], ssmT[TR + NSTATE:]
    dtT = _launch_dtproj(tsT, inp["L0_attn_dt_proj_w"], inp["L0_attn_dt_proj_b"])
    xcT_full = np.concatenate(xcT, axis=1)
    xcT_byd = [xcT_full[c * DLOC:(c + 1) * DLOC] for c in range(NC)]
    yT_s = _launch_scan(dtT, xcT_byd, Bm, Cm, inp["L0_attn_A_log"], inp["L0_attn_D"])
    yT_full = np.concatenate(yT_s, axis=0)
    yT_tok = [np.ascontiguousarray(yT_full[:, c * TLOC:(c + 1) * TLOC]) for c in range(NC)]
    ygT = _launch_mul("yg", yT_tok, gsT)
    opT = _f32(np.asarray(inp["L0_attn_out_proj_w"]).T)
    res = _launch_matmul("outproj", ygT, opT, np.zeros(H, np.float32),
                         [dict(name="s0T", O=H, act=AF.Copy, bias=False, scale=None)], F32)
    s0T = [r["s0T"] for r in res]
    xnTo = [np.ascontiguousarray(x[:, 128:]) for x in xnT]
    w0T = _f32(np.asarray(inp["L0_ffn_W0"]).T)
    res = _launch_matmul("ffn1", xnTo, w0T, inp["L0_ffn_bias0"],
                         [dict(name="fT", O=4 * H, act=AF.Gelu, bias=True, scale=None)], BF16)
    fT = [r["fT"] for r in res]
    w1T = _f32(np.asarray(inp["L0_ffn_W1"]).T)
    res = _launch_matmul("ffn2", fT, w1T, inp["L0_ffn_bias1"],
                         [dict(name="g0T", O=H, act=AF.Silu, bias=True, scale=None)], F32)
    g0T = [r["g0T"] for r in res]
    res = _launch_epilogue("epi0", False, True,
                           [x0[c * TLOC:(c + 1) * TLOC] for c in range(NC)],
                           [np.ascontiguousarray(s.T) for s in s0T],
                           [np.ascontiguousarray(g.T) for g in g0T])
    x1 = [r["xo"] for r in res]
    xn1T = [r["xnt"] for r in res]

    wqkvT = np.concatenate([_f32(np.asarray(inp["L1_attn_Wq"]).T),
                            _f32(np.asarray(inp["L1_attn_Wk"]).T),
                            _f32(np.asarray(inp["L1_attn_Wv"]).T)], axis=1)
    bqkv = np.concatenate([_f32(inp["L1_attn_bq"]), _f32(inp["L1_attn_bk"]),
                           _f32(inp["L1_attn_bv"])])
    res = _launch_matmul("qkv", xn1T, wqkvT, bqkv,
                         [dict(name="qT", O=H, act=AF.Copy, bias=True, scale=None),
                          dict(name="kT", O=DH, act=AF.Copy, bias=True, scale=None),
                          dict(name="vT", O=DH, act=AF.Copy, bias=True, scale=None)], BF16)
    qT = [r["qT"] for r in res]
    kT_full = np.ascontiguousarray(np.concatenate([r["kT"] for r in res], axis=1))
    v_full = np.ascontiguousarray(np.concatenate([r["vT"] for r in res], axis=1).T)
    s1T = _launch_attn(qT, kT_full, v_full, inp["L1_attn_Wo"], inp["L1_attn_bo"])
    accT = _launch_moe(xn1T, {k[len("L1_ffn_"):]: v for k, v in inp.items()
                              if k.startswith("L1_ffn_")})
    res = _launch_epilogue("epi1", True, False,
                           x1,
                           [np.ascontiguousarray(s.T) for s in s1T],
                           [np.ascontiguousarray(a.T) for a in accT])
    x2 = np.concatenate([r["xo"] for r in res], axis=0)
    return x2.reshape(1, L, H).astype(np.float32)


# version 05 (final)
# speedup vs baseline: 1.0137x; 1.0091x over previous
import numpy as np
import ml_dtypes
import concourse.bass as bass
import concourse.mybir as mybir
from concourse.alu_op_type import AluOpType
from concourse.bass_utils import run_bass_kernel_spmd
from contextlib import ExitStack

F32 = mybir.dt.float32
BF16 = mybir.dt.bfloat16
AF = mybir.ActivationFunctionType
AX = mybir.AxisListType.X
BF = ml_dtypes.bfloat16

NC = 8
L = 8192
H = 2048
DI = 4096
NSTATE = 16
TR = 128
EX = 7
DH = 128
NH = 16
TLOC = L // NC
DLOC = DI // NC
NEG = -3.0e29
SCALE = 1.0 / np.sqrt(DH)
SIM_SAFE_ACT = False

_prog_cache = {}


def _cached(key, fn):
    if key not in _prog_cache:
        _prog_cache[key] = fn()
    return _prog_cache[key]


def _run(nc, in_maps):
    return run_bass_kernel_spmd(nc, in_maps, core_ids=list(range(NC))).results


def _f32(x):
    return np.ascontiguousarray(np.asarray(x), dtype=np.float32)


def _ctx(nc, ctx):
    sb = lambda n, s, d=F32: ctx.enter_context(nc.sbuf_tensor(n, s, d))
    ps = lambda n, s, d=F32: ctx.enter_context(nc.psum_tensor(n, s, d))
    sem = lambda n: ctx.enter_context(nc.semaphore(n))
    return sb, ps, sem


def _mk_ln(TT):
    nc = bass.Bass("TRN2", target_bir_lowering=False)
    x_in = nc.dram_tensor("x", [TT * 128, H], F32, kind="ExternalInput").ap()
    idn = nc.dram_tensor("idn", [128, 128], BF16, kind="ExternalInput").ap()
    xnt_out = nc.dram_tensor("xnt", [H, TT * 128], BF16, kind="ExternalOutput").ap()
    nJ = H // 128
    with ExitStack() as ctx:
        sb, ps, sem = _ctx(nc, ctx)
        s_x = [sb(f"s_x{p}", [128, H]) for p in range(2)]
        s_sq = sb("s_sq", [128, H])
        s_m = sb("s_m", [128, 1])
        s_r = sb("s_r", [128, 1])
        s_t1 = sb("s_t1", [128, 1])
        s_t2 = sb("s_t2", [128, 1])
        s_xn = [sb(f"s_xn{p}", [128, H], BF16) for p in range(2)]
        s_id = sb("s_id", [128, 128], BF16)
        s_out = [sb(f"s_out{p}", [128, 128], BF16) for p in range(4)]
        p_tp = [ps(f"p_tp{p}", [128, 128], BF16) for p in range(2)]
        lds, lx0, lx1, ac, dv, pe, ev = (sem(n) for n in
            ["lds", "lx0", "lx1", "ac", "dv", "pe", "ev"])
        stb = [sem(f"st{b}") for b in range(4)]
        lx = [lx0, lx1]
        block = ctx.enter_context(nc.Block())

        @block.sync
        def _(sync):
            sync.dma_start(out=s_id[:], in_=idn).then_inc(lds, 16)
            for it in range(min(2, TT)):
                sync.dma_start(out=s_x[it % 2][:],
                               in_=x_in[it * 128:(it + 1) * 128, :]).then_inc(lx[it % 2], 16)
            for it in range(TT):
                for j in range(nJ):
                    k = it * nJ + j
                    sync.wait_ge(ev, k + 1)
                    sync.dma_start(out=xnt_out[j * 128:(j + 1) * 128, it * 128:(it + 1) * 128],
                                   in_=s_out[k % 4][:]).then_inc(stb[k % 4], 16)
                if it + 2 < TT:
                    sync.wait_ge(dv, 3 * it + 3)
                    sync.dma_start(out=s_x[it % 2][:],
                                   in_=x_in[(it + 2) * 128:(it + 3) * 128, :]).then_inc(lx[it % 2], 16)
            for b in range(4):
                sync.wait_ge(stb[b], 16 * ((nJ * TT - b + 3) // 4))

        @block.scalar
        def _(scalar):
            for it in range(TT):
                scalar.wait_ge(lx[it % 2], 16 * (it // 2 + 1))
                if it >= 1:
                    scalar.wait_ge(dv, 3 * (it - 1) + 2)
                scalar.activation(s_sq[:], s_x[it % 2][:], AF.Square).then_inc(ac, 1)
                scalar.wait_ge(dv, it * 3 + 2)
                scalar.activation(s_t2[:, 0:1], s_t1[:, 0:1], AF.Ln)
                scalar.drain()
                scalar.activation(s_r[:, 0:1], s_t2[:, 0:1], AF.Exp, scale=-0.5).then_inc(ac, 1)

        @block.vector
        def _(vector):
            for it in range(TT):
                x = s_x[it % 2]
                vector.wait_ge(lx[it % 2], 16 * (it // 2 + 1))
                vector.reduce_sum(s_m[:, 0:1], x[:], axis=AX)
                vector.drain()
                vector.tensor_scalar(s_m[:, 0:1], s_m[:, 0:1], 1.0 / H, None,
                                     AluOpType.mult).then_inc(dv, 1)
                vector.wait_ge(ac, it * 2 + 1)
                vector.reduce_sum(s_t1[:, 0:1], s_sq[:], axis=AX)
                vector.drain()
                vector.tensor_tensor(s_t2[:, 0:1], s_m[:, 0:1], s_m[:, 0:1], AluOpType.mult)
                vector.drain()
                vector.tensor_scalar(s_t1[:, 0:1], s_t1[:, 0:1], 1.0 / H, None, AluOpType.mult)
                vector.drain()
                vector.tensor_scalar(s_t1[:, 0:1], s_t1[:, 0:1], s_t2[:, 0:1], 1e-5,
                                     AluOpType.subtract, AluOpType.add).then_inc(dv, 1)
                vector.wait_ge(ac, it * 2 + 2)
                if it >= 2:
                    vector.wait_ge(pe, nJ * (it - 1))
                vector.tensor_scalar(s_xn[it % 2][:], x[:], s_m[:, 0:1], s_r[:, 0:1],
                                     AluOpType.subtract, AluOpType.mult).then_inc(dv, 1)
                for j in range(nJ):
                    k = it * nJ + j
                    vector.wait_ge(pe, k + 1)
                    if k >= 4:
                        vector.wait_ge(stb[k % 4], 16 * (k // 4))
                    vector.tensor_copy(s_out[k % 4][:], p_tp[k % 2][:]).then_inc(ev, 1)

        @block.tensor
        def _(tensor):
            tensor.wait_ge(lds, 16)
            for it in range(TT):
                tensor.wait_ge(dv, it * 3 + 3)
                for j in range(nJ):
                    k = it * nJ + j
                    if k >= 2:
                        tensor.wait_ge(ev, k - 1)
                    tensor.transpose(p_tp[k % 2][:], s_xn[it % 2][:, j * 128:(j + 1) * 128],
                                     s_id[:]).then_inc(pe, 1)
    return nc


def _launch_ln(x0):
    nc = _cached(("ln", 9), lambda: _mk_ln(9))
    idn = np.eye(128, dtype=BF)
    ims = []
    for c in range(NC):
        lo = c * TLOC - 128
        if lo < 0:
            xh = np.concatenate([np.zeros((128, H), np.float32), x0[:(c + 1) * TLOC]], 0)
        else:
            xh = x0[lo:(c + 1) * TLOC]
        ims.append(dict(x=_f32(xh), idn=idn))
    return [r["xnt"] for r in _run(nc, ims)]


def _mk_matmul(K, T, outs, out_dtype):
    nKT = K // 128
    nch = (T + 511) // 512
    Otot = sum(o["O"] for o in outs)
    nOT = Otot // 128
    nc = bass.Bass("TRN2", target_bir_lowering=False)
    rhs_in = nc.dram_tensor("rhsT", [K, T], BF16, kind="ExternalInput").ap()
    w_in = nc.dram_tensor("wT", [K, Otot], F32, kind="ExternalInput").ap()
    b_in = nc.dram_tensor("bias", [128, Otot // 128], F32, kind="ExternalInput").ap()
    d_out = {o["name"]: nc.dram_tensor(o["name"], [o["O"], T], out_dtype, kind="ExternalOutput").ap()
             for o in outs}

    def tinfo(ot):
        base = 0
        for o in outs:
            if base <= ot * 128 < base + o["O"]:
                return o, base
            base += o["O"]
        raise AssertionError

    with ExitStack() as ctx:
        sb, ps, sem = _ctx(nc, ctx)
        s_rhs = sb("s_rhs", [128, nKT * T], BF16)
        s_w = [sb(f"s_w{p}", [128, nKT * 128], BF16) for p in range(2)]
        s_b = sb("s_b", [128, nOT])
        s_st = [sb(f"s_st{p}", [128, 512], out_dtype) for p in range(4)]
        p_mm = [ps(f"p_mm{p}", [128, 512]) for p in range(2)]
        ldr, lw0, lw1, mm, ev = (sem(n) for n in ["ldr", "lw0", "lw1", "mm", "ev"])
        stb = [sem(f"st{b}") for b in range(4)]
        lw = [lw0, lw1]
        block = ctx.enter_context(nc.Block())

        @block.sync
        def _(sync):
            sync.dma_start(out=s_rhs[:],
                           in_=rhs_in.rearrange("(kt p) t -> p kt t", p=128)).then_inc(ldr, 16)
            sync.dma_start(out=s_b[:], in_=b_in).then_inc(ldr, 16)
            for ot in range(nOT):
                o, base = tinfo(ot)
                for ch in range(nch):
                    k = ot * nch + ch
                    c0 = ch * 512
                    cw = min(512, T - c0)
                    sync.wait_ge(ev, k + 1)
                    sync.dma_start(out=d_out[o["name"]][ot * 128 - base:ot * 128 - base + 128, c0:c0 + cw],
                                   in_=s_st[k % 4][:, 0:cw]).then_inc(stb[k % 4], 16)
            for b in range(4):
                sync.wait_ge(stb[b], 16 * ((nOT * nch - b + 3) // 4))

        @block.gpsimd
        def _(gpsimd):
            for ot in range(nOT):
                if ot >= 2:
                    gpsimd.wait_ge(mm, (ot - 1) * nch)
                gpsimd.dma_start(out=s_w[ot % 2][:],
                                 in_=w_in[:, ot * 128:(ot + 1) * 128].rearrange(
                                     "(kt p) o -> p kt o", p=128)).then_inc(lw[ot % 2], 16)

        @block.tensor
        def _(tensor):
            tensor.wait_ge(ldr, 32)
            for ot in range(nOT):
                tensor.wait_ge(lw[ot % 2], 16 * (ot // 2 + 1))
                for ch in range(nch):
                    k = ot * nch + ch
                    c0 = ch * 512
                    cw = min(512, T - c0)
                    if k >= 2:
                        tensor.wait_ge(ev, k - 1)
                    for kt in range(nKT):
                        ins = tensor.matmul(p_mm[k % 2][:, 0:cw],
                                            s_w[ot % 2][:, kt * 128:(kt + 1) * 128],
                                            s_rhs[:, kt * T + c0:kt * T + c0 + cw],
                                            start=(kt == 0), stop=(kt == nKT - 1))
                        if kt == nKT - 1:
                            ins.then_inc(mm, 1)

        @block.scalar
        def _(scalar):
            scalar.wait_ge(ldr, 32)
            for ot in range(nOT):
                o, base = tinfo(ot)
                for ch in range(nch):
                    k = ot * nch + ch
                    cw = min(512, T - ch * 512)
                    scalar.wait_ge(mm, k + 1)
                    if k >= 4:
                        scalar.wait_ge(stb[k % 4], 16 * (k // 4))
                    kw = {}
                    fn = o["act"]
                    if o["bias"]:
                        kw["bias"] = s_b[:, ot:ot + 1]
                        if fn == AF.Copy:
                            fn = AF.Relu if SIM_SAFE_ACT else AF.Prelu
                            if not SIM_SAFE_ACT:
                                kw["alpha"] = 1.0
                    if o.get("scale") is not None:
                        kw["scale"] = o["scale"]
                    scalar.activation(s_st[k % 4][:, 0:cw], p_mm[k % 2][:, 0:cw],
                                      fn, **kw).then_inc(ev, 1)
    return nc


def _launch_matmul(key, rhsT_list, wT, bias, outs, out_dtype):
    K, T = wT.shape[0], rhsT_list[0].shape[1]
    nc = _cached((key, K, T), lambda: _mk_matmul(K, T, outs, out_dtype))
    wTc = _f32(wT)
    bc = np.ascontiguousarray(_f32(bias).reshape(-1, 128).T)
    ims = [dict(rhsT=np.ascontiguousarray(r), wT=wTc, bias=bc) for r in rhsT_list]
    return _run(nc, ims)


def _mk_conv():
    TH = TLOC + 3
    nG = DI // 128
    nc = bass.Bass("TRN2", target_bir_lowering=False)
    hs_in = nc.dram_tensor("hsT", [DI, 1152], F32, kind="ExternalInput").ap()
    cw_in = nc.dram_tensor("cw", [DI, 4], F32, kind="ExternalInput").ap()
    cb_in = nc.dram_tensor("cb", [128, DI // 128], F32, kind="ExternalInput").ap()
    xpw_in = nc.dram_tensor("xpwT", [DI, 160], F32, kind="ExternalInput").ap()
    xc_out = nc.dram_tensor("xcT", [DI, TLOC], F32, kind="ExternalOutput").ap()
    ssm_out = nc.dram_tensor("ssm", [160, TLOC], F32, kind="ExternalOutput").ap()
    with ExitStack() as ctx:
        sb, ps, sem = _ctx(nc, ctx)
        s_hs = [sb(f"s_hs{p}", [128, TH]) for p in range(2)]
        s_acc = [sb(f"s_acc{p}", [128, TLOC]) for p in range(2)]
        s_xc = [sb(f"s_xc{p}", [128, TLOC]) for p in range(2)]
        s_xc16 = sb("s_xc16", [128, nG * TLOC], BF16)
        s_cw = sb("s_cw", [128, nG * 4])
        s_cb = sb("s_cb", [128, nG])
        s_w16 = sb("s_w16", [128, nG * 160], BF16)
        s_ssm = sb("s_ssm", [128, TLOC])
        p_mm = [ps(f"p_mm{p}", [128, 512]) for p in range(2)]
        lds, lh0, lh1, ldw, dv, ac, mm, ev = (sem(n) for n in
            ["lds", "lh0", "lh1", "ldw", "dv", "ac", "mm", "ev"])
        stx = [sem("stx0"), sem("stx1")]
        sts = [sem("sts0"), sem("sts1")]
        lh = [lh0, lh1]
        block = ctx.enter_context(nc.Block())

        @block.sync
        def _(sync):
            sync.dma_start(out=s_cw[:], in_=cw_in.rearrange("(g p) k -> p g k", p=128)).then_inc(lds, 16)
            sync.dma_start(out=s_cb[:], in_=cb_in).then_inc(lds, 16)
            sync.dma_start(out=s_hs[0][:], in_=hs_in[0:128, 125:125 + TH]).then_inc(lh0, 16)
            sync.dma_start(out=s_hs[1][:], in_=hs_in[128:256, 125:125 + TH]).then_inc(lh1, 16)
            for g in range(nG):
                sync.wait_ge(ac, g + 1)
                sync.dma_start(out=xc_out[g * 128:(g + 1) * 128, :],
                               in_=s_xc[g % 2][:]).then_inc(stx[g % 2], 16)
                if g + 2 < nG:
                    sync.wait_ge(dv, 2 * g + 1)
                    sync.dma_start(out=s_hs[g % 2][:],
                                   in_=hs_in[(g + 2) * 128:(g + 3) * 128, 125:125 + TH]).then_inc(lh[g % 2], 16)
            for k in range(4):
                sync.wait_ge(ev, k + 1)
                mt, chk = divmod(k, 2)
                mw = 128 if mt == 0 else 32
                sync.dma_start(out=ssm_out[mt * 128:mt * 128 + mw, chk * 512:(chk + 1) * 512],
                               in_=s_ssm[0:mw, chk * 512:(chk + 1) * 512]).then_inc(sts[k % 2], 16)
            sync.wait_ge(stx[0], 16 * (nG // 2))
            sync.wait_ge(stx[1], 16 * (nG // 2))
            sync.wait_ge(sts[0], 32)
            sync.wait_ge(sts[1], 32)

        @block.gpsimd
        def _(gpsimd):
            gpsimd.dma_start(out=s_w16[:],
                             in_=xpw_in.rearrange("(g p) o -> p g o", p=128)).then_inc(ldw, 16)

        @block.vector
        def _(vector):
            vector.wait_ge(lds, 32)
            for g in range(nG):
                vector.wait_ge(lh[g % 2], 16 * (g // 2 + 1))
                hs, acc = s_hs[g % 2], s_acc[g % 2]
                vector.wait_ge(ac, g)
                vector.tensor_scalar(acc[:], hs[:, 0:TLOC], s_cw[:, g * 4:g * 4 + 1], None,
                                     AluOpType.mult)
                vector.drain()
                for j in range(1, 4):
                    vector.scalar_tensor_tensor(acc[:], hs[:, j:j + TLOC],
                                                s_cw[:, g * 4 + j:g * 4 + j + 1],
                                                acc[:], AluOpType.mult, AluOpType.add)
                    vector.drain()
                vector.sem_inc(dv, 1)
                vector.wait_ge(ac, g + 1)
                vector.tensor_copy(s_xc16[:, g * TLOC:(g + 1) * TLOC], s_xc[g % 2][:]).then_inc(dv, 1)
            for k in range(4):
                vector.wait_ge(mm, k + 1)
                mt, chk = divmod(k, 2)
                mw = 128 if mt == 0 else 32
                if k >= 2:
                    vector.wait_ge(sts[k % 2], 16)
                vector.tensor_copy(s_ssm[0:mw, chk * 512:(chk + 1) * 512],
                                   p_mm[k % 2][0:mw, :]).then_inc(ev, 1)

        @block.scalar
        def _(scalar):
            scalar.wait_ge(lds, 32)
            for g in range(nG):
                scalar.wait_ge(dv, 2 * g + 1)
                if g >= 2:
                    scalar.wait_ge(stx[g % 2], 16 * (g // 2))
                    scalar.wait_ge(dv, 2 * (g - 2) + 2)
                scalar.activation(s_xc[g % 2][:], s_acc[g % 2][:], AF.Silu,
                                  bias=s_cb[:, g:g + 1]).then_inc(ac, 1)

        @block.tensor
        def _(tensor):
            tensor.wait_ge(ldw, 16)
            tensor.wait_ge(dv, 2 * nG)
            for k in range(4):
                mt, chk = divmod(k, 2)
                mw = 128 if mt == 0 else 32
                if k >= 2:
                    tensor.wait_ge(ev, k - 1)
                for g in range(nG):
                    ins = tensor.matmul(p_mm[k % 2][0:mw, :],
                                        s_w16[:, g * 160 + mt * 128:g * 160 + mt * 128 + mw],
                                        s_xc16[:, g * TLOC + chk * 512:g * TLOC + (chk + 1) * 512],
                                        start=(g == 0), stop=(g == nG - 1))
                    if g == nG - 1:
                        ins.then_inc(mm, 1)
    return nc


def _launch_conv(hsT_list, conv_w, conv_b, x_proj_w):
    nc = _cached("conv", _mk_conv)
    cw = _f32(conv_w)
    cb = np.ascontiguousarray(_f32(conv_b).reshape(DI // 128, 128).T)
    xpwT = _f32(np.asarray(x_proj_w).T)
    ims = [dict(hsT=np.ascontiguousarray(h), cw=cw, cb=cb, xpwT=xpwT) for h in hsT_list]
    res = _run(nc, ims)
    return [r["xcT"] for r in res], [r["ssm"] for r in res]


def _mk_dtproj():
    nMT = DLOC // 128
    nch = L // 512
    nc = bass.Bass("TRN2", target_bir_lowering=False)
    ts_in = nc.dram_tensor("tsT", [TR, L], F32, kind="ExternalInput").ap()
    w_in = nc.dram_tensor("dtwT", [TR, DLOC], F32, kind="ExternalInput").ap()
    b_in = nc.dram_tensor("dtb", [128, DLOC // 128], F32, kind="ExternalInput").ap()
    dt_out = nc.dram_tensor("dtT", [DLOC, L], F32, kind="ExternalOutput").ap()
    with ExitStack() as ctx:
        sb, ps, sem = _ctx(nc, ctx)
        s_tsf = sb("s_tsf", [128, L])
        s_ts = sb("s_ts", [128, L], BF16)
        s_wf = sb("s_wf", [128, DLOC])
        s_w = sb("s_w", [128, DLOC], BF16)
        s_b = sb("s_b", [128, nMT])
        s_x = [sb(f"s_x{p}", [128, 512]) for p in range(2)]
        s_a = [sb(f"s_a{p}", [128, 512]) for p in range(2)]
        s_o = [sb(f"s_o{p}", [128, 512]) for p in range(4)]
        p_mm = [ps(f"p_mm{p}", [128, 512]) for p in range(2)]
        ld, mm, dv, ac, ev = (sem(n) for n in ["ld", "mm", "dv", "ac", "ev"])
        stb = [sem(f"st{b}") for b in range(4)]
        block = ctx.enter_context(nc.Block())

        @block.sync
        def _(sync):
            sync.dma_start(out=s_tsf[:], in_=ts_in).then_inc(ld, 16)
            sync.dma_start(out=s_wf[:], in_=w_in).then_inc(ld, 16)
            sync.dma_start(out=s_b[:], in_=b_in).then_inc(ld, 16)
            for k in range(nMT * nch):
                mt, ch = divmod(k, nch)
                sync.wait_ge(dv, 1 + 2 * k + 2)
                sync.dma_start(out=dt_out[mt * 128:(mt + 1) * 128, ch * 512:(ch + 1) * 512],
                               in_=s_o[k % 4][:]).then_inc(stb[k % 4], 16)
            for b in range(4):
                sync.wait_ge(stb[b], 16 * ((nMT * nch - b + 3) // 4))

        @block.vector
        def _(vector):
            vector.wait_ge(ld, 48)
            vector.tensor_copy(s_ts[:], s_tsf[:])
            vector.drain()
            vector.tensor_copy(s_w[:], s_wf[:]).then_inc(dv, 1)
            for k in range(nMT * nch):
                mt, ch = divmod(k, nch)
                vector.wait_ge(mm, k + 1)
                vector.tensor_scalar(s_x[k % 2][:], p_mm[k % 2][:], s_b[:, mt:mt + 1], None,
                                     AluOpType.add)
                vector.drain()
                vector.sem_inc(dv, 1)
                vector.wait_ge(ac, k + 1)
                vector.tensor_tensor(s_o[k % 4][:], s_o[k % 4][:], s_a[k % 2][:], AluOpType.add)
                vector.drain()
                vector.sem_inc(dv, 1)

        @block.scalar
        def _(scalar):
            for k in range(nMT * nch):
                scalar.wait_ge(dv, 1 + 2 * k + 1)
                scalar.activation(s_a[k % 2][:], s_x[k % 2][:], AF.Abs)
                scalar.drain()
                scalar.activation(s_a[k % 2][:], s_a[k % 2][:], AF.Exp, scale=-1.0)
                scalar.drain()
                scalar.activation(s_a[k % 2][:], s_a[k % 2][:], AF.Ln, bias=1.0)
                scalar.drain()
                if k >= 4:
                    scalar.wait_ge(stb[k % 4], 16 * (k // 4))
                scalar.activation(s_o[k % 4][:], s_x[k % 2][:], AF.Relu).then_inc(ac, 1)

        @block.tensor
        def _(tensor):
            tensor.wait_ge(dv, 1)
            for k in range(nMT * nch):
                mt, ch = divmod(k, nch)
                if k >= 2:
                    tensor.wait_ge(dv, 1 + 2 * (k - 2) + 1)
                tensor.matmul(p_mm[k % 2][:], s_w[:, mt * 128:(mt + 1) * 128],
                              s_ts[:, ch * 512:(ch + 1) * 512],
                              start=True, stop=True).then_inc(mm, 1)
    return nc


def _launch_dtproj(tsT, dt_proj_w, dt_proj_b):
    nc = _cached("dtproj", _mk_dtproj)
    dtwT = _f32(np.asarray(dt_proj_w).T)
    ims = []
    for c in range(NC):
        sl = slice(c * DLOC, (c + 1) * DLOC)
        ims.append(dict(tsT=_f32(tsT), dtwT=_f32(dtwT[:, sl]),
                        dtb=np.ascontiguousarray(_f32(np.asarray(dt_proj_b)[sl]).reshape(DLOC // 128, 128).T)))
    return [r["dtT"] for r in _run(nc, ims)]


def _mk_scan():
    TCH = 1024
    nTC = L // TCH
    nG = DLOC // 128
    W4 = nG * TCH
    nc = bass.Bass("TRN2", target_bir_lowering=False)
    dt_in = nc.dram_tensor("dtT", [DLOC, L], F32, kind="ExternalInput").ap()
    xc_in = nc.dram_tensor("xcT", [DLOC, L], F32, kind="ExternalInput").ap()
    b_in = nc.dram_tensor("Bm", [NSTATE, L], F32, kind="ExternalInput").ap()
    c_in = nc.dram_tensor("Cm", [NSTATE, L], F32, kind="ExternalInput").ap()
    al_in = nc.dram_tensor("Alog", [DLOC, NSTATE], F32, kind="ExternalInput").ap()
    d_in = nc.dram_tensor("Dv", [128, DLOC // 128], F32, kind="ExternalInput").ap()
    y_out = nc.dram_tensor("yT", [DLOC, L], F32, kind="ExternalOutput").ap()

    def dv_u(tc):
        return 1 + tc * 17 + 1

    def dv_step(tc, n):
        return dv_u(tc) + n + 1

    def ac_dA(q):
        return 1 + q + 1

    with ExitStack() as ctx:
        sb, ps, sem = _ctx(nc, ctx)
        s_dt = sb("s_dt", [128, W4])
        s_xc = sb("s_xc", [128, W4])
        s_u = sb("s_u", [128, W4])
        s_B = [sb(f"s_B{p}", [128, W4]) for p in range(2)]
        s_C = [sb(f"s_C{p}", [128, W4]) for p in range(2)]
        s_al = sb("s_al", [128, nG * NSTATE])
        s_D = sb("s_D", [128, nG])
        s_dA = [sb(f"s_dA{p}", [128, W4]) for p in range(2)]
        s_W = sb("s_W", [128, W4])
        s_h = sb("s_h", [128, W4])
        s_state = sb("s_state", [128, nG * NSTATE])
        s_y = sb("s_y", [128, W4])
        lds, lt0, lt1, lb0, lb1, dv, ac, sty = (sem(n) for n in
            ["lds", "lt0", "lt1", "lb0", "lb1", "dv", "ac", "sty"])
        lb = [lb0, lb1]
        block = ctx.enter_context(nc.Block())

        @block.sync
        def _(sync):
            sync.dma_start(out=s_al[:], in_=al_in.rearrange("(g p) n -> p g n", p=128)).then_inc(lds, 16)
            sync.dma_start(out=s_D[:], in_=d_in).then_inc(lds, 16)
            sync.dma_start(out=s_dt[:], in_=dt_in[:, 0:TCH].rearrange("(g p) t -> p g t", p=128)).then_inc(lt0, 16)
            sync.dma_start(out=s_xc[:], in_=xc_in[:, 0:TCH].rearrange("(g p) t -> p g t", p=128)).then_inc(lt1, 16)
            for tc in range(nTC):
                sync.wait_ge(dv, dv_step(tc, 15))
                sync.dma_start(out=y_out[:, tc * TCH:(tc + 1) * TCH].rearrange("(g p) t -> p g t", p=128),
                               in_=s_y[:]).then_inc(sty, 16)
                if tc + 1 < nTC:
                    t2 = tc + 1
                    sync.wait_ge(ac, ac_dA(tc * 16 + 15))
                    sync.dma_start(out=s_dt[:],
                                   in_=dt_in[:, t2 * TCH:(t2 + 1) * TCH].rearrange("(g p) t -> p g t", p=128)).then_inc(lt0, 16)
                    sync.dma_start(out=s_xc[:],
                                   in_=xc_in[:, t2 * TCH:(t2 + 1) * TCH].rearrange("(g p) t -> p g t", p=128)).then_inc(lt1, 16)
            sync.wait_ge(sty, 16 * nTC)

        @block.gpsimd
        def _(gpsimd):
            for tc in range(nTC):
                for n in range(NSTATE):
                    q = tc * NSTATE + n
                    if q >= 2:
                        tcp, npp = divmod(q - 2, NSTATE)
                        gpsimd.wait_ge(dv, dv_step(tcp, npp))
                    for g in range(nG):
                        gpsimd.dma_start(out=s_B[q % 2][:, g * TCH:(g + 1) * TCH],
                                         in_=b_in[n:n + 1, tc * TCH:(tc + 1) * TCH].broadcast_to([128, TCH])).then_inc(lb[q % 2], 16)
                        gpsimd.dma_start(out=s_C[q % 2][:, g * TCH:(g + 1) * TCH],
                                         in_=c_in[n:n + 1, tc * TCH:(tc + 1) * TCH].broadcast_to([128, TCH])).then_inc(lb[q % 2], 16)

        @block.scalar
        def _(scalar):
            scalar.wait_ge(lds, 32)
            scalar.activation(s_al[:], s_al[:], AF.Exp).then_inc(ac, 1)
            for tc in range(nTC):
                scalar.wait_ge(lt0, 16 * (tc + 1))
                scalar.wait_ge(dv, 1)
                for n in range(NSTATE):
                    q = tc * NSTATE + n
                    if q >= 2:
                        tcp, npp = divmod(q - 2, NSTATE)
                        scalar.wait_ge(dv, dv_step(tcp, npp))
                    for g in range(nG):
                        ins = scalar.activation(s_dA[q % 2][:, g * TCH:(g + 1) * TCH],
                                                s_dt[:, g * TCH:(g + 1) * TCH],
                                                AF.Exp,
                                                scale=s_al[:, g * NSTATE + n:g * NSTATE + n + 1])
                        if g == nG - 1:
                            ins.then_inc(ac, 1)

        @block.vector
        def _(vector):
            vector.wait_ge(ac, 1)
            vector.tensor_scalar(s_al[:], s_al[:], -1.0, None, AluOpType.mult)
            vector.drain()
            vector.sem_inc(dv, 1)
            for tc in range(nTC):
                vector.wait_ge(lt0, 16 * (tc + 1))
                vector.wait_ge(lt1, 16 * (tc + 1))
                vector.tensor_tensor(s_u[:], s_dt[:], s_xc[:], AluOpType.mult)
                vector.drain()
                vector.sem_inc(dv, 1)
                for n in range(NSTATE):
                    q = tc * NSTATE + n
                    vector.wait_ge(lb[q % 2], 128 * (q // 2 + 1))
                    vector.tensor_tensor(s_W[:], s_u[:], s_B[q % 2][:], AluOpType.mult)
                    vector.drain()
                    vector.wait_ge(ac, ac_dA(q))
                    for g in range(nG):
                        ini = s_state[:, g * NSTATE + n:g * NSTATE + n + 1] if tc > 0 else 0.0
                        vector.tensor_tensor_scan(s_h[:, g * TCH:(g + 1) * TCH],
                                                  s_dA[q % 2][:, g * TCH:(g + 1) * TCH],
                                                  s_W[:, g * TCH:(g + 1) * TCH],
                                                  ini, AluOpType.mult, AluOpType.add)
                    vector.drain()
                    for g in range(nG):
                        vector.tensor_copy(s_state[:, g * NSTATE + n:g * NSTATE + n + 1],
                                           s_h[:, (g + 1) * TCH - 1:(g + 1) * TCH])
                    if n == 0:
                        if tc >= 1:
                            vector.wait_ge(sty, 16 * tc)
                        for g in range(nG):
                            vector.tensor_scalar(s_y[:, g * TCH:(g + 1) * TCH],
                                                 s_xc[:, g * TCH:(g + 1) * TCH],
                                                 s_D[:, g:g + 1], None, AluOpType.mult)
                        vector.drain()
                    vector.tensor_tensor(s_W[:], s_h[:], s_C[q % 2][:], AluOpType.mult)
                    vector.drain()
                    vector.tensor_tensor(s_y[:], s_y[:], s_W[:], AluOpType.add).then_inc(dv, 1)
                    vector.drain()
    return nc


def _launch_scan(dtT_list, xcT_byd, Bm, Cm, A_log, Dvec):
    nc = _cached("scan", _mk_scan)
    ims = []
    for c in range(NC):
        sl = slice(c * DLOC, (c + 1) * DLOC)
        ims.append(dict(dtT=np.ascontiguousarray(dtT_list[c]),
                        xcT=np.ascontiguousarray(xcT_byd[c]),
                        Bm=_f32(Bm), Cm=_f32(Cm),
                        Alog=_f32(np.asarray(A_log)[sl]),
                        Dv=np.ascontiguousarray(_f32(np.asarray(Dvec)[sl]).reshape(DLOC // 128, 128).T)))
    return [r["yT"] for r in _run(nc, ims)]


def _mk_mul(O, T):
    nOT = O // 128
    nc = bass.Bass("TRN2", target_bir_lowering=False)
    a_in = nc.dram_tensor("a", [O, T], F32, kind="ExternalInput").ap()
    b_in = nc.dram_tensor("b", [O, T], F32, kind="ExternalInput").ap()
    o_out = nc.dram_tensor("o", [O, T], BF16, kind="ExternalOutput").ap()
    with ExitStack() as ctx:
        sb, ps, sem = _ctx(nc, ctx)
        s_a = [sb(f"s_a{p}", [128, T]) for p in range(2)]
        s_b = [sb(f"s_b{p}", [128, T]) for p in range(2)]
        s_o = [sb(f"s_o{p}", [128, T], BF16) for p in range(2)]
        la0, la1, dv, st0, st1 = (sem(n) for n in ["la0", "la1", "dv", "st0", "st1"])
        la = [la0, la1]
        stp = [st0, st1]
        block = ctx.enter_context(nc.Block())

        @block.sync
        def _(sync):
            for ot in range(min(2, nOT)):
                sync.dma_start(out=s_a[ot % 2][:], in_=a_in[ot * 128:(ot + 1) * 128, :]).then_inc(la[ot % 2], 16)
                sync.dma_start(out=s_b[ot % 2][:], in_=b_in[ot * 128:(ot + 1) * 128, :]).then_inc(la[ot % 2], 16)
            for ot in range(nOT):
                sync.wait_ge(dv, ot + 1)
                sync.dma_start(out=o_out[ot * 128:(ot + 1) * 128, :], in_=s_o[ot % 2][:]).then_inc(stp[ot % 2], 16)
                if ot + 2 < nOT:
                    sync.dma_start(out=s_a[ot % 2][:], in_=a_in[(ot + 2) * 128:(ot + 3) * 128, :]).then_inc(la[ot % 2], 16)
                    sync.dma_start(out=s_b[ot % 2][:], in_=b_in[(ot + 2) * 128:(ot + 3) * 128, :]).then_inc(la[ot % 2], 16)
            sync.wait_ge(st0, 16 * ((nOT + 1) // 2))
            sync.wait_ge(st1, 16 * (nOT // 2))

        @block.vector
        def _(vector):
            for ot in range(nOT):
                vector.wait_ge(la[ot % 2], 32 * (ot // 2 + 1))
                if ot >= 2:
                    vector.wait_ge(stp[ot % 2], 16 * (ot // 2))
                vector.tensor_tensor(s_o[ot % 2][:], s_a[ot % 2][:], s_b[ot % 2][:],
                                     AluOpType.mult)
                vector.drain()
                vector.sem_inc(dv, 1)
    return nc


def _launch_mul(key, a_list, b_list):
    O, T = a_list[0].shape
    nc = _cached((key, "mul", O, T), lambda: _mk_mul(O, T))
    ims = [dict(a=np.ascontiguousarray(a), b=np.ascontiguousarray(b))
           for a, b in zip(a_list, b_list)]
    return [r["o"] for r in _run(nc, ims)]


def _mk_epilogue(with_silu, with_ln):
    TT = TLOC // 128
    nJ = H // 128
    nc = bass.Bass("TRN2", target_bir_lowering=False)
    x_in = nc.dram_tensor("x", [TLOC, H], F32, kind="ExternalInput").ap()
    a_in = nc.dram_tensor("a", [TLOC, H], F32, kind="ExternalInput").ap()
    b_in = nc.dram_tensor("b", [TLOC, H], F32, kind="ExternalInput").ap()
    xo_out = nc.dram_tensor("xo", [TLOC, H], F32, kind="ExternalOutput").ap()
    if with_ln:
        idn = nc.dram_tensor("idn", [128, 128], BF16, kind="ExternalInput").ap()
        xnt_out = nc.dram_tensor("xnt", [H, TLOC], BF16, kind="ExternalOutput").ap()
    with ExitStack() as ctx:
        sb, ps, sem = _ctx(nc, ctx)
        s_x = [sb(f"s_x{p}", [128, H]) for p in range(2)]
        s_a = [sb(f"s_a{p}", [128, H]) for p in range(2)]
        s_b = [sb(f"s_b{p}", [128, H]) for p in range(2)]
        s_sq = sb("s_sq", [128, H])
        s_m = sb("s_m", [128, 1])
        s_r = sb("s_r", [128, 1])
        s_t1 = sb("s_t1", [128, 1])
        s_t2 = sb("s_t2", [128, 1])
        if with_ln:
            s_xn = [sb(f"s_xn{p}", [128, H], BF16) for p in range(2)]
            s_id = sb("s_id", [128, 128], BF16)
            s_out = [sb(f"s_out{p}", [128, 128], BF16) for p in range(4)]
            p_tp = [ps(f"p_tp{p}", [128, 128], BF16) for p in range(2)]
        lds, le0, le1, ac, dv, pe, ev = (sem(n) for n in
            ["lds", "le0", "le1", "ac", "dv", "pe", "ev"])
        stx = [sem("stx0"), sem("stx1")]
        stb = [sem(f"st{b}") for b in range(4)]
        le = [le0, le1]
        block = ctx.enter_context(nc.Block())
        NST = (1 + nJ) if with_ln else 1
        DVT = 4 if with_ln else 1
        ACT = (1 if with_silu else 0) + (2 if with_ln else 0)

        @block.sync
        def _(sync):
            if with_ln:
                sync.dma_start(out=s_id[:], in_=idn).then_inc(lds, 16)
            for it in range(min(2, TT)):
                sl = slice(it * 128, (it + 1) * 128)
                sync.dma_start(out=s_x[it % 2][:], in_=x_in[sl, :]).then_inc(le[it % 2], 16)
                sync.dma_start(out=s_a[it % 2][:], in_=a_in[sl, :]).then_inc(le[it % 2], 16)
                sync.dma_start(out=s_b[it % 2][:], in_=b_in[sl, :]).then_inc(le[it % 2], 16)
            for it in range(TT):
                sync.wait_ge(dv, DVT * it + 1)
                sync.dma_start(out=xo_out[it * 128:(it + 1) * 128, :],
                               in_=s_x[it % 2][:]).then_inc(stx[it % 2], 16)
                if with_ln:
                    for j in range(nJ):
                        k = it * nJ + j
                        sync.wait_ge(ev, k + 1)
                        sync.dma_start(out=xnt_out[j * 128:(j + 1) * 128, it * 128:(it + 1) * 128],
                                       in_=s_out[k % 4][:]).then_inc(stb[k % 4], 16)
                if it + 2 < TT:
                    sync.wait_ge(dv, DVT * it + DVT)
                    sync.wait_ge(stx[it % 2], 16 * (it // 2 + 1))
                    sl = slice((it + 2) * 128, (it + 3) * 128)
                    sync.dma_start(out=s_x[it % 2][:], in_=x_in[sl, :]).then_inc(le[it % 2], 16)
                    sync.dma_start(out=s_a[it % 2][:], in_=a_in[sl, :]).then_inc(le[it % 2], 16)
                    sync.dma_start(out=s_b[it % 2][:], in_=b_in[sl, :]).then_inc(le[it % 2], 16)
            sync.wait_ge(stx[0], 16 * ((TT + 1) // 2))
            sync.wait_ge(stx[1], 16 * (TT // 2))
            if with_ln:
                for b in range(4):
                    sync.wait_ge(stb[b], 16 * ((nJ * TT - b + 3) // 4))

        @block.scalar
        def _(scalar):
            for it in range(TT):
                scalar.wait_ge(le[it % 2], 48 * (it // 2 + 1))
                if with_silu:
                    scalar.activation(s_b[it % 2][:], s_b[it % 2][:], AF.Silu).then_inc(ac, 1)
                if with_ln:
                    scalar.wait_ge(dv, DVT * it + 1)
                    if it >= 1:
                        scalar.wait_ge(dv, DVT * (it - 1) + 3)
                    scalar.activation(s_sq[:], s_x[it % 2][:], AF.Square).then_inc(ac, 1)
                    scalar.wait_ge(dv, DVT * it + 3)
                    scalar.activation(s_t2[:, 0:1], s_t1[:, 0:1], AF.Ln)
                    scalar.drain()
                    scalar.activation(s_r[:, 0:1], s_t2[:, 0:1], AF.Exp, scale=-0.5).then_inc(ac, 1)

        @block.vector
        def _(vector):
            for it in range(TT):
                vector.wait_ge(le[it % 2], 48 * (it // 2 + 1))
                if with_silu:
                    vector.wait_ge(ac, ACT * it + 1)
                vector.tensor_tensor(s_a[it % 2][:], s_a[it % 2][:], s_b[it % 2][:], AluOpType.mult)
                vector.drain()
                vector.tensor_tensor(s_x[it % 2][:], s_x[it % 2][:], s_a[it % 2][:], AluOpType.add)
                vector.drain()
                vector.sem_inc(dv, 1)
                if with_ln:
                    x = s_x[it % 2]
                    vector.reduce_sum(s_m[:, 0:1], x[:], axis=AX)
                    vector.drain()
                    vector.tensor_scalar(s_m[:, 0:1], s_m[:, 0:1], 1.0 / H, None,
                                         AluOpType.mult).then_inc(dv, 1)
                    vector.wait_ge(ac, ACT * it + (2 if with_silu else 1))
                    vector.reduce_sum(s_t1[:, 0:1], s_sq[:], axis=AX)
                    vector.drain()
                    vector.tensor_tensor(s_t2[:, 0:1], s_m[:, 0:1], s_m[:, 0:1], AluOpType.mult)
                    vector.drain()
                    vector.tensor_scalar(s_t1[:, 0:1], s_t1[:, 0:1], 1.0 / H, None, AluOpType.mult)
                    vector.drain()
                    vector.tensor_scalar(s_t1[:, 0:1], s_t1[:, 0:1], s_t2[:, 0:1], 1e-5,
                                         AluOpType.subtract, AluOpType.add).then_inc(dv, 1)
                    vector.wait_ge(ac, ACT * it + ACT)
                    if it >= 2:
                        vector.wait_ge(pe, nJ * (it - 1))
                    vector.tensor_scalar(s_xn[it % 2][:], x[:], s_m[:, 0:1], s_r[:, 0:1],
                                         AluOpType.subtract, AluOpType.mult).then_inc(dv, 1)
                    for j in range(nJ):
                        k = it * nJ + j
                        vector.wait_ge(pe, k + 1)
                        if k >= 4:
                            vector.wait_ge(stb[k % 4], 16 * (k // 4))
                        vector.tensor_copy(s_out[k % 4][:], p_tp[k % 2][:]).then_inc(ev, 1)

        if with_ln:
            @block.tensor
            def _(tensor):
                tensor.wait_ge(lds, 16)
                for it in range(TT):
                    tensor.wait_ge(dv, DVT * it + 4)
                    for j in range(nJ):
                        k = it * nJ + j
                        if k >= 2:
                            tensor.wait_ge(ev, k - 1)
                        tensor.transpose(p_tp[k % 2][:], s_xn[it % 2][:, j * 128:(j + 1) * 128],
                                         s_id[:]).then_inc(pe, 1)
    return nc


def _launch_epilogue(key, with_silu, with_ln, x_list, a_list, b_list):
    nc = _cached((key, with_silu, with_ln), lambda: _mk_epilogue(with_silu, with_ln))
    idn = np.eye(128, dtype=BF)
    ims = []
    for c in range(NC):
        im = dict(x=np.ascontiguousarray(x_list[c]), a=np.ascontiguousarray(a_list[c]),
                  b=np.ascontiguousarray(b_list[c]))
        if with_ln:
            im["idn"] = idn
        ims.append(im)
    return _run(nc, ims)


def _mk_attn():
    nJT = L // 128
    nIT = TLOC // 128
    nHT = H // 128
    NM = nJT - 1
    UH = nIT + NM

    def I0(j):
        return (j + 1) // 8

    nc = bass.Bass("TRN2", target_bir_lowering=False)
    q_in = nc.dram_tensor("qT", [H, TLOC], BF16, kind="ExternalInput").ap()
    k_in = nc.dram_tensor("kT", [DH, L], BF16, kind="ExternalInput").ap()
    v_in = nc.dram_tensor("vtok", [L, DH], BF16, kind="ExternalInput").ap()
    kl_in = nc.dram_tensor("kloc", [DH, TLOC], BF16, kind="ExternalInput").ap()
    vl_in = nc.dram_tensor("vloc", [TLOC, DH], BF16, kind="ExternalInput").ap()
    wo_in = nc.dram_tensor("woT", [H, H], F32, kind="ExternalInput").ap()
    bo_in = nc.dram_tensor("bo", [128, H // 128], F32, kind="ExternalInput").ap()
    mv_in = nc.dram_tensor("mvec", [nJT, TLOC], BF16, kind="ExternalInput").ap()
    sel_in = nc.dram_tensor("sel64", [64, 64 * 128], BF16, kind="ExternalInput").ap()
    dm_in = nc.dram_tensor("dmask", [128, 128], F32, kind="ExternalInput").ap()
    lf_in = nc.dram_tensor("lfix", [1, TLOC], F32, kind="ExternalInput").ap()
    rm_in = nc.dram_tensor("rmask", [1, TLOC], F32, kind="ExternalInput").ap()
    on1_in = nc.dram_tensor("one1", [128, 1], BF16, kind="ExternalInput").ap()
    s1_out = nc.dram_tensor("s1T", [H, TLOC], F32, kind="ExternalOutput").ap()
    rb_dram = nc.dram_tensor("rb_dram", [NH, TLOC], F32)
    with ExitStack() as ctx:
        sb, ps, sem = _ctx(nc, ctx)
        s_q = sb("s_q", [128, NH * TLOC], BF16)
        s_k = sb("s_k", [128, L], BF16)
        s_v = sb("s_v", [128, nJT * DH], BF16)
        s_kl = sb("s_kl", [128, TLOC], BF16)
        s_vl = sb("s_vl", [128, nIT * DH], BF16)
        s_mv = sb("s_mv", [64, TLOC], BF16)
        s_sel = sb("s_sel", [64, 64 * 128], BF16)
        s_dm = sb("s_dm", [128, 128])
        s_lf = sb("s_lf", [1, TLOC])
        s_rm = sb("s_rm", [1, TLOC])
        s_onec = sb("s_onec", [128, 1], BF16)
        s_P = [sb(f"s_P{p}", [128, 1024], BF16) for p in range(3)]
        s_l = sb("s_l", [1, TLOC])
        s_lr = sb("s_lr", [1, TLOC])
        s_rb = [sb(f"s_rb{p}", [128, TLOC]) for p in range(2)]
        s_ao = sb("s_ao", [128, NH * TLOC], BF16)
        s_wo = [sb(f"s_wo{p}", [128, nHT * 128], BF16) for p in range(2)]
        s_bo = sb("s_bo", [128, nHT])
        s_st = [sb(f"s_st{p}", [128, 512]) for p in range(4)]
        p_S = [ps(f"p_S{p}", [128, 1024]) for p in range(2)]
        p_O = ps("p_O", [128, 1024])
        p_l = ps("p_l", [128, 1024])
        ld, ldm, lw0, lw1, vd, mm, ac, lv, oe, pvd, pvu, mm2, ev = (
            sem(n) for n in ["ld", "ldm", "lw0", "lw1", "vd", "mm", "ac", "lv", "oe",
                             "pvd", "pvu", "mm2", "ev"])
        stb = [sem(f"st{b}") for b in range(4)]
        lw = [lw0, lw1]
        block = ctx.enter_context(nc.Block())
        NLD = 12

        def emit_pv(tensor, w):
            h, u = divmod(w, UH)
            last = (u == UH - 1)
            if u == 0 and h >= 1:
                tensor.wait_ge(oe, h)
            tensor.wait_ge(ac, w + 1)
            if u < nIT:
                i = u
                st0 = (i == 0 or i == 4)
                tensor.matmul(p_O[:, i * 128:(i + 1) * 128],
                              s_vl[:, i * 128:(i + 1) * 128], s_P[w % 3][:, 0:128],
                              start=st0, stop=False)
                ins = tensor.matmul(p_l[0:1, i * 128:(i + 1) * 128],
                                    s_onec[:, 0:1], s_P[w % 3][:, 0:128],
                                    start=st0, stop=False)
            else:
                j = u - nIT
                t0 = I0(j) * 128
                chunks = ([(t0, 512), (512, 1024)] if t0 < 512 else [(t0, 1024)])
                ml = None
                for a, b in chunks:
                    stp = (j == 30 and b == 512) or (j == 62)
                    tensor.matmul(p_O[:, a:b],
                                  s_v[:, j * 128:(j + 1) * 128], s_P[w % 3][:, a:b],
                                  start=False, stop=stp)
                    ml = tensor.matmul(p_l[0:1, a:b],
                                       s_onec[:, 0:1], s_P[w % 3][:, a:b],
                                       start=False, stop=stp)
                ml.then_inc(pvu, 1)
                return
            ins.then_inc(pvu, 1)

        @block.sync
        def _(sync):
            for dst, src in [(s_q, q_in.rearrange("(ht p) t -> p ht t", p=128)),
                             (s_k, k_in), (s_v, v_in.rearrange("(j p) d -> p j d", p=128)),
                             (s_kl, kl_in), (s_vl, vl_in.rearrange("(i p) d -> p i d", p=128)),
                             (s_mv, mv_in), (s_sel, sel_in), (s_dm, dm_in),
                             (s_lf, lf_in), (s_rm, rm_in), (s_onec, on1_in)]:
                sync.dma_start(out=dst[:], in_=src).then_inc(ld, 16)
            sync.dma_start(out=s_bo[:], in_=bo_in).then_inc(ld, 16)
            for h in range(NH):
                sync.wait_ge(lv, h + 1)
                sync.dma_start(out=rb_dram.ap()[h:h + 1, :], in_=s_lr[:]).then_inc(ldm, 16)
            sync.wait_ge(ldm, 16 * NH)
            for h in range(NH):
                if h >= 2:
                    sync.wait_ge(oe, NH + h - 1)
                sync.dma_start(out=s_rb[h % 2][:],
                               in_=rb_dram.ap()[h:h + 1, :].broadcast_to([128, TLOC])).then_inc(ldm, 16)
            for k in range(nHT * 2):
                sync.wait_ge(ev, k + 1)
                ot, ch = divmod(k, 2)
                sync.dma_start(out=s1_out[ot * 128:(ot + 1) * 128, ch * 512:(ch + 1) * 512],
                               in_=s_st[k % 4][:]).then_inc(stb[k % 4], 16)
            for b in range(4):
                sync.wait_ge(stb[b], 16 * ((nHT * 2 - b + 3) // 4))

        @block.gpsimd
        def _(gpsimd):
            for ot in range(nHT):
                if ot >= 2:
                    gpsimd.wait_ge(mm2, (ot - 1) * 2)
                gpsimd.dma_start(out=s_wo[ot % 2][:],
                                 in_=wo_in[:, ot * 128:(ot + 1) * 128].rearrange(
                                     "(kt p) o -> p kt o", p=128)).then_inc(lw[ot % 2], 16)

        @block.tensor
        def _(tensor):
            tensor.wait_ge(ld, 16 * NLD)
            NW = NH * UH
            for w in range(NW):
                h, u = divmod(w, UH)
                if w >= 2:
                    tensor.wait_ge(ac, w - 1)
                if u < nIT:
                    i = u
                    tensor.matmul(p_S[w % 2][:, 0:128],
                                  s_kl[:, i * 128:(i + 1) * 128],
                                  s_q[:, h * TLOC + i * 128:h * TLOC + (i + 1) * 128],
                                  start=True, stop=True).then_inc(mm, 1)
                else:
                    j = u - nIT
                    t0 = I0(j) * 128
                    chunks = ([(t0, 512), (512, 1024)] if t0 < 512 else [(t0, 1024)])
                    chunks = [(a, b) for a, b in chunks if b - a <= 512] or [(t0, 512), (512, 1024)]
                    for ci, (a, b) in enumerate(chunks):
                        tensor.matmul(p_S[w % 2][:, a:b],
                                      s_k[:, j * 128:(j + 1) * 128],
                                      s_q[:, h * TLOC + a:h * TLOC + b],
                                      start=True, stop=False)
                        ins = tensor.matmul(p_S[w % 2][:, a:b],
                                            s_sel[:, j * 128:(j + 1) * 128],
                                            s_mv[:, a:b],
                                            start=False, stop=True)
                        if ci == len(chunks) - 1:
                            ins.then_inc(mm, 1)
                if w >= 2:
                    emit_pv(tensor, w - 2)
            for wl in range(NW - 2, NW):
                emit_pv(tensor, wl)
            tensor.wait_ge(oe, 2 * NH)
            for ot in range(nHT):
                tensor.wait_ge(lw[ot % 2], 16 * (ot // 2 + 1))
                for ch in range(2):
                    k = ot * 2 + ch
                    if k >= 2:
                        tensor.wait_ge(ev, k - 1)
                    for kt in range(nHT):
                        ins = tensor.matmul(p_S[k % 2][:, 0:512],
                                            s_wo[ot % 2][:, kt * 128:(kt + 1) * 128],
                                            s_ao[:, kt * TLOC + ch * 512:kt * TLOC + ch * 512 + 512],
                                            start=(kt == 0), stop=(kt == nHT - 1))
                        if kt == nHT - 1:
                            ins.then_inc(mm2, 1)

        @block.scalar
        def _(scalar):
            for h in range(NH):
                for u in range(UH):
                    w = h * UH + u
                    if w >= 3:
                        scalar.wait_ge(pvu, w - 2)
                    if u < nIT:
                        scalar.wait_ge(vd, h * nIT + u + 1)
                        scalar.activation(s_P[w % 3][:, 0:128], p_S[w % 2][:, 0:128],
                                          AF.Exp, scale=SCALE).then_inc(ac, 1)
                    else:
                        t0 = I0(u - nIT) * 128
                        scalar.wait_ge(mm, w + 1)
                        scalar.activation(s_P[w % 3][:, t0:1024], p_S[w % 2][:, t0:1024],
                                          AF.Exp, scale=SCALE).then_inc(ac, 1)
            for k in range(nHT * 2):
                scalar.wait_ge(mm2, k + 1)
                if k >= 4:
                    scalar.wait_ge(stb[k % 4], 16 * (k // 4))
                ot = k // 2
                if SIM_SAFE_ACT:
                    scalar.activation(s_st[k % 4][:], p_S[k % 2][:, 0:512], AF.Relu,
                                      bias=s_bo[:, ot:ot + 1]).then_inc(ev, 1)
                else:
                    scalar.activation(s_st[k % 4][:], p_S[k % 2][:, 0:512], AF.Prelu,
                                      bias=s_bo[:, ot:ot + 1], alpha=1.0).then_inc(ev, 1)

        @block.vector
        def _(vector):
            vector.wait_ge(ld, 16 * NLD)

            def evac_head(h):
                vector.wait_ge(pvu, (h + 1) * UH)
                vector.tensor_copy(s_ao[:, h * TLOC:(h + 1) * TLOC], p_O[:, 0:1024])
                vector.drain()
                vector.tensor_copy(s_l[:, 0:1024], p_l[0:1, 0:1024])
                vector.drain()
                vector.sem_inc(oe, 1)
                vector.tensor_tensor(s_l[:], s_l[:], s_lf[:], AluOpType.add)
                vector.drain()
                vector.reciprocal(s_l[:], s_l[:])
                vector.drain()
                if h >= 1:
                    vector.wait_ge(ldm, 16 * h)
                vector.tensor_tensor(s_lr[:], s_l[:], s_rm[:], AluOpType.mult)
                vector.drain()
                vector.sem_inc(lv, 1)

            def dmask_add(h, i):
                w = h * UH + i
                vector.wait_ge(mm, w + 1)
                vector.tensor_tensor(p_S[w % 2][:, 0:128], p_S[w % 2][:, 0:128],
                                     s_dm[:], AluOpType.add)
                vector.drain()
                vector.sem_inc(vd, 1)

            for h in range(NH):
                dmask_add(h, 0)
                dmask_add(h, 1)
                if h >= 1:
                    evac_head(h - 1)
                for i in range(2, nIT):
                    dmask_add(h, i)
            evac_head(NH - 1)
            for h in range(NH):
                vector.wait_ge(ldm, 16 * NH + 16 * (h + 1))
                vector.tensor_tensor(s_ao[:, h * TLOC:(h + 1) * TLOC],
                                     s_ao[:, h * TLOC:(h + 1) * TLOC],
                                     s_rb[h % 2][:], AluOpType.mult)
                vector.drain()
                vector.sem_inc(oe, 1)
    return nc


def _launch_attn(qT_list, kT_full, v_full, Wo, bo):
    nc = _cached("attn", _mk_attn)
    woT = _f32(np.asarray(Wo).T)
    boc = np.ascontiguousarray(_f32(bo).reshape(H // 128, 128).T)
    dmask = np.where(np.arange(128)[:, None] < np.arange(128)[None, :], -30.0 / SCALE, NEG).astype(np.float32)
    sel64 = np.repeat(np.eye(64, dtype=BF), 128, axis=1)
    one1 = np.ones((128, 1), BF)
    qT_full = np.concatenate(qT_list, axis=1)
    ims = []
    for c in range(NC):
        cols = (np.arange(TLOC) // 128) * 8 * 128 + c * 128 + np.arange(TLOC) % 128
        g_of_t = 8 * (np.arange(TLOC) // 128) + c
        mvec = np.where(np.arange(64)[:, None] < g_of_t[None, :], -30.0 / SCALE, NEG).astype(BF)
        lfix = np.zeros((1, TLOC), np.float32)
        rmask = np.ones((1, TLOC), np.float32)
        if c == 0:
            lfix[0, 0] = 1.0
            rmask[0, 0] = 0.0
        ims.append(dict(qT=np.ascontiguousarray(qT_full[:, cols]), kT=kT_full, vtok=v_full,
                        kloc=np.ascontiguousarray(kT_full[:, cols]),
                        vloc=np.ascontiguousarray(v_full[cols, :]),
                        woT=woT, bo=boc, mvec=mvec, sel64=sel64, dmask=dmask,
                        lfix=lfix, rmask=rmask, one1=one1))
    res = _run(nc, ims)
    s1_full = np.empty((H, L), np.float32)
    for c in range(NC):
        cols = (np.arange(TLOC) // 128) * 8 * 128 + c * 128 + np.arange(TLOC) % 128
        s1_full[:, cols] = res[c]["s1T"]
    return [np.ascontiguousarray(s1_full[:, c * TLOC:(c + 1) * TLOC]) for c in range(NC)]
def _mk_moe():
    nKT = H // 128
    nOT = H // 128
    nTT = TLOC // 128
    NPH = 3 * EX + 3
    nc = bass.Bass("TRN2", target_bir_lowering=False)
    x_in = nc.dram_tensor("xn1T", [H, TLOC], BF16, kind="ExternalInput").ap()
    wr_in = nc.dram_tensor("Wr", [H, EX], F32, kind="ExternalInput").ap()
    br_in = nc.dram_tensor("br", [EX, 1], F32, kind="ExternalInput").ap()
    gb_in = nc.dram_tensor("gbb", [128, EX], F32, kind="ExternalInput").ap()
    wup_in = nc.dram_tensor("Wup", [EX * H, H], F32, kind="ExternalInput").ap()
    wgt_in = nc.dram_tensor("Wgate", [EX * H, H], F32, kind="ExternalInput").ap()
    wdn_in = nc.dram_tensor("Wdown", [EX * H, H], F32, kind="ExternalInput").ap()
    bup_in = nc.dram_tensor("bup", [128, EX * H // 128], F32, kind="ExternalInput").ap()
    bgt_in = nc.dram_tensor("bgate", [128, EX * H // 128], F32, kind="ExternalInput").ap()
    bdn_in = nc.dram_tensor("bdn", [EX, H], F32, kind="ExternalInput").ap()
    wsu_in = nc.dram_tensor("Wsu", [H, H], F32, kind="ExternalInput").ap()
    wsg_in = nc.dram_tensor("Wsg", [H, H], F32, kind="ExternalInput").ap()
    wsd_in = nc.dram_tensor("Wsd", [H, H], F32, kind="ExternalInput").ap()
    bsu_in = nc.dram_tensor("bsu", [128, H // 128], F32, kind="ExternalInput").ap()
    bsg_in = nc.dram_tensor("bsg", [128, H // 128], F32, kind="ExternalInput").ap()
    bsd_in = nc.dram_tensor("bsd", [128, H // 128], F32, kind="ExternalInput").ap()
    idf_in = nc.dram_tensor("idf", [128, 128], F32, kind="ExternalInput").ap()
    acc_out = nc.dram_tensor("accT", [H, TLOC], F32, kind="ExternalOutput").ap()
    pr_dram = nc.dram_tensor("pr_dram", [EX, TLOC], BF16)

    W_of = {}
    for e in range(EX):
        W_of[3 * e] = ("up", e, wup_in, e * H)
        W_of[3 * e + 1] = ("gate", e, wgt_in, e * H)
        W_of[3 * e + 2] = ("down", e, wdn_in, e * H)
    W_of[21] = ("up", -1, wsu_in, 0)
    W_of[22] = ("gate", -1, wsg_in, 0)
    W_of[23] = ("down", -1, wsd_in, 0)

    with ExitStack() as ctx:
        sb, ps, sem = _ctx(nc, ctx)
        s_x = sb("s_x", [128, nKT * TLOC], BF16)
        s_u = sb("s_u", [128, nOT * TLOC], BF16)
        s_sg = sb("s_sg", [128, nOT * TLOC], BF16)
        s_acc = sb("s_acc", [128, nOT * TLOC])
        s_w = [sb(f"s_w{p}", [128, nKT * 128], BF16) for p in range(2)]
        s_wr = sb("s_wr", [128, nKT * EX], BF16)
        s_br = sb("s_br", [EX, 1])
        s_gbb = sb("s_gbb", [128, EX])
        s_bup = sb("s_bup", [128, EX * nOT])
        s_bgt = sb("s_bgt", [128, EX * nOT])
        s_bd16 = sb("s_bd16", [EX, H], BF16)
        s_bsu = sb("s_bsu", [128, nOT])
        s_bsg = sb("s_bsg", [128, nOT])
        s_bsd = sb("s_bsd", [128, nOT])
        s_idf = sb("s_idf", [128, 128])
        s_g = sb("s_g", [128, TLOC])
        s_gt = sb("s_gt", [128, nTT * EX])
        s_pt = sb("s_pt", [128, nTT * EX])
        s_prT = sb("s_prT", [EX, TLOC])
        s_prT16 = sb("s_prT16", [EX, TLOC], BF16)
        s_pb = sb("s_pb", [128, TLOC], BF16)
        s_m1 = sb("s_m1", [128, 1])
        s_m2 = sb("s_m2", [128, 1])
        s_nm = sb("s_nm", [128, 1])
        s_z = sb("s_z", [128, 1])
        s_e7 = sb("s_e7", [128, EX])
        s_bg = sb("s_bg", [128, EX])
        s_msk = sb("s_msk", [128, EX])
        s_t7 = sb("s_t7", [128, EX])
        p_mm = [ps(f"p_mm{p}", [128, 512]) for p in range(2)]
        p_t = ps("p_t", [128, 128])
        ld, ldg, lw0, lw1, mm, eva, evv, te, vv, g1, g2, pv2, gv, lpb, fin, st = (
            sem(n) for n in ["ld", "ldg", "lw0", "lw1", "mm", "eva", "evv", "te", "vv", "g1",
                             "g2", "pv2", "gv", "lpb", "fin", "st"])
        lpr = sem("lpr")

        def SUG(q):
            return sum(1 for qq in range(q) if qq % 3 != 2 if qq < 21) + \
                   sum(1 for qq in range(21, q) if qq in (21, 22))

        def SD(q):
            return sum(1 for qq in range(min(q, 24)) if (qq < 21 and qq % 3 == 2) or qq == 23)

        def evac_of_chunk(c):
            if c < 2:
                return ("v", c + 1)
            if c < 2 + NPH * 32:
                q, r = divmod(c - 2, 32)
                kind = W_of[q][0]
                if kind == "down":
                    return ("v", 2 + 32 * SD(q) + r + 1)
                return ("a", 32 * SUG(q) + r + 1)
            r = c - (2 + NPH * 32)
            return ("v", 2 + 32 * (SD(24)) + r + 1)
        lw = [lw0, lw1]
        block = ctx.enter_context(nc.Block())
        MM_GATE, MM_GT, MM_PT, MM_EXP, MM_BD = 0, 2, 10, 18, 786
        EV_GATE, EV_EXP, EV_BD = 0, 2, 770
        def vv_after(e):
            return 32 * (e + 1)
        VV_SH = 32 * EX + 16

        @block.sync
        def _(sync):
            for dst, src in [
                (s_x, x_in.rearrange("(kt p) t -> p kt t", p=128)),
                (s_br, br_in),
                (s_gbb, gb_in),
                (s_bup, bup_in),
                (s_bgt, bgt_in),
                (s_bsu, bsu_in),
                (s_bsg, bsg_in),
                (s_bsd, bsd_in),
                (s_idf, idf_in),
            ]:
                sync.dma_start(out=dst[:], in_=src).then_inc(ld, 16)
            sync.wait_ge(gv, 1)
            sync.dma_start(out=pr_dram.ap(), in_=s_prT16[:]).then_inc(lpr, 16)
            for ot in range(nOT):
                sync.wait_ge(fin, ot + 1)
                sync.dma_start(out=acc_out[ot * 128:(ot + 1) * 128, :],
                               in_=s_acc[:, ot * TLOC:(ot + 1) * TLOC]).then_inc(st, 16)
            sync.wait_ge(st, 16 * nOT)

        @block.gpsimd
        def _(gpsimd):
            gpsimd.dma_start(out=s_wr[:], in_=wr_in.rearrange("(kt p) e -> p kt e", p=128)).then_inc(ldg, 16)
            gpsimd.dma_start(out=s_bd16[:], in_=bdn_in).then_inc(ldg, 16)
            for q in range(NPH):
                kind, e, w_dram, row0 = W_of[q]
                for ot in range(nOT):
                    u = q * nOT + ot
                    if u >= 2:
                        gpsimd.wait_ge(mm, MM_EXP + (u - 1) * 2)
                    gpsimd.dma_start(out=s_w[u % 2][:],
                                     in_=w_dram[row0:row0 + H, ot * 128:(ot + 1) * 128].rearrange(
                                         "(kt p) o -> p kt o", p=128)).then_inc(lw[u % 2], 16)
                if kind == "up" and e >= 0:
                    gpsimd.wait_ge(lpr, 16)
                    if e >= 1:
                        gpsimd.wait_ge(vv, vv_after(e - 1))
                    gpsimd.dma_start(out=s_pb[:],
                                     in_=pr_dram.ap()[e:e + 1, :].broadcast_to([128, TLOC])).then_inc(lpb, 16)

        @block.tensor
        def _(tensor):
            tensor.wait_ge(ld, 16 * 9)
            tensor.wait_ge(ldg, 32)
            for ch in range(2):
                for kt in range(nKT):
                    ins = tensor.matmul(p_mm[ch][0:EX, 0:512],
                                        s_wr[:, kt * EX:(kt + 1) * EX],
                                        s_x[:, kt * TLOC + ch * 512:kt * TLOC + ch * 512 + 512],
                                        start=(kt == 0), stop=(kt == nKT - 1))
                    if kt == nKT - 1:
                        ins.then_inc(mm, 1)
            for tt in range(nTT):
                tensor.wait_ge(evv, 2)
                tensor.wait_ge(te, tt)
                tensor.transpose(p_t[:, 0:128], s_g[:, tt * 128:(tt + 1) * 128],
                                 s_idf[:]).then_inc(mm, 1)
            for tt in range(nTT):
                tensor.wait_ge(pv2, tt + 1)
                tensor.wait_ge(te, nTT + tt)
                tensor.transpose(p_t[0:EX, 0:128], s_pt[:, tt * EX:(tt + 1) * EX],
                                 s_idf[:]).then_inc(mm, 1)
            for q in range(NPH):
                kind, e, w_dram, row0 = W_of[q]
                rhs = {"up": s_x, "gate": s_u, "down": s_sg}[kind]
                if kind == "gate":
                    tensor.wait_ge(eva, 32 * (SUG(q - 1) + 1))
                if kind == "down":
                    tensor.wait_ge(vv, vv_after(e) if e >= 0 else VV_SH)
                if q == 21:
                    tensor.wait_ge(vv, vv_after(EX - 1))
                for ot in range(nOT):
                    u = q * nOT + ot
                    tensor.wait_ge(lw[u % 2], 16 * (u // 2 + 1))
                    for ch in range(2):
                        c = u * 2 + ch
                        cg = 2 + c
                        kk, cnt = evac_of_chunk(cg - 2)
                        tensor.wait_ge(eva if kk == "a" else evv, cnt)
                        for kt in range(nKT):
                            ins = tensor.matmul(p_mm[c % 2][:, 0:512],
                                                s_w[u % 2][:, kt * 128:(kt + 1) * 128],
                                                rhs[:, kt * TLOC + ch * 512:kt * TLOC + ch * 512 + 512],
                                                start=(kt == 0), stop=(kt == nKT - 1))
                            if kt == nKT - 1:
                                ins.then_inc(mm, 1)
            tensor.wait_ge(gv, 2)
            for ot in range(nOT):
                for ch in range(2):
                    c = NPH * 32 + ot * 2 + ch
                    cg = 2 + c
                    kk, cnt = evac_of_chunk(cg - 2)
                    tensor.wait_ge(eva if kk == "a" else evv, cnt)
                    tensor.matmul(p_mm[(cg) % 2][:, 0:512],
                                  s_bd16[:, ot * 128:(ot + 1) * 128],
                                  s_prT16[:, ch * 512:(ch + 1) * 512],
                                  start=True, stop=True).then_inc(mm, 1)

        @block.scalar
        def _(scalar):
            for tt in range(nTT):
                scalar.wait_ge(g1, tt + 1)
                scalar.activation(s_e7[:], s_gt[:, tt * EX:(tt + 1) * EX], AF.Exp,
                                  bias=s_nm[:, 0:1]).then_inc(g2, 1)
            for q in range(NPH):
                kind, e, w_dram, row0 = W_of[q]
                if kind == "down":
                    continue
                dst = s_u if kind == "up" else s_sg
                bias = (s_bup if kind == "up" else s_bgt)
                act = (AF.Relu if SIM_SAFE_ACT else AF.Prelu) if kind == "up" else AF.Silu
                for ot in range(nOT):
                    u = q * nOT + ot
                    for ch in range(2):
                        c = u * 2 + ch
                        scalar.wait_ge(mm, MM_EXP + c + 1)
                        bcol = (bias[:, e * nOT + ot:e * nOT + ot + 1] if e >= 0
                                else (s_bsu if kind == "up" else s_bsg)[:, ot:ot + 1])
                        kw = dict(bias=bcol)
                        if act == AF.Prelu:
                            kw["alpha"] = 1.0
                        scalar.activation(dst[:, ot * TLOC + ch * 512:ot * TLOC + ch * 512 + 512],
                                          p_mm[(2 + c) % 2][:, 0:512], act, **kw).then_inc(eva, 1)

        @block.vector
        def _(vector):
            vector.memset(s_acc[:], 0.0)
            vector.memset(s_g[:], 0.0)
            vector.drain()
            for ch in range(2):
                vector.wait_ge(mm, ch + 1)
                vector.tensor_scalar(s_g[0:EX, ch * 512:(ch + 1) * 512], p_mm[ch][0:EX, 0:512],
                                     s_br[:, 0:1], None, AluOpType.add)
                vector.drain()
                vector.sem_inc(evv, 1)
            for tt in range(nTT):
                vector.wait_ge(mm, MM_GT + tt + 1)
                vector.tensor_copy(s_gt[:, tt * EX:(tt + 1) * EX], p_t[:, 0:EX])
                vector.drain()
                vector.sem_inc(te, 1)
                gsl = s_gt[:, tt * EX:(tt + 1) * EX]
                vector.wait_ge(ld, 16 * 9)
                vector.tensor_tensor(s_bg[:], gsl, s_gbb[:], AluOpType.add)
                vector.drain()
                vector.reduce_max(s_m1[:, 0:1], s_bg[:], axis=AX)
                vector.drain()
                vector.tensor_scalar(s_msk[:], s_bg[:], s_m1[:, 0:1], None, AluOpType.is_equal)
                vector.drain()
                vector.scalar_tensor_tensor(s_t7[:], s_msk[:], NEG, s_bg[:],
                                            AluOpType.mult, AluOpType.add)
                vector.drain()
                vector.reduce_max(s_m2[:, 0:1], s_t7[:], axis=AX)
                vector.drain()
                vector.tensor_scalar(s_msk[:], s_bg[:], s_m2[:, 0:1], None, AluOpType.is_ge)
                vector.drain()
                vector.reduce_max(s_m1[:, 0:1], gsl, axis=AX)
                vector.drain()
                vector.tensor_scalar(s_nm[:, 0:1], s_m1[:, 0:1], -1.0, None,
                                     AluOpType.mult)
                vector.drain()
                vector.sem_inc(g1, 1)
                vector.wait_ge(g2, tt + 1)
                vector.tensor_tensor(s_t7[:], s_msk[:], s_e7[:], AluOpType.mult)
                vector.drain()
                vector.reduce_sum(s_z[:, 0:1], s_t7[:], axis=AX)
                vector.drain()
                vector.reciprocal(s_z[:, 0:1], s_z[:, 0:1])
                vector.drain()
                vector.tensor_scalar(s_pt[:, tt * EX:(tt + 1) * EX], s_t7[:],
                                     s_z[:, 0:1], None, AluOpType.mult)
                vector.drain()
                vector.sem_inc(pv2, 1)
            for tt in range(nTT):
                vector.wait_ge(mm, MM_PT + tt + 1)
                vector.tensor_copy(s_prT[0:EX, tt * 128:(tt + 1) * 128], p_t[0:EX, 0:128])
                vector.drain()
                vector.sem_inc(te, 1)
            vector.tensor_copy(s_prT16[0:EX, :], s_prT[0:EX, :])
            vector.drain()
            vector.sem_inc(gv, 1)
            vector.sem_inc(gv, 1)
            for e in range(EX):
                q = 3 * e + 1
                vector.wait_ge(lpb, 16 * (e + 1))
                for ot in range(nOT):
                    vector.wait_ge(eva, 32 * SUG(q) + ot * 2 + 2)
                    usl = s_u[:, ot * TLOC:(ot + 1) * TLOC]
                    ssl = s_sg[:, ot * TLOC:(ot + 1) * TLOC]
                    vector.tensor_tensor(ssl, ssl, usl, AluOpType.mult)
                    vector.drain()
                    vector.sem_inc(vv, 1)
                    vector.tensor_tensor(ssl, ssl, s_pb[:], AluOpType.mult)
                    vector.drain()
                    vector.sem_inc(vv, 1)
                qd = 3 * e + 2
                for ot in range(nOT):
                    for ch in range(2):
                        c = (qd * nOT + ot) * 2 + ch
                        vector.wait_ge(mm, MM_EXP + c + 1)
                        asl = s_acc[:, ot * TLOC + ch * 512:ot * TLOC + ch * 512 + 512]
                        vector.tensor_tensor(asl, asl, p_mm[(2 + c) % 2][:, 0:512], AluOpType.add)
                        vector.drain()
                        vector.sem_inc(evv, 1)
            q = 22
            for ot in range(nOT):
                vector.wait_ge(eva, 32 * SUG(q) + ot * 2 + 2)
                usl = s_u[:, ot * TLOC:(ot + 1) * TLOC]
                ssl = s_sg[:, ot * TLOC:(ot + 1) * TLOC]
                vector.tensor_tensor(ssl, ssl, usl, AluOpType.mult)
                vector.drain()
                vector.sem_inc(vv, 1)
            qd = 23
            for ot in range(nOT):
                for ch in range(2):
                    c = (qd * nOT + ot) * 2 + ch
                    vector.wait_ge(mm, MM_EXP + c + 1)
                    asl = s_acc[:, ot * TLOC + ch * 512:ot * TLOC + ch * 512 + 512]
                    vector.tensor_tensor(asl, asl, p_mm[(2 + c) % 2][:, 0:512], AluOpType.add)
                    vector.drain()
                    vector.sem_inc(evv, 1)
            for ot in range(nOT):
                asl = s_acc[:, ot * TLOC:(ot + 1) * TLOC]
                vector.tensor_scalar(asl, asl, s_bsd[:, ot:ot + 1], None, AluOpType.add)
                vector.drain()
                for ch in range(2):
                    c = NPH * 32 + ot * 2 + ch
                    vector.wait_ge(mm, MM_EXP + c + 1)
                    a2 = s_acc[:, ot * TLOC + ch * 512:ot * TLOC + ch * 512 + 512]
                    vector.tensor_tensor(a2, a2, p_mm[(2 + c) % 2][:, 0:512], AluOpType.add)
                    vector.drain()
                    vector.sem_inc(evv, 1)
                vector.sem_inc(fin, 1)
    return nc


def _launch_moe(xn1T_list, W):
    nc = _cached("moe", _mk_moe)
    com = dict(
        Wr=_f32(W["Wr"]), br=_f32(W["br"]).reshape(EX, 1),
        gbb=np.tile(_f32(W["gate_bias"]).reshape(1, EX), (128, 1)),
        Wup=_f32(W["Wup"]).reshape(EX * H, H), Wgate=_f32(W["Wgate"]).reshape(EX * H, H),
        Wdown=_f32(W["Wdown"]).reshape(EX * H, H),
        bup=np.ascontiguousarray(_f32(W["bup"]).reshape(-1, 128).T),
        bgate=np.ascontiguousarray(_f32(W["bgate"]).reshape(-1, 128).T),
        bdn=_f32(W["bdown"]), Wsu=_f32(W["Wsu"]), Wsg=_f32(W["Wsg"]), Wsd=_f32(W["Wsd"]),
        bsu=np.ascontiguousarray(_f32(W["bsu"]).reshape(-1, 128).T),
        bsg=np.ascontiguousarray(_f32(W["bsg"]).reshape(-1, 128).T),
        bsd=np.ascontiguousarray(_f32(W["bsd"]).reshape(-1, 128).T), idf=np.eye(128, dtype=np.float32))
    ims = [dict(xn1T=np.ascontiguousarray(x), **com) for x in xn1T_list]
    return [r["accT"] for r in _run(nc, ims)]


def kernel(**inp):
    x0 = _f32(inp["L0_attn_hidden_states"]).reshape(L, H)

    xnT = _launch_ln(x0)
    ipT = _f32(np.asarray(inp["L0_attn_in_proj_w"]).T)
    res = _launch_matmul("inproj", xnT, ipT, np.zeros(2 * DI, np.float32),
                         [dict(name="hsT", O=DI, act=AF.Copy, bias=False, scale=None),
                          dict(name="gsT", O=DI, act=AF.Silu, bias=False, scale=None)],
                         F32)
    hsT = [r["hsT"] for r in res]
    gsT = [r["gsT"][:, 128:] for r in res]
    xcT, ssm_loc = _launch_conv(hsT, inp["L0_attn_conv_w"], inp["L0_attn_conv_b"],
                                inp["L0_attn_x_proj_w"])
    ssmT = np.concatenate(ssm_loc, axis=1)
    tsT, Bm, Cm = ssmT[:TR], ssmT[TR:TR + NSTATE], ssmT[TR + NSTATE:]
    dtT = _launch_dtproj(tsT, inp["L0_attn_dt_proj_w"], inp["L0_attn_dt_proj_b"])
    xcT_full = np.concatenate(xcT, axis=1)
    xcT_byd = [xcT_full[c * DLOC:(c + 1) * DLOC] for c in range(NC)]
    yT_s = _launch_scan(dtT, xcT_byd, Bm, Cm, inp["L0_attn_A_log"], inp["L0_attn_D"])
    yT_full = np.concatenate(yT_s, axis=0)
    yT_tok = [np.ascontiguousarray(yT_full[:, c * TLOC:(c + 1) * TLOC]) for c in range(NC)]
    ygT = _launch_mul("yg", yT_tok, gsT)
    opT = _f32(np.asarray(inp["L0_attn_out_proj_w"]).T)
    res = _launch_matmul("outproj", ygT, opT, np.zeros(H, np.float32),
                         [dict(name="s0T", O=H, act=AF.Copy, bias=False, scale=None)], F32)
    s0T = [r["s0T"] for r in res]
    xnTo = [np.ascontiguousarray(x[:, 128:]) for x in xnT]
    w0T = _f32(np.asarray(inp["L0_ffn_W0"]).T)
    res = _launch_matmul("ffn1", xnTo, w0T, inp["L0_ffn_bias0"],
                         [dict(name="fT", O=4 * H, act=AF.Gelu, bias=True, scale=None)], BF16)
    fT = [r["fT"] for r in res]
    w1T = _f32(np.asarray(inp["L0_ffn_W1"]).T)
    res = _launch_matmul("ffn2", fT, w1T, inp["L0_ffn_bias1"],
                         [dict(name="g0T", O=H, act=AF.Silu, bias=True, scale=None)], F32)
    g0T = [r["g0T"] for r in res]
    res = _launch_epilogue("epi0", False, True,
                           [x0[c * TLOC:(c + 1) * TLOC] for c in range(NC)],
                           [np.ascontiguousarray(s.T) for s in s0T],
                           [np.ascontiguousarray(g.T) for g in g0T])
    x1 = [r["xo"] for r in res]
    xn1T = [r["xnt"] for r in res]

    wqkvT = np.concatenate([_f32(np.asarray(inp["L1_attn_Wq"]).T),
                            _f32(np.asarray(inp["L1_attn_Wk"]).T),
                            _f32(np.asarray(inp["L1_attn_Wv"]).T)], axis=1)
    bqkv = np.concatenate([_f32(inp["L1_attn_bq"]), _f32(inp["L1_attn_bk"]),
                           _f32(inp["L1_attn_bv"])])
    res = _launch_matmul("qkv", xn1T, wqkvT, bqkv,
                         [dict(name="qT", O=H, act=AF.Copy, bias=True, scale=None),
                          dict(name="kT", O=DH, act=AF.Copy, bias=True, scale=None),
                          dict(name="vT", O=DH, act=AF.Copy, bias=True, scale=None)], BF16)
    qT = [r["qT"] for r in res]
    kT_full = np.ascontiguousarray(np.concatenate([r["kT"] for r in res], axis=1))
    v_full = np.ascontiguousarray(np.concatenate([r["vT"] for r in res], axis=1).T)
    s1T = _launch_attn(qT, kT_full, v_full, inp["L1_attn_Wo"], inp["L1_attn_bo"])
    accT = _launch_moe(xn1T, {k[len("L1_ffn_"):]: v for k, v in inp.items()
                              if k.startswith("L1_ffn_")})
    res = _launch_epilogue("epi1", True, False,
                           x1,
                           [np.ascontiguousarray(s.T) for s in s1T],
                           [np.ascontiguousarray(a.T) for a in accT])
    x2 = np.concatenate([r["xo"] for r in res], axis=0)
    return x2.reshape(1, L, H).astype(np.float32)
